# version 13
# baseline (speedup 1.0000x reference)
import os as _os


import numpy as np
import ml_dtypes
from contextlib import ExitStack

import concourse.bass as bass
import concourse.mybir as mybir
from concourse.bass_utils import run_bass_kernel_spmd

F32 = mybir.dt.float32
BF16 = mybir.dt.bfloat16
AF = mybir.ActivationFunctionType
ALU = mybir.AluOpType
AX = mybir.AxisListType

D = 1024
S = 2048
NB = 2
NTOK = NB * S
NCHUNK = D // 128
HD = 64
NH = 16
NKV = 4
DFF = 4096
NEXP = 8
LAM = 10.0
EPS = 1e-5

PHASES = ("A", "B", "C", "D")
UNFUSED = False


class Sem:
    def __init__(self, h, name):
        self.h = h
        self.name = name
        self.n = 0
        self.pts = {}


class MK:
    def __init__(self, phases=PHASES):
        self.phases = phases
        self.cur_eng = None
        self.nc = bass.Bass("TRN2", target_bir_lowering=False)
        self.dry = True
        self.sems = {}
        self.stack = ExitStack()
        self._mk_io()
        self._mk_consts()
        for n in ("sD2DA", "sD2DB", "sD2DC", "sD2DD",
                  "sX", "sW", "sW2", "sV", "sV2", "sU", "sU2", "sT", "sT2", "sZ", "sZ2",
                  "sPE", "sACT", "sDVE", "sPOOL", "sPH"):
            self.S(n)
        self.pidx = 0

    def S(self, name):
        if name not in self.sems:
            h = self.stack.enter_context(self.nc.semaphore(name))
            self.sems[name] = Sem(h, name)
        return self.sems[name]

    SELF_EDGE = {"vector": "sDVE", "scalar": "sACT", "gpsimd": "sPOOL"}

    def op(self, eng, name, *args, waits=(), inc=None, mark=None, **kw):
        own = None
        if "dma" not in name and self.cur_eng in self.SELF_EDGE:
            own = self.sems[self.SELF_EDGE[self.cur_eng]]
            if eng is not None and own.n > 0:
                eng.wait_ge(own.h, own.n)
        if eng is not None:
            for sem, key in waits:
                tgt = sem.pts[key] if isinstance(key, str) else key
                eng.wait_ge(sem.h, tgt)
        ins = None
        if eng is not None:
            ins = getattr(eng, name)(*args, **kw)
        if inc is None and own is not None:
            inc = (own, 1)
        if inc is not None:
            sem, k = inc
            assert own is None or sem is own, (name, sem.name)
            sem.n += k
            if ins is not None:
                ins.then_inc(sem.h, k)
            if mark is not None:
                for key in ([mark] if isinstance(mark, str) else mark):
                    if self.dry:
                        sem.pts[key] = sem.n
                    else:
                        assert sem.pts.get(key) == sem.n, (key, sem.pts.get(key), sem.n)
        else:
            assert mark is None
        return ins

    def dma(self, eng, out, in_, sem, mark=None, waits=(), **kw):
        return self.op(eng, "dma_start", out, in_, waits=waits,
                       inc=(sem, 16), mark=mark, **kw)

    def wait(self, eng, sem, key):
        if eng is not None:
            tgt = sem.pts[key] if isinstance(key, str) else key
            eng.wait_ge(sem.h, tgt)

    def barrier(self):
        sem = self.sems["sPH"]
        self.pidx += 1
        tgt = 5 * self.pidx
        with self.nc.Block() as blk:
            for attr in ("tensor", "vector", "scalar", "gpsimd", "sync"):
                def mk(eng):
                    eng.nop().then_inc(sem.h, 1)
                    eng.wait_ge(sem.h, tgt)
                getattr(blk, attr)(mk)
        sem.n += 5

    def run_phase(self, streams):
        snap = {k: v.n for k, v in self.sems.items()}
        self.dry = True
        for name in ("sync", "gpsimd", "tensor", "vector", "scalar"):
            if name in streams:
                self.cur_eng = name
                streams[name](None)
        for k, v in snap.items():
            self.sems[k].n = v
        self.dry = False
        with self.nc.Block() as blk:
            for name in ("sync", "gpsimd", "tensor", "vector", "scalar"):
                if name in streams:
                    def wrapped(eng, fn=streams[name], nm=name):
                        self.cur_eng = nm
                        fn(eng)
                    getattr(blk, name)(wrapped)

    def sb(self, st, name, shape, dt):
        return st.enter_context(self.nc.sbuf_tensor(name, shape, dt))

    def ps(self, st, name, shape, dt=F32):
        return st.enter_context(self.nc.psum_tensor(name, shape, dt))

    def _mk_io(self):
        nc = self.nc
        ph = self.phases
        self.ext_in = []
        self.ext_out = []

        def ein(name, shape, dt=F32):
            self.ext_in.append(name)
            return nc.dram_tensor(name, shape, dt, kind="ExternalInput")

        def eout(name, shape, dt=F32):
            self.ext_out.append(name)
            return nc.dram_tensor(name, shape, dt, kind="ExternalOutput")

        def stream(name, producer, consumers):
            prod = producer in ph
            cons = any(p in ph for p in consumers)
            if prod and cons:
                return (nc.dram_tensor(name + "_32", [NTOK, D], F32, kind="Internal"),
                        nc.dram_tensor(name + "_16", [NTOK, D], BF16, kind="Internal"))
            if prod:
                return eout(name + "_32", [NTOK, D]), eout(name + "_16", [NTOK, D], BF16)
            if cons:
                return ein(name + "_32", [NTOK, D]), ein(name + "_16", [NTOK, D], BF16)
            return None, None

        t = {}
        if "A" in ph:
            t["x"] = ein("L0_attn_x", [NB, S, D])
            t["Wqkv"] = ein("L0_attn_Wqkv", [NH, D, 3 * HD])
            t["bqkv"] = ein("L0_attn_bqkv", [NH, 3 * HD])
            t["Wout"] = ein("L0_attn_Wout", [D, D])
            t["bout"] = ein("L0_attn_bout", [D])
            t["Wqkv16"] = nc.dram_tensor("Wqkv16", [NH, D, 3 * HD], BF16, kind="Internal")
            t["Wout16d"] = nc.dram_tensor("Wout16d", [D, D], BF16, kind="Internal")
            t["S16_0"] = nc.dram_tensor("S16_0", [NB, S, D], BF16, kind="Internal")
        t["z1_32"], t["z1_16"] = stream("z1", "A", ("B", "C", "D"))
        if "B" in ph:
            t["rw"] = ein("L0_ffn_router_w", [NEXP, D])
            t["wg"] = ein("L0_ffn_w_gate", [NEXP, D, DFF])
            t["wu"] = ein("L0_ffn_w_up", [NEXP, D, DFF])
            t["wd"] = ein("L0_ffn_w_down", [NEXP, DFF, D])
            t["wg16"] = nc.dram_tensor("wg16d", [NEXP, D, DFF], BF16, kind="Internal")
            t["wu16"] = nc.dram_tensor("wu16d", [NEXP, D, DFF], BF16, kind="Internal")
            t["wd16"] = nc.dram_tensor("wd16d", [NEXP, DFF, D], BF16, kind="Internal")
        t["z2_32"], t["z2_16"] = stream("z2", "B", ("C", "D"))
        if "C" in ph:
            t["wq1"] = ein("L1_attn_w_q", [D, D])
            t["wk1"] = ein("L1_attn_w_k", [D, NKV * HD])
            t["wv1"] = ein("L1_attn_w_v", [D, NKV * HD])
            t["wo1"] = ein("L1_attn_w_o", [D, D])
            for n in ("wq1", "wk1", "wv1", "wo1"):
                t[n + "_16"] = nc.dram_tensor(n + "_16d", list(t[n].shape), BF16, kind="Internal")
        t["z3_32"], t["z3_16"] = stream("z3", "C", ("D",))
        if "D" in ph:
            for n, shp in (("cn_w1", [D, D]), ("cn_w2", [D, 512]), ("cn_w3", [512, 256]),
                           ("cn_w4", [256, 1]), ("gc_w1", [D, 512]), ("gc_w2", [512, 256]),
                           ("gc_w3", [256, 1])):
                t[n] = ein("L1_ffn_" + n, shp)
                t[n + "_16"] = nc.dram_tensor(n + "_16d", shp, BF16, kind="Internal")
            for n, shp in (("cn_b1", [D]), ("cn_b2", [512]), ("cn_b3", [256]), ("cn_b4", [1]),
                           ("gc_b1", [512]), ("gc_b2", [256]), ("gc_b3", [1]),
                           ("adaptive_range", [1])):
                t[n] = ein("L1_ffn_" + n, shp)
            t["out"] = eout("out", [NB, S, D])
        self.t = t

    def _mk_consts(self):
        nc = self.nc
        self.c_ident32 = nc.inline_tensor(np.eye(128, dtype=np.float32), name="ident32")
        self.c_ident16 = nc.inline_tensor(np.eye(128, dtype=ml_dtypes.bfloat16), name="ident16")
        ones = np.ones((1, 128), dtype=np.float32)
        self.c_ones1x128_32 = nc.inline_tensor(ones, name="ones1x128_32")
        self.c_ones1x128_16 = nc.inline_tensor(ones.astype(ml_dtypes.bfloat16), name="ones1x128_16")
        self.c_ones64x1_16 = nc.inline_tensor(np.ones((64, 1), dtype=ml_dtypes.bfloat16), name="ones64x1_16")
        masks = np.concatenate(
            [(np.arange(512)[None, :] - np.arange(128)[:, None] >= 128 * m)
             for m in range(4)], axis=1).astype(ml_dtypes.bfloat16)
        self.c_cmask = nc.inline_tensor(masks, name="cmask")
        half = HD // 2
        inv = 1.0 / (10000.0 ** (np.arange(half, dtype=np.float64) * 2.0 / HD))
        ang = np.arange(S, dtype=np.float64)[None, :] * inv[:, None]
        self.c_ropecos = nc.inline_tensor(np.cos(ang).astype(ml_dtypes.bfloat16), name="ropecos")
        self.c_ropesin = nc.inline_tensor(np.sin(ang).astype(ml_dtypes.bfloat16), name="ropesin")
        sel = np.zeros((8, 8 * 128), dtype=ml_dtypes.bfloat16)
        for e in range(8):
            sel[e, e * 128:(e + 1) * 128] = 1.0
        self.c_sel8 = nc.inline_tensor(sel, name="sel8")
        self.c_allones = nc.inline_tensor(
            np.ones((128, 128), dtype=ml_dtypes.bfloat16), name="allones")
        self.c_strictut = nc.inline_tensor(
            np.triu(np.ones((128, 128), dtype=np.float32), 1)
            .astype(ml_dtypes.bfloat16), name="strictut")

    def emit_prologue(self):
        t = self.t

        def g_stream(g):
            if "A" in self.phases:
                sD = self.sems["sD2DA"]
                self.dma(g, t["Wout16d"][:, :], t["Wout"][:, :], sD, mark="d2dAo")
                self.wait(g, sD, "d2dAo")
                self.dma(g, t["S16_0"][0, :, :], t["x"][0, :, :], sD, mark="d2dAx0")
                self.wait(g, sD, "d2dAx0")
                self.dma(g, t["Wqkv16"][0:2, :, :], t["Wqkv"][0:2, :, :], sD,
                         mark="d2dAw0")
                self.wait(g, sD, "d2dAw0")
                self.dma(g, t["Wqkv16"][2:NH, :, :], t["Wqkv"][2:NH, :, :], sD,
                         mark="d2dAw")
                self.wait(g, sD, "d2dAw")
                self.dma(g, t["S16_0"][1, :, :], t["x"][1, :, :], sD, mark="d2dAx1")
                self.wait(g, sD, "d2dAx1")
            if "C" in self.phases:
                sD = self.sems["sD2DC"]
                for n in ("wq1", "wk1", "wv1", "wo1"):
                    self.dma(g, t[n + "_16"][:, :], t[n][:, :], sD,
                             mark="d2dC" if n == "wo1" else None)
            if "D" in self.phases:
                sD = self.sems["sD2DD"]
                for n in ("cn_w1", "cn_w2", "cn_w3", "cn_w4",
                          "gc_w1", "gc_w2", "gc_w3"):
                    self.dma(g, t[n + "_16"][:, :], t[n][:, :], sD,
                             mark="d2dD" if n == "gc_w3" else None)
            if "B" in self.phases:
                sD = self.sems["sD2DB"]
                for e in range(NEXP):
                    self.dma(g, t["wg16"][e, :, :], t["wg"][e, :, :], sD)
                    self.dma(g, t["wu16"][e, :, :], t["wu"][e, :, :], sD)
                    self.dma(g, t["wd16"][e, :, :], t["wd"][e, :, :], sD,
                             mark="d2dB" if e == NEXP - 1 else None)

        self.run_phase({"gpsimd": g_stream})

    def emit_phaseA(self):
        t = self.t
        sX = self.sems["sX"]
        sW, sW2 = self.sems["sW"], self.sems["sW2"]
        sV, sV2 = self.sems["sV"], self.sems["sV2"]
        sZ, sZ2 = self.sems["sZ"], self.sems["sZ2"]
        sPE, sACT, sDVE = self.sems["sPE"], self.sems["sACT"], self.sems["sDVE"]
        sD = self.sems["sD2DA"]

        items = []
        ga = 0

        def G(kind, **kw):
            items.append(dict(kind=kind, **kw))

        G("biasT", ga=ga, esem="sACT"); ga += 1
        G("coutrow", ga=ga, nh2=0, esem="sACT"); ga += 1
        G("coutrow", ga=ga, nh2=1, esem="sACT"); ga += 1
        G("coutb")
        for b in range(NB):
            gs = 0
            gy = 0
            gt = 0
            for h in range(NH):
                for j in range(3):
                    for qg in range(4):
                        G("proj", b=b, h=h, j=j, qg=qg, ga=ga,
                          esem="sACT" if j == 2 else "sDVE")
                        ga += 1
                G("sq0", b=b, h=h)
                for qg in range(4):
                    G("qnorm", b=b, h=h, qg=qg, ga=ga, esem="sDVE"); ga += 1
                G("sq1", b=b, h=h)
                for kt in range(16):
                    G("vtrans", b=b, h=h, kt=kt, gt=gt); gt += 1
                    if kt < 8:
                        if kt % 2 == 1:
                            G("knorm", b=b, h=h, qg=kt // 2, ga=ga, esem="sACT"); ga += 1
                G("maxb", b=b, h=h)
                G("aug", b=b, h=h)
                G("kones", b=b, h=h)
                for qg in range(4):
                    nk = 4 * qg + 4
                    for i, kt in enumerate(range(nk)):
                        G("score", b=b, h=h, qg=qg, kt=kt, gs=gs, i=i,
                          straddle=(kt >= 4 * qg), gy=gy, nk=nk)
                        gs += 1
                    G("recip", b=b, h=h, qg=qg, gy=gy)
                    G("bcast", b=b, h=h, qg=qg, ga=ga, gy=gy, esem="sACT"); ga += 1
                    G("ymul", b=b, h=h, qg=qg, gy=gy,
                      last=(h == NH - 1 and qg == 3))
                    gy += 1
            for tt in range(16):
                G("outproj", b=b, tt=tt, g=b * 16 + tt)
                G("epilog", b=b, tt=tt, g=b * 16 + tt)
        self.A_items = items
        ga_item = {}
        for it in items:
            if "ga" in it:
                ga_item[it["ga"]] = it
        self.A_ga_item = ga_item

        with ExitStack() as st:
            xh = self.sb(st, "A_xh", [128, NCHUNK * S], BF16)
            wqkv = self.sb(st, "A_wqkv", [128, 2 * NCHUNK * 192], BF16)
            wo = self.sb(st, "A_wo", [128, NCHUNK * D], BF16)
            qh = self.sb(st, "A_qh", [96, S], BF16)
            kh = self.sb(st, "A_kh", [96, S], BF16)
            vt = self.sb(st, "A_vt", [64, S], F32)
            va = self.sb(st, "A_va", [128, 16 * 96], BF16)
            q2 = self.sb(st, "A_q2", [64, S], BF16)
            n_row = self.sb(st, "A_nrow", [1, S], F32)
            b_row = self.sb(st, "A_brow", [1, S], F32)
            m2c = self.sb(st, "A_m2c", [1, 2], F32)
            pbuf = self.sb(st, "A_pbuf", [128, 2 * 1024], BF16)
            rzrow = self.sb(st, "A_rzrow", [1, 512], F32)
            rzb = self.sb(st, "A_rzb", [64, 512], F32)
            yt = self.sb(st, "A_yt", [128, NCHUNK * S], BF16)
            bq = self.sb(st, "A_bq", [16, 192], F32)
            bqT = self.sb(st, "A_bqT", [64, 3 * 16], F32)
            bqTs = self.sb(st, "A_bqTs", [64, 16], F32)
            brow = self.sb(st, "A_boutrow", [1, D], F32)
            brow16 = self.sb(st, "A_boutrow16", [1, D], BF16)
            bvrep = self.sb(st, "A_bvrep", [128, 8], F32)
            bvrep16 = self.sb(st, "A_bvrep16", [128, 8], BF16)
            coutrow = self.sb(st, "A_coutrow", [1, D], F32)
            coutb = self.sb(st, "A_coutb", [128, D], F32)
            id16 = self.sb(st, "A_id16", [128, 128], BF16)
            id32 = self.sb(st, "A_id32", [128, 128], F32)
            o128_32 = self.sb(st, "A_o128_32", [1, 128], F32)
            o128_16 = self.sb(st, "A_o128_16", [1, 128], BF16)
            o64_16 = self.sb(st, "A_o64_16", [64, 1], BF16)
            cmask = self.sb(st, "A_cmask", [128, 4 * 512], BF16)
            x0b = self.sb(st, "A_x0b", [128, 2 * D], F32)
            rb = self.sb(st, "A_rb", [128, 2 * D], F32)
            cb = self.sb(st, "A_cb", [128, 2 * D], F32)
            sqb = rb
            zb = rb
            zb16 = self.sb(st, "A_zb16", [128, 2 * D], BF16)
            stat = self.sb(st, "A_stat", [128, 8], F32)
            epsc = self.sb(st, "A_epsc", [128, 1], F32)

            psA = self.ps(st, "A_psA", [64, 2 * 512])
            psS = self.ps(st, "A_psS", [128, 4 * 512])
            psY = self.ps(st, "A_psY", [96, 512])
            psO = self.ps(st, "A_psO", [128, 512])

            def pa(sl):
                return psA[:, (sl % 2) * 512:(sl % 2 + 1) * 512]

            def pss(gs):
                sp, half = (gs // 2) % 2, gs % 2
                return psS[:, sp * 1024 + half * 512: sp * 1024 + (half + 1) * 512]

            def pssp(gp):
                sp = gp % 2
                return psS[:, sp * 1024:(sp + 1) * 1024]

            def pb(gs):
                sp, half = (gs // 2) % 2, gs % 2
                return pbuf[:, sp * 1024 + half * 512: sp * 1024 + (half + 1) * 512]

            def pbp(gp):
                sp = gp % 2
                return pbuf[:, sp * 1024:(sp + 1) * 1024]

            def f_sync(sy):
                self.wait(sy, sD, "d2dAo")
                self.dma(sy, bass.AP(wo, 0, [[NCHUNK * D, 128], [D, NCHUNK], [1, D]]),
                         t["Wout16d"][:, :].rearrange("(c p) e -> p c e", p=128), sX)
                self.dma(sy, bq[:, :], t["bqkv"][:, :], sX)
                self.dma(sy, brow[:, :], t["bout"][:].rearrange("(a d) -> a d", a=1), sX)
                self.dma(sy, id16[:, :], self.c_ident16[:, :], sX)
                self.dma(sy, id32[:, :], self.c_ident32[:, :], sX)
                self.dma(sy, o128_32[:, :], self.c_ones1x128_32[:, :], sX)
                self.dma(sy, o128_16[:, :], self.c_ones1x128_16[:, :], sX)
                self.dma(sy, o64_16[:, :], self.c_ones64x1_16[:, :], sX)
                self.dma(sy, cmask[:, :], self.c_cmask[:, :], sX, mark="A.setup")
                self.wait(sy, sX, "A.setup")
                for b in range(NB):
                    if b > 0:
                        self.wait(sy, sPE, f"A.xhfree.{b - 1}")
                    self.wait(sy, sD, "d2dAx0" if b == 0 else "d2dAx1")
                    for c in range(NCHUNK):
                        self.dma(sy, xh[:, c * S:(c + 1) * S],
                                 t["S16_0"][b, :, c * 128:(c + 1) * 128], sX,
                                 transpose=True,
                                 mark=f"A.xh.{b}" if c == NCHUNK - 1 else None)
                    for h in range(NH):
                        semw = sV if h % 2 == 0 else sV2
                        if b == 0 and h == 0:
                            self.wait(sy, sD, "d2dAw0")
                        if b == 0 and h == 2:
                            self.wait(sy, sD, "d2dAw")
                        if b * NH + h >= 2:
                            hp, bp = (h - 2) % NH, b if h >= 2 else b - 1
                            self.wait(sy, sPE, f"A.wfree.{bp}.{hp}")
                        self.dma(sy, bass.AP(wqkv, (h % 2) * NCHUNK * 192,
                                             [[2 * NCHUNK * 192, 128], [192, NCHUNK], [1, 192]]),
                                 t["Wqkv16"][h, :, :].rearrange("(c p) e -> p c e", p=128),
                                 semw, mark=f"A.wh.{b}.{h}")
                    for g in range(b * 16, b * 16 + 16):
                        tt = g % 16
                        semx = sW if g % 2 == 0 else sW2
                        if g >= 2:
                            self.wait(sy, sDVE, f"A.x0used.{g - 2}")
                        self.dma(sy, x0b[:, (g % 2) * D:(g % 2 + 1) * D],
                                 t["x"][b, tt * 128:(tt + 1) * 128, :], semx,
                                 mark=f"A.x0.{g}")
                        if g >= 2:
                            gw = g - 2
                            semz = sZ if gw % 2 == 0 else sZ2
                            self.wait(sy, sDVE, f"A.z.{gw}")
                            self.wait(sy, sACT, f"A.z16.{gw}")
                            r0 = (gw // 16) * S + (gw % 16) * 128
                            self.dma(sy, t["z1_32"][r0:r0 + 128, :],
                                     zb[:, (gw % 2) * D:(gw % 2 + 1) * D], semz)
                            self.dma(sy, t["z1_16"][r0:r0 + 128, :],
                                     zb16[:, (gw % 2) * D:(gw % 2 + 1) * D], semz,
                                     mark=f"A.zw.{gw}")
                for gw in (NB * 16 - 2, NB * 16 - 1):
                    semz = sZ if gw % 2 == 0 else sZ2
                    self.wait(sy, sDVE, f"A.z.{gw}")
                    self.wait(sy, sACT, f"A.z16.{gw}")
                    r0 = (gw // 16) * S + (gw % 16) * 128
                    self.dma(sy, t["z1_32"][r0:r0 + 128, :],
                             zb[:, (gw % 2) * D:(gw % 2 + 1) * D], semz)
                    self.dma(sy, t["z1_16"][r0:r0 + 128, :],
                             zb16[:, (gw % 2) * D:(gw % 2 + 1) * D], semz,
                             mark=f"A.zw.{gw}")
                self.wait(sy, sZ, sZ.n)
                self.wait(sy, sZ2, sZ2.n)

            def f_tensor(te):
                self.wait(te, sX, "A.setup")
                for it in self.A_items:
                    k = it["kind"]
                    if k == "biasT":
                        for j in range(3):
                            self.op(te, "transpose", psA[0:64, j * 16:(j + 1) * 16],
                                    bq[:, j * 64:(j + 1) * 64], id32[0:16, 0:16],
                                    inc=(sPE, 1), mark=f"A.ga.{it['ga']}" if j == 2 else None)
                    elif k == "coutrow":
                        nh2 = it["nh2"]
                        self.wait(te, sACT, "A.bvrep16")
                        self.wait(te, sACT, "A.brow16")
                        if it["ga"] >= 2:
                            it2 = self.A_ga_item[it["ga"] - 2]
                            self.wait(te, self.sems[it2["esem"]], f"A.gadone.{it['ga'] - 2}")
                        sl = pa(it["ga"])[0:1, :]
                        for ci in range(NCHUNK):
                            self.op(te, "matmul", sl, bvrep16[:, ci:ci + 1],
                                    wo[:, ci * D + nh2 * 512: ci * D + (nh2 + 1) * 512],
                                    start=(ci == 0), stop=False)
                        self.op(te, "matmul", sl, o128_16[0:1, 0:1],
                                brow16[0:1, nh2 * 512:(nh2 + 1) * 512],
                                start=False, stop=True, inc=(sPE, 1),
                                mark=f"A.ga.{it['ga']}")
                    elif k == "coutb":
                        self.wait(te, sACT, "A.coutrow")
                        for nh2 in range(2):
                            if nh2 == 1:
                                self.wait(te, sACT, "A.coutbe.0")
                            self.op(te, "matmul", psO[:, 0:512],
                                    o128_32[0:1, :], coutrow[0:1, nh2 * 512:(nh2 + 1) * 512],
                                    start=True, stop=True, inc=(sPE, 1),
                                    mark=f"A.coutb.{nh2}")
                    elif k == "proj":
                        b, h, j, qg, g = it["b"], it["h"], it["j"], it["qg"], it["ga"]
                        if h == 0 and j == 0 and qg == 0:
                            self.wait(te, sX, f"A.xh.{b}")
                        if j == 0 and qg == 0:
                            semw = sV if h % 2 == 0 else sV2
                            self.wait(te, semw, f"A.wh.{b}.{h}")
                        if g >= 2:
                            it2 = self.A_ga_item[g - 2]
                            self.wait(te, self.sems[it2["esem"]], f"A.gadone.{g - 2}")
                        woff = (h % 2) * NCHUNK * 192 + j * 64
                        marks = [f"A.ga.{g}"]
                        if j == 2 and qg == 3:
                            marks.append(f"A.wfree.{b}.{h}")
                            if h == NH - 1:
                                marks.append(f"A.xhfree.{b}")
                        for c in range(NCHUNK):
                            self.op(te, "matmul", pa(g)[0:64, :],
                                    wqkv[:, woff + c * 192: woff + c * 192 + 64],
                                    xh[:, c * S + qg * 512: c * S + (qg + 1) * 512],
                                    start=(c == 0), stop=(c == NCHUNK - 1),
                                    inc=(sPE, 1) if c == NCHUNK - 1 else None,
                                    mark=marks if c == NCHUNK - 1 else None)
                    elif k in ("qnorm", "knorm"):
                        b, h, g = it["b"], it["h"], it["ga"]
                        self.wait(te, sDVE, f"A.sq{0 if k == 'qnorm' else 1}.{b}.{h}")
                        if g >= 2:
                            it2 = self.A_ga_item[g - 2]
                            self.wait(te, self.sems[it2["esem"]], f"A.gadone.{g - 2}")
                        qg = it["qg"]
                        self.op(te, "matmul", pa(g)[0:1, :],
                                o64_16[:, :], q2[:, qg * 512:(qg + 1) * 512],
                                start=True, stop=True, inc=(sPE, 1),
                                mark=f"A.ga.{g}")
                    elif k == "vtrans":
                        b, h, kt, gt = it["b"], it["h"], it["kt"], it["gt"]
                        if kt == 0:
                            self.wait(te, sACT, f"A.vt.{b}.{h}")
                            if h == 0:
                                if b == 0:
                                    self.wait(te, sACT, "A.coutbe")
                                else:
                                    self.wait(te, sDVE, f"A.psO.done.{b * 16 - 1}.1")
                        if gt >= 1:
                            self.wait(te, sDVE, f"A.psT.done.{b}.{gt - 1}")
                        self.op(te, "transpose", psO[:, 0:64],
                                vt[:, kt * 128:(kt + 1) * 128], id32[0:64, 0:64],
                                inc=(sPE, 1), mark=f"A.psT.{b}.{gt}")
                    elif k == "score":
                        b, h, qg, kt, gs, i = (it["b"], it["h"], it["qg"],
                                               it["kt"], it["gs"], it["i"])
                        if i == 0 and qg == 0:
                            self.wait(te, sDVE, f"A.aug.{b}.{h}")
                            self.wait(te, sDVE, f"A.kones.{b}.{h}")
                        if gs % 2 == 0 and gs >= 4:
                            self.wait(te, sACT, f"A.expdone.{b}.{gs // 2 - 2}")
                        self.op(te, "matmul", pss(gs),
                                kh[:, kt * 128:(kt + 1) * 128],
                                qh[:, qg * 512:(qg + 1) * 512],
                                start=True, stop=True,
                                inc=(sPE, 1), mark=f"A.psS.{b}.{gs}")
                        if gs % 2 == 1:
                            self._y_mm(te, b, it, gs - 3)
                            self._y_mm(te, b, it, gs - 2)
                        if i == it["nk"] - 1:
                            self._y_mm(te, b, it, gs - 1)
                            self._y_mm(te, b, it, gs)
                    elif k == "bcast":
                        b, h, qg, g = it["b"], it["h"], it["qg"], it["ga"]
                        self.wait(te, sDVE, f"A.rz.{b}.{h}.{qg}")
                        if g >= 2:
                            it2 = self.A_ga_item[g - 2]
                            self.wait(te, self.sems[it2["esem"]], f"A.gadone.{g - 2}")
                        self.op(te, "matmul", pa(g)[0:64, :],
                                o128_32[0:1, 0:64], rzrow[0:1, :],
                                start=True, stop=True, inc=(sPE, 1),
                                mark=f"A.ga.{g}")
                    elif k == "outproj":
                        b, tt, g = it["b"], it["tt"], it["g"]
                        if g == 0:
                            self.wait(te, sACT, "A.coutbe")
                        if tt == 0:
                            self.wait(te, sDVE, f"A.ytdone.{b}")
                        for dg in range(2):
                            if not (g == 0 and dg == 0):
                                self.wait(te, sDVE,
                                          f"A.psO.done.{g}.0" if dg == 1
                                          else f"A.psO.done.{g - 1}.1")
                            for ci in range(NCHUNK):
                                self.op(te, "matmul",
                                        psO[:, 0:512],
                                        yt[:, ci * S + tt * 128: ci * S + (tt + 1) * 128],
                                        wo[:, ci * D + dg * 512: ci * D + (dg + 1) * 512],
                                        start=(ci == 0), stop=(ci == NCHUNK - 1),
                                        inc=(sPE, 1) if ci == NCHUNK - 1 else None,
                                        mark=f"A.psO.{g}.{dg}" if ci == NCHUNK - 1 else None)

            self._A_ymeta = {}
            for it in items:
                if it["kind"] == "score":
                    self._A_ymeta[(it["b"], it["gs"])] = it

            def y_mm(te, b, cur, gs):
                if gs < 0:
                    return
                itm = self._A_ymeta.get((b, gs))
                if itm is None or itm.get("ydone"):
                    return
                itm["ydone"] = True
                h, qg, kt, i = itm["h"], itm["qg"], itm["kt"], itm["i"]
                gy = itm["gy"]
                sem, key = ((sDVE, f"A.pmask.{b}.{gs}") if itm["straddle"]
                            else (sACT, f"A.expp.{b}.{gs // 2}"))
                self.wait(te, sem, key)
                if i == 0 and gy >= 1:
                    self.wait(te, sDVE, f"A.ymul.{b}.{gy - 1}")
                last = (i == itm["nk"] - 1)
                self.op(te, "matmul", psY[:, :],
                        va[:, kt * 96:(kt + 1) * 96], pb(gs),
                        start=(i == 0), stop=last,
                        inc=(sPE, 1) if last else None,
                        mark=f"A.psY.{b}.{gy}" if last else None)

            self._y_mm = lambda te, b, cur, gs: y_mm(te, b, cur, gs)

            def f_scalar(se):
                for it in self.A_items:
                    k = it["kind"]
                    if k == "biasT":
                        g = it["ga"]
                        self.wait(se, sPE, f"A.ga.{g}")
                        for j in range(3):
                            self.op(se, "copy", bqT[:, j * 16:(j + 1) * 16],
                                    psA[0:64, j * 16:(j + 1) * 16],
                                    inc=(sACT, 1),
                                    mark=[f"A.gadone.{g}", "A.bqTe"] if j == 2 else None)
                        self.wait(se, sDVE, "A.bvrep")
                        self.op(se, "copy", bvrep16[:, :], bvrep[:, :],
                                inc=(sACT, 1), mark="A.bvrep16")
                        self.wait(se, sX, "A.setup")
                        self.op(se, "copy", brow16[:, :], brow[:, :],
                                inc=(sACT, 1), mark="A.brow16")
                    elif k == "coutrow":
                        g, nh2 = it["ga"], it["nh2"]
                        self.wait(se, sPE, f"A.ga.{g}")
                        self.op(se, "copy", coutrow[0:1, nh2 * 512:(nh2 + 1) * 512],
                                pa(g)[0:1, :], inc=(sACT, 1),
                                mark=[f"A.gadone.{g}"] + (["A.coutrow"] if nh2 == 1 else []))
                    elif k == "coutb":
                        for nh2 in range(2):
                            self.wait(se, sPE, f"A.coutb.{nh2}")
                            self.op(se, "copy", coutb[:, nh2 * 512:(nh2 + 1) * 512],
                                    psO[:, 0:512],
                                    inc=(sACT, 1),
                                    mark="A.coutbe" if nh2 == 1 else "A.coutbe.0")
                    elif k == "proj":
                        if it["j"] != 2:
                            continue
                        b, h, qg, g = it["b"], it["h"], it["qg"], it["ga"]
                        self.wait(se, sPE, f"A.ga.{g}")
                        marks = [f"A.gadone.{g}"]
                        if qg == 3:
                            marks.append(f"A.vt.{b}.{h}")
                        self.op(se, "copy", vt[0:64, qg * 512:(qg + 1) * 512],
                                pa(g)[0:64, :], inc=(sACT, 1), mark=marks)
                    elif k == "knorm":
                        b, h, qg, g = it["b"], it["h"], it["qg"], it["ga"]
                        self.wait(se, sPE, f"A.ga.{g}")
                        self.op(se, "copy",
                                n_row[0:1, qg * 512:(qg + 1) * 512],
                                pa(g)[0:1, :], inc=(sACT, 1),
                                mark=[f"A.gadone.{g}"] +
                                     ([f"A.nrow1.{b}.{h}"] if qg == 3 else []))
                    elif k == "score":
                        b, gs = it["b"], it["gs"]
                        if gs % 2 == 0:
                            continue
                        gp = gs // 2
                        self.wait(se, sPE, f"A.psS.{b}.{gs}")
                        self.op(se, "activation", pbp(gp), pssp(gp), AF.Exp,
                                inc=(sACT, 1),
                                mark=[f"A.expp.{b}.{gp}", f"A.expdone.{b}.{gp}"])
                    elif k == "bcast":
                        b, h, qg, g = it["b"], it["h"], it["qg"], it["ga"]
                        self.wait(se, sPE, f"A.ga.{g}")
                        self.op(se, "copy", rzb[:, :], pa(g)[0:64, :],
                                inc=(sACT, 1),
                                mark=[f"A.gadone.{g}", f"A.rzb.{b}.{h}.{qg}"])
                    elif k == "epilog":
                        g = it["g"]
                        par = g % 2
                        self.wait(se, sDVE, f"A.var.{g}")
                        self.op(se, "activation", stat[:, par * 4 + 3:par * 4 + 4],
                                stat[:, par * 4 + 2:par * 4 + 3], AF.Sqrt,
                                scale=1.0 / D, bias=epsc[:, :],
                                inc=(sACT, 1), mark=f"A.sd.{g}")
                        self.wait(se, sDVE, f"A.z.{g}")
                        self.op(se, "activation", zb16[:, par * D:(par + 1) * D],
                                zb[:, par * D:(par + 1) * D], AF.Copy,
                                inc=(sACT, 1), mark=f"A.z16.{g}")

            def f_vector(ve):
                for it in self.A_items:
                    k = it["kind"]
                    if k == "biasT":
                        self.wait(ve, sACT, f"A.gadone.{it['ga']}")
                        self.op(ve, "tensor_scalar_mul", bqTs[:, :], bqT[:, 0:16],
                                0.125, inc=(sDVE, 1))
                        self.op(ve, "memset", epsc[:, :], EPS, inc=(sDVE, 1))
                        self.op(ve, "memset", qh[64:96, :], 0.0, inc=(sDVE, 1))
                        self.op(ve, "memset", kh[64:96, :], 0.0, inc=(sDVE, 1))
                        self.op(ve, "memset", va[:, :], 0.0, inc=(sDVE, 1))
                        for c in range(8):
                            for r in range(2):
                                h = 2 * c + r
                                self.op(ve, "tensor_copy",
                                        bvrep[r * 64:(r + 1) * 64, c:c + 1],
                                        bqT[:, 32 + h:32 + h + 1],
                                        inc=(sDVE, 1),
                                        mark="A.bvrep" if h == 15 else None)
                        self.op(ve, "memset",
                                bass.AP(va, 64, [[16 * 96, 128], [96, 16], [1, 1]]), 1.0,
                                inc=(sDVE, 1), mark="A.vones")
                    elif k == "proj":
                        if it["j"] == 2:
                            continue
                        b, h, j, qg, g = it["b"], it["h"], it["j"], it["qg"], it["ga"]
                        self.wait(ve, sPE, f"A.ga.{g}")
                        marks = [f"A.gadone.{g}"]
                        if qg == 3:
                            marks.append(f"A.{'qk'[j]}evac.{b}.{h}")
                        if j == 0:
                            self.op(ve, "tensor_scalar",
                                    qh[0:64, qg * 512:(qg + 1) * 512],
                                    pa(g)[0:64, :], 0.125, bqTs[:, h:h + 1],
                                    ALU.mult, ALU.add,
                                    inc=(sDVE, 1), mark=marks)
                        else:
                            self.op(ve, "tensor_scalar_add",
                                    kh[0:64, qg * 512:(qg + 1) * 512],
                                    pa(g)[0:64, :], bqT[:, 16 + h:16 + h + 1],
                                    inc=(sDVE, 1), mark=marks)
                    elif k == "qnorm":
                        b, h, qg, g = it["b"], it["h"], it["qg"], it["ga"]
                        self.wait(ve, sPE, f"A.ga.{g}")
                        self.op(ve, "tensor_scalar_mul",
                                b_row[0:1, qg * 512:(qg + 1) * 512],
                                pa(g)[0:1, :], 0.5 * LAM,
                                inc=(sDVE, 1),
                                mark=[f"A.gadone.{g}"] +
                                     ([f"A.nrow0.{b}.{h}"] if qg == 3 else []))
                    elif k == "aug":
                        b, h = it["b"], it["h"]
                        self.op(ve, "tensor_scalar_mul", qh[64:65, :], b_row[0:1, :],
                                -1.0, inc=(sDVE, 1), mark=f"A.aug.{b}.{h}")
                    elif k == "sq0":
                        b, h = it["b"], it["h"]
                        self.op(ve, "tensor_mul", q2[:, :], qh[0:64, :], qh[0:64, :],
                                inc=(sDVE, 1), mark=f"A.sq0.{b}.{h}")
                    elif k == "sq1":
                        b, h = it["b"], it["h"]
                        self.wait(ve, sPE, f"A.ga.{it_ga_last_qnorm[(b, h)]}")
                        self.op(ve, "tensor_mul", q2[:, :], kh[0:64, :], kh[0:64, :],
                                inc=(sDVE, 1), mark=f"A.sq1.{b}.{h}")
                    elif k == "vtrans":
                        b, gt, kt = it["b"], it["gt"], it["kt"]
                        self.wait(ve, sPE, f"A.psT.{b}.{gt}")
                        self.op(ve, "tensor_copy", va[:, kt * 96:kt * 96 + 64],
                                psO[:, 0:64], inc=(sDVE, 1),
                                mark=f"A.psT.done.{b}.{gt}")
                    elif k == "maxb":
                        b, h = it["b"], it["h"]
                        self.wait(ve, sACT, f"A.nrow1.{b}.{h}")
                        self.op(ve, "reduce_max", m2c[0:1, 0:1], n_row[0:1, 0:S],
                                axis=AX.X, inc=(sDVE, 1))
                        self.op(ve, "tensor_scalar_mul", m2c[0:1, 1:2], m2c[0:1, 0:1],
                                0.5 / LAM, inc=(sDVE, 1))
                        self.op(ve, "tensor_scalar_add", b_row[0:1, :], b_row[0:1, :],
                                m2c[0:1, 1:2],
                                inc=(sDVE, 1), mark=f"A.brow.{b}.{h}")
                    elif k == "kones":
                        b, h = it["b"], it["h"]
                        self.op(ve, "memset", kh[64:65, :], 1.0,
                                inc=(sDVE, 1), mark=f"A.kones.{b}.{h}")
                    elif k == "score":
                        if it["straddle"]:
                            b, gs = it["b"], it["gs"]
                            m = it["kt"] - 4 * it["qg"]
                            self.wait(ve, sACT, f"A.expp.{b}.{gs // 2}")
                            self.op(ve, "tensor_mul", pb(gs), pb(gs),
                                    cmask[:, m * 512:(m + 1) * 512],
                                    inc=(sDVE, 1), mark=f"A.pmask.{b}.{gs}")
                    elif k == "recip":
                        b, h, qg, gy = it["b"], it["h"], it["qg"], it["gy"]
                        self.wait(ve, sPE, f"A.psY.{b}.{gy}")
                        self.op(ve, "reciprocal", rzrow[0:1, :], psY[64:65, :],
                                inc=(sDVE, 1), mark=f"A.rz.{b}.{h}.{qg}")
                    elif k == "ymul":
                        b, h, qg, gy = it["b"], it["h"], it["qg"], it["gy"]
                        self.wait(ve, sACT, f"A.rzb.{b}.{h}.{qg}")
                        strip, row = h // 2, (h % 2) * 64
                        self.op(ve, "tensor_mul",
                                yt[row:row + 64,
                                   strip * S + qg * 512: strip * S + (qg + 1) * 512],
                                psY[0:64, :], rzb[:, :],
                                inc=(sDVE, 1),
                                mark=[f"A.ymul.{b}.{gy}"] +
                                     ([f"A.ytdone.{b}"] if it["last"] else []))
                    elif k == "epilog":
                        b, tt, g = it["b"], it["tt"], it["g"]
                        par = g % 2
                        semx = sW if g % 2 == 0 else sW2
                        self.wait(ve, semx, f"A.x0.{g}")
                        if g >= 2:
                            semz = sZ if g % 2 == 0 else sZ2
                            self.wait(ve, semz, f"A.zw.{g - 2}")
                        self.wait(ve, sPE, f"A.psO.{g}.0")
                        self.op(ve, "tensor_add", rb[:, par * D:par * D + 512],
                                psO[:, 0:512], x0b[:, par * D:par * D + 512],
                                inc=(sDVE, 1), mark=f"A.psO.done.{g}.0")
                        self.wait(ve, sPE, f"A.psO.{g}.1")
                        self.op(ve, "tensor_add", rb[:, par * D + 512:(par + 1) * D],
                                psO[:, 0:512], x0b[:, par * D + 512:(par + 1) * D],
                                inc=(sDVE, 1),
                                mark=[f"A.psO.done.{g}.1", f"A.x0used.{g}"])
                        self.op(ve, "tensor_add", rb[:, par * D:(par + 1) * D],
                                rb[:, par * D:(par + 1) * D], coutb[:, :],
                                inc=(sDVE, 1))
                        self.op(ve, "reduce_sum", stat[:, par * 4:par * 4 + 1],
                                rb[:, par * D:(par + 1) * D], axis=AX.X,
                                negate=True, inc=(sDVE, 1))
                        self.op(ve, "tensor_scalar_mul", stat[:, par * 4 + 1:par * 4 + 2],
                                stat[:, par * 4:par * 4 + 1], 1.0 / D, inc=(sDVE, 1))
                        self.op(ve, "tensor_scalar_add", cb[:, par * D:(par + 1) * D],
                                rb[:, par * D:(par + 1) * D],
                                stat[:, par * 4 + 1:par * 4 + 2], inc=(sDVE, 1))
                        self.op(ve, "tensor_mul", sqb[:, par * D:(par + 1) * D],
                                cb[:, par * D:(par + 1) * D],
                                cb[:, par * D:(par + 1) * D], inc=(sDVE, 1))
                        self.op(ve, "reduce_sum", stat[:, par * 4 + 2:par * 4 + 3],
                                sqb[:, par * D:(par + 1) * D], axis=AX.X,
                                inc=(sDVE, 1), mark=f"A.var.{g}")
                        self.wait(ve, sACT, f"A.sd.{g}")
                        self.op(ve, "reciprocal", stat[:, par * 4 + 3:par * 4 + 4],
                                stat[:, par * 4 + 3:par * 4 + 4], inc=(sDVE, 1))
                        self.op(ve, "tensor_scalar_mul", zb[:, par * D:(par + 1) * D],
                                cb[:, par * D:(par + 1) * D],
                                stat[:, par * 4 + 3:par * 4 + 4],
                                inc=(sDVE, 1), mark=f"A.z.{g}")

            it_ga_last_qnorm = {}
            for it in items:
                if it["kind"] == "qnorm":
                    it_ga_last_qnorm[(it["b"], it["h"])] = it["ga"]

            def reset_items():
                for it in items:
                    it.pop("ydone", None)

            streams = {"sync": f_sync, "tensor": lambda e: (reset_items(), f_tensor(e)),
                       "vector": f_vector, "scalar": f_scalar}
            self.run_phase(streams)

    def emit_phaseB(self):
        t = self.t
        nc = self.nc
        sX = self.sems["sX"]
        sW, sW2 = self.sems["sW"], self.sems["sW2"]
        sV, sV2 = self.sems["sV"], self.sems["sV2"]
        sZ, sZ2 = self.sems["sZ"], self.sems["sZ2"]
        sPE, sACT = self.sems["sPE"], self.sems["sACT"]
        sDVE, sPOOL = self.sems["sDVE"], self.sems["sPOOL"]
        sD = self.sems["sD2DB"]
        I32 = mybir.dt.int32
        from concourse.bass_core_rust import IndirectOffsetOnAxis

        C = 768
        NTL = C // 128
        TPW = ((0, 512), (512, C - 512))
        ROWW = D + 128
        NS = 4
        FS = DFF // NS
        NFC = FS // 128

        xg_d = [nc.dram_tensor(f"B_xg{e}", [C, ROWW], BF16, kind="Internal")
                for e in range(NEXP)]
        dn_d = [nc.dram_tensor(f"B_dn{e}", [C, D], BF16, kind="Internal")
                for e in range(NEXP)]

        items = []
        god = 0

        def G(kind, **kw):
            items.append(dict(kind=kind, **kw))

        for c in range(NCHUNK):
            G("rwT", c=c, god=god, esem="sACT"); god += 1
        for b in range(NB):
            for h2 in range(4):
                G("xh", b=b, h2=h2)
                for tt in range(h2 * 4, h2 * 4 + 4):
                    G("logit", b=b, tt=tt, h2=h2, god=god, esem="sACT"); god += 1
                    G("smax", b=b, tt=tt)
            for tt in range(16):
                G("pos", b=b, tt=tt, god=god, esem="sDVE"); god += 1
            G("accinit", b=b)
            for tt in range(16):
                G("xstg", b=b, tt=tt)
                G("xscat", b=b, tt=tt, last=(tt == 15))
            sid = 0
            for e in range(NEXP):
                G("xgload", b=b, e=e)
                G("cwbE", b=b, e=e, god=god, esem="sACT"); god += 1
                G("cwbE2", b=b, e=e, god=god, esem="sACT"); god += 1
                if e >= 1:
                    G("dstore", b=b, e=e - 1)
                for s in range(NS):
                    G("wload", b=b, e=e, s=s, sid=b * NEXP * NS + e * NS + s)
                    gid0 = (b * NEXP * NS + e * NS + s) * 16
                    for fc in range(NFC):
                        for tp in range(2):
                            G("gu", b=b, e=e, s=s, fc=fc, tp=tp,
                              tok0=TPW[tp][0], tw=TPW[tp][1],
                              sid=b * NEXP * NS + e * NS + s,
                              gid=gid0 + fc * 2 + tp)
                    for tl in range(NTL):
                        G("down", b=b, e=e, s=s, tl=tl,
                          sid=b * NEXP * NS + e * NS + s,
                          god=god, esem=("sACT" if s == 0 else "sDVE"))
                        god += 1
                G("accg", b=b, e=e)
            G("dstore", b=b, e=NEXP - 1)
            G("accglast", b=b)
            for tt in range(16):
                G("epilog", b=b, tt=tt, g=b * 16 + tt)
        self.B_items = items
        self.B_god_item = {it["god"]: it for it in items if "god" in it}
        self.B_last_gu = {}
        for it in items:
            if it["kind"] == "gu":
                self.B_last_gu[(it["b"], it["e"])] = it["gid"]

        with ExitStack() as st:
            xh = self.sb(st, "B_xh", [128, NCHUNK * 512], BF16)
            xs = self.sb(st, "B_xs", [128, 2 * ROWW], BF16)
            xgT = self.sb(st, "B_xgT", [128, NCHUNK * C], BF16)
            cwTg = self.sb(st, "B_cwTg", [128, C], BF16)
            cwb = self.sb(st, "B_cwb", [128, C], BF16)
            hb = self.sb(st, "B_h", [128, NFC * C], BF16)
            acc = self.sb(st, "B_acc", [128, 16 * 1024], BF16)
            stg = self.sb(st, "B_stg", [128, NTL * 1024], BF16)
            wga = self.sb(st, "B_wg", [128, 2 * NCHUNK * 1024], BF16)
            wua = self.sb(st, "B_wu", [128, 2 * NCHUNK * 1024], BF16)
            wda = self.sb(st, "B_wd", [128, NFC * 1024], BF16)
            sg = self.sb(st, "B_sg", [128, 2 * 512], BF16)
            ub = self.sb(st, "B_ub", [128, 2 * 512], BF16)
            rwf = self.sb(st, "B_rwf", [8, D], F32)
            rwT = self.sb(st, "B_rwT", [128, NCHUNK * 8], BF16)
            lgt = self.sb(st, "B_lgt", [128, 8], F32)
            ee = self.sb(st, "B_ee", [128, 8], F32)
            eq = self.sb(st, "B_eq", [128, 8], F32)
            eqa = self.sb(st, "B_eqa", [128, 16 * 8], BF16)
            cwa = self.sb(st, "B_cwa", [128, 16 * 8], BF16)
            posf = self.sb(st, "B_posf", [128, 8], F32)
            offw = self.sb(st, "B_offw", [128, 8], F32)
            offs = self.sb(st, "B_offs", [128, 16 * 8], I32)
            rstat = self.sb(st, "B_rstat", [128, 4], F32)
            sel8 = self.sb(st, "B_sel8", [8, 8 * 128], BF16)
            allo = self.sb(st, "B_allo", [128, 128], BF16)
            sut = self.sb(st, "B_sut", [128, 128], BF16)
            id32 = self.sb(st, "B_id32", [128, 128], F32)
            rb = self.sb(st, "B_rb", [128, 2 * D], F32)
            cb = self.sb(st, "B_cb", [128, D], F32)
            zb16 = self.sb(st, "B_zb16", [128, 2 * D], BF16)
            stat = self.sb(st, "B_stat", [128, 8], F32)
            epsc = self.sb(st, "B_epsc", [128, 1], F32)

            bcreg = st.enter_context(nc.gpsimd.register("B_bc"))
            psG = self.ps(st, "B_psG", [128, 2 * 512])
            psU = self.ps(st, "B_psU", [128, 2 * 512])
            psO = self.ps(st, "B_psO", [128, 2 * 1024])

            def pg(i):
                return psG[:, (i % 2) * 512:(i % 2 + 1) * 512]

            def pu(i):
                return psU[:, (i % 2) * 512:(i % 2 + 1) * 512]

            def po(i):
                return psO[:, (i % 2) * 1024:(i % 2 + 1) * 1024]

            def god_wait(eng, g):
                if g >= 2:
                    it2 = self.B_god_item[g - 2]
                    self.wait(eng, self.sems[it2["esem"]], f"B.godone.{g - 2}")

            def f_sync(sy):
                self.dma(sy, id32[:, :], self.c_ident32[:, :], sX)
                self.dma(sy, sel8[:, :], self.c_sel8[:, :], sX)
                self.dma(sy, allo[:, :], self.c_allones[:, :], sX)
                self.dma(sy, sut[:, :], self.c_strictut[:, :], sX)
                self.dma(sy, rwf[:, :], t["rw"][:, :], sX, mark="B.setup")
                self.wait(sy, sX, "B.setup")
                self.wait(sy, sD, "d2dB")
                self.wait(sy, sDVE, "B.xs0")
                for e in range(NEXP):
                    for r8 in range(C // 128):
                        self.dma(sy, xg_d[e][r8 * 128:(r8 + 1) * 128, :],
                                 xs[:, 0:ROWW], sX,
                                 mark="B.xgz" if (e == NEXP - 1 and r8 == C // 128 - 1)
                                 else None)
                self.wait(sy, sX, "B.xgz")
                for it in self.B_items:
                    k = it["kind"]
                    if k == "xh":
                        b, h2 = it["b"], it["h2"]
                        if b > 0 or h2 > 0:
                            self.wait(sy, sPE, f"B.xhfree.{b}.{h2 - 1}"
                                      if h2 > 0 else f"B.xhfree.{b - 1}.3")
                        for c in range(NCHUNK):
                            self.dma(sy, xh[:, c * 512:(c + 1) * 512],
                                     t["z1_16"][b * S + h2 * 512:
                                                b * S + (h2 + 1) * 512,
                                                c * 128:(c + 1) * 128], sX,
                                     transpose=True,
                                     mark=f"B.xh.{b}.{h2}" if c == NCHUNK - 1 else None)
                    elif k == "xstg":
                        b, tt = it["b"], it["tt"]
                        semt = self.sems["sT"] if tt % 2 == 0 else self.sems["sT2"]
                        if b * 16 + tt >= 2:
                            pb, pt = (b, tt - 2) if tt >= 2 else (b - 1, tt + 14)
                            self.wait(sy, sV, f"B.xscat.{pb}.{pt}")
                        self.dma(sy, xs[:, (tt % 2) * ROWW:(tt % 2) * ROWW + D],
                                 t["z1_16"][b * S + tt * 128:
                                            b * S + (tt + 1) * 128, :], semt,
                                 mark=f"B.xstg.{b}.{tt}")
                    elif k == "xgload":
                        b, e = it["b"], it["e"]
                        self.wait(sy, sV, f"B.xscatall.{b}")
                        if (b, e) != (0, 0):
                            pb, pe = (b, e - 1) if e > 0 else (b - 1, NEXP - 1)
                            self.wait(sy, sPE, f"B.gufree.{pb}.{pe}")
                        for c in range(NCHUNK):
                            self.dma(sy, xgT[:, c * C:(c + 1) * C],
                                     xg_d[e][:, c * 128:(c + 1) * 128], sW,
                                     transpose=True)
                        self.dma(sy, cwTg[:, :], xg_d[e][:, D:D + 128], sW,
                                 transpose=True, mark=f"B.xgl.{b}.{e}")
                    elif k == "wload":
                        e, s, sid = it["e"], it["s"], it["sid"]
                        semw = self.sems["sU"] if sid % 2 == 0 else self.sems["sU2"]
                        if sid >= 2:
                            self.wait(sy, sPE, f"B.wfree.{sid - 2}")
                        w = sid % 2
                        self.dma(sy, bass.AP(wga, w * NCHUNK * 1024,
                                             [[2 * NCHUNK * 1024, 128], [1024, NCHUNK], [1, 1024]]),
                                 t["wg16"][e, :, s * FS:(s + 1) * FS]
                                 .rearrange("(c p) f -> p c f", p=128), semw)
                        self.dma(sy, bass.AP(wua, w * NCHUNK * 1024,
                                             [[2 * NCHUNK * 1024, 128], [1024, NCHUNK], [1, 1024]]),
                                 t["wu16"][e, :, s * FS:(s + 1) * FS]
                                 .rearrange("(c p) f -> p c f", p=128), semw)
                        if sid >= 1:
                            self.wait(sy, sPE, f"B.dfree.{sid - 1}")
                        self.dma(sy, bass.AP(wda, 0,
                                             [[NFC * 1024, 128], [1024, NFC], [1, 1024]]),
                                 t["wd16"][e, s * FS:(s + 1) * FS, :]
                                 .rearrange("(c p) f -> p c f", p=128), semw,
                                 mark=f"B.wl.{sid}")
                    elif k == "dstore":
                        b, e = it["b"], it["e"]
                        self.wait(sy, sDVE, f"B.stg.{b}.{e}")
                        self.dma(sy,
                                 dn_d[e][:, :].rearrange("(tl p) d -> p tl d", p=128),
                                 bass.AP(stg, 0, [[NTL * 1024, 128], [1024, NTL], [1, 1024]]),
                                 sW2, mark=f"B.dst.{b}.{e}")
                        self.wait(sy, sW2, f"B.dst.{b}.{e}")
                    elif k == "epilog":
                        b, tt, g = it["b"], it["tt"], it["g"]
                        if g >= 2:
                            gw = g - 2
                            semz = sZ if gw % 2 == 0 else sZ2
                            self.wait(sy, sDVE, f"B.z.{gw}")
                            self.wait(sy, sACT, f"B.z16.{gw}")
                            rw0 = (gw // 16) * S + (gw % 16) * 128
                            self.dma(sy, t["z2_32"][rw0:rw0 + 128, :],
                                     rb[:, (gw % 2) * D:(gw % 2 + 1) * D], semz)
                            self.dma(sy, t["z2_16"][rw0:rw0 + 128, :],
                                     zb16[:, (gw % 2) * D:(gw % 2 + 1) * D], semz,
                                     mark=f"B.zw.{gw}")
                for gw in (NB * 16 - 2, NB * 16 - 1):
                    semz = sZ if gw % 2 == 0 else sZ2
                    self.wait(sy, sDVE, f"B.z.{gw}")
                    self.wait(sy, sACT, f"B.z16.{gw}")
                    rw0 = (gw // 16) * S + (gw % 16) * 128
                    self.dma(sy, t["z2_32"][rw0:rw0 + 128, :],
                             rb[:, (gw % 2) * D:(gw % 2 + 1) * D], semz)
                    self.dma(sy, t["z2_16"][rw0:rw0 + 128, :],
                             zb16[:, (gw % 2) * D:(gw % 2 + 1) * D], semz,
                             mark=f"B.zw.{gw}")
                self.wait(sy, sZ, sZ.n)
                self.wait(sy, sZ2, sZ2.n)

            def f_tensor(te):
                self.wait(te, sX, "B.setup")
                for it in self.B_items:
                    k = it["kind"]
                    if k == "rwT":
                        c, g = it["c"], it["god"]
                        god_wait(te, g)
                        self.op(te, "transpose", po(g)[0:128, 0:8],
                                rwf[:, c * 128:(c + 1) * 128], id32[0:8, 0:8],
                                inc=(sPE, 1), mark=f"B.god.{g}")
                    elif k == "logit":
                        b, tt, h2, g = it["b"], it["tt"], it["h2"], it["god"]
                        if tt % 4 == 0:
                            self.wait(te, sX, f"B.xh.{b}.{h2}")
                            if b == 0 and h2 == 0:
                                self.wait(te, sACT, "B.rwTe")
                        god_wait(te, g)
                        tl = tt % 4
                        for dc in range(NCHUNK):
                            self.op(te, "matmul", po(g)[:, 0:8],
                                    xh[:, dc * 512 + tl * 128: dc * 512 + (tl + 1) * 128],
                                    rwT[:, dc * 8:(dc + 1) * 8],
                                    start=(dc == 0), stop=(dc == NCHUNK - 1),
                                    inc=(sPE, 1) if dc == NCHUNK - 1 else None,
                                    mark=[f"B.god.{g}"] +
                                         ([f"B.xhfree.{b}.{h2}"] if tl == 3 and dc == NCHUNK - 1 else [])
                                    if dc == NCHUNK - 1 else None)
                    elif k == "pos":
                        b, tt, g = it["b"], it["tt"], it["god"]
                        self.wait(te, sDVE, f"B.eqa.{b}.15")
                        god_wait(te, g)
                        nk = tt + 1
                        for kt in range(nk):
                            lt = sut if kt == tt else allo
                            self.op(te, "matmul", po(g)[:, 0:8],
                                    lt[:, :], eqa[:, kt * 8:(kt + 1) * 8],
                                    start=(kt == 0), stop=(kt == nk - 1),
                                    inc=(sPE, 1) if kt == nk - 1 else None,
                                    mark=f"B.god.{g}" if kt == nk - 1 else None)
                    elif k in ("cwbE", "cwbE2"):
                        b, e, g = it["b"], it["e"], it["god"]
                        half = 0 if k == "cwbE" else 1
                        hw = 512 if half == 0 else C - 512
                        if half == 0:
                            self.wait(te, sW, f"B.xgl.{b}.{e}")
                        god_wait(te, g)
                        self.op(te, "matmul", po(g)[:, 0:hw],
                                sel8[:, e * 128:(e + 1) * 128],
                                cwTg[0:8, half * 512:half * 512 + hw],
                                start=True, stop=True,
                                inc=(sPE, 1), mark=f"B.god.{g}")
                    elif k == "gu":
                        b, e, s, fc, tp = it["b"], it["e"], it["s"], it["fc"], it["tp"]
                        sid, gg = it["sid"], it["gid"]
                        if fc == 0 and tp == 0:
                            semw = self.sems["sU"] if sid % 2 == 0 else self.sems["sU2"]
                            self.wait(te, semw, f"B.wl.{sid}")
                            if s == 0:
                                self.wait(te, sACT, f"B.cwbe.{b}.{e}.1")
                        if gg >= 2:
                            self.wait(te, sACT, f"B.sgdone.{gg - 2}")
                            self.wait(te, sDVE, f"B.ubdone.{gg - 2}")
                        w = sid % 2
                        tok0, tw = it["tok0"], it["tw"]
                        for dc in range(NCHUNK):
                            self.op(te, "matmul", pg(gg)[:, 0:tw],
                                    wga[:, w * NCHUNK * 1024 + dc * 1024 + fc * 128:
                                         w * NCHUNK * 1024 + dc * 1024 + fc * 128 + 128],
                                    xgT[:, dc * C + tok0: dc * C + tok0 + tw],
                                    start=(dc == 0), stop=(dc == NCHUNK - 1),
                                    inc=(sPE, 1) if dc == NCHUNK - 1 else None,
                                    mark=f"B.g.{gg}" if dc == NCHUNK - 1 else None)
                        umarks = [f"B.u.{gg}"]
                        if s == NS - 1 and fc == NFC - 1 and tp == 1:
                            umarks.append(f"B.gufree.{b}.{e}")
                        for dc in range(NCHUNK):
                            self.op(te, "matmul", pu(gg)[:, 0:tw],
                                    wua[:, w * NCHUNK * 1024 + dc * 1024 + fc * 128:
                                         w * NCHUNK * 1024 + dc * 1024 + fc * 128 + 128],
                                    xgT[:, dc * C + tok0: dc * C + tok0 + tw],
                                    start=(dc == 0), stop=(dc == NCHUNK - 1),
                                    inc=(sPE, 1) if dc == NCHUNK - 1 else None,
                                    mark=umarks if dc == NCHUNK - 1 else None)
                    elif k == "down":
                        b, e, s, tl = it["b"], it["e"], it["s"], it["tl"]
                        sid, g = it["sid"], it["god"]
                        if tl == 0:
                            self.wait(te, sPOOL, f"B.h.{sid}")
                        god_wait(te, g)
                        marks = [f"B.god.{g}"]
                        if tl == NTL - 1:
                            marks.append(f"B.dfree.{sid}")
                            marks.append(f"B.wfree.{sid}")
                            marks.append(f"B.hfree.{sid}")
                        nmm = 2 * NFC
                        i = 0
                        for dg in range(2):
                            for fc in range(NFC):
                                i += 1
                                self.op(te, "matmul",
                                        po(g)[:, dg * 512:(dg + 1) * 512],
                                        hb[:, fc * C + tl * 128: fc * C + tl * 128 + 128],
                                        wda[:, fc * 1024 + dg * 512: fc * 1024 + (dg + 1) * 512],
                                        start=(fc == 0), stop=(fc == NFC - 1),
                                        inc=(sPE, 1) if i == nmm else None,
                                        mark=marks if i == nmm else None)

            def f_scalar(se):
                for it in self.B_items:
                    k = it["kind"]
                    if k == "rwT":
                        c, g = it["c"], it["god"]
                        self.wait(se, sPE, f"B.god.{g}")
                        self.op(se, "copy", rwT[:, c * 8:(c + 1) * 8],
                                po(g)[0:128, 0:8], inc=(sACT, 1),
                                mark=[f"B.godone.{g}"] +
                                     (["B.rwTe"] if c == NCHUNK - 1 else []))
                    elif k == "logit":
                        b, tt, g = it["b"], it["tt"], it["god"]
                        self.wait(se, sPE, f"B.god.{g}")
                        if (b, tt) != (0, 0):
                            pb, pt = (b, tt - 1) if tt > 0 else (b - 1, 15)
                            self.wait(se, sDVE, f"B.cw.{pb}.{pt}")
                        self.op(se, "copy", lgt[:, :], po(g)[:, 0:8],
                                inc=(sACT, 1),
                                mark=[f"B.godone.{g}", f"B.lgt.{b}.{tt}"])
                    elif k == "smax":
                        b, tt = it["b"], it["tt"]
                        self.wait(se, sDVE, f"B.nrmax.{b}.{tt}")
                        self.op(se, "activation", ee[:, :], lgt[:, :], AF.Exp,
                                bias=rstat[:, 0:1], inc=(sACT, 1),
                                mark=f"B.ee.{b}.{tt}")
                    elif k == "pos":
                        b, tt, g = it["b"], it["tt"], it["god"]
                        self.wait(se, sPE, f"B.god.{g}")
                        if (b, tt) != (0, 0):
                            pb, pt = (b, tt - 1) if tt > 0 else (b - 1, 15)
                            self.wait(se, sDVE, f"B.offs.{pb}.{pt}")
                        self.op(se, "copy", posf[:, :], po(g)[:, 0:8],
                                inc=(sACT, 1),
                                mark=[f"B.posf.{b}.{tt}"])
                    elif k in ("cwbE", "cwbE2"):
                        b, e, g = it["b"], it["e"], it["god"]
                        half = 0 if k == "cwbE" else 1
                        hw = 512 if half == 0 else C - 512
                        self.wait(se, sPE, f"B.god.{g}")
                        if half == 0 and (b, e) != (0, 0):
                            pb, pe = (b, e - 1) if e > 0 else (b - 1, NEXP - 1)
                            self.wait(se, sDVE, f"B.ub.{self.B_last_gu[(pb, pe)]}")
                        self.op(se, "copy", cwb[:, half * 512:half * 512 + hw],
                                po(g)[:, 0:hw], inc=(sACT, 1),
                                mark=[f"B.godone.{g}", f"B.cwbe.{b}.{e}.{half}"])
                    elif k == "gu":
                        gg = it["gid"]
                        par = gg % 2
                        tw = it["tw"]
                        self.wait(se, sPE, f"B.g.{gg}")
                        if gg >= 2:
                            self.wait(se, sPOOL, f"B.hm.{gg - 2}")
                        self.op(se, "activation", sg[:, par * 512:par * 512 + tw],
                                pg(gg)[:, 0:tw], AF.Silu, inc=(sACT, 1),
                                mark=[f"B.sg.{gg}", f"B.sgdone.{gg}"])
                    elif k == "down":
                        if it["s"] != 0:
                            continue
                        b, e, tl, g = it["b"], it["e"], it["tl"], it["god"]
                        self.wait(se, sPE, f"B.god.{g}")
                        if tl == 0 and (b, e) != (0, 0):
                            pb, pe = (b, e - 1) if e > 0 else (b - 1, NEXP - 1)
                            self.wait(se, sW2, f"B.dst.{pb}.{pe}")
                        self.op(se, "copy", stg[:, tl * 1024:(tl + 1) * 1024],
                                po(g)[:, :], inc=(sACT, 1),
                                mark=[f"B.godone.{g}"])
                    elif k == "epilog":
                        b, tt, g = it["b"], it["tt"], it["g"]
                        par = g % 2
                        if tt == 0:
                            self.wait(se, sV, f"B.accg.{b}.{NEXP - 1}.15")
                        if g >= 2:
                            semz = sZ if g % 2 == 0 else sZ2
                            self.wait(se, semz, f"B.zw.{g - 2}")
                        self.op(se, "copy", rb[:, par * D:(par + 1) * D],
                                acc[:, tt * 1024:(tt + 1) * 1024],
                                inc=(sACT, 1), mark=f"B.racc.{g}")
                        self.wait(se, sDVE, f"B.var.{g}")
                        self.op(se, "activation", stat[:, par * 4 + 3:par * 4 + 4],
                                stat[:, par * 4 + 2:par * 4 + 3], AF.Sqrt,
                                scale=1.0 / D, bias=epsc[:, :],
                                inc=(sACT, 1), mark=f"B.sd.{g}")
                        self.wait(se, sDVE, f"B.z.{g}")
                        self.op(se, "activation", zb16[:, par * D:(par + 1) * D],
                                rb[:, par * D:(par + 1) * D], AF.Copy,
                                inc=(sACT, 1), mark=f"B.z16.{g}")

            def f_vector(ve):
                self.op(ve, "memset", epsc[:, :], EPS, inc=(sDVE, 1))
                self.op(ve, "memset", xs[:, :], 0.0, inc=(sDVE, 1), mark="B.xs0")
                for it in self.B_items:
                    k = it["kind"]
                    if k == "smax":
                        b, tt = it["b"], it["tt"]
                        self.wait(ve, sACT, f"B.lgt.{b}.{tt}")
                        self.op(ve, "reduce_max", rstat[:, 0:1], lgt[:, :],
                                axis=AX.X, negate=True, inc=(sDVE, 1),
                                mark=f"B.nrmax.{b}.{tt}")
                        self.op(ve, "tensor_scalar_mul", rstat[:, 1:2],
                                rstat[:, 0:1], -1.0, inc=(sDVE, 1))
                        self.op(ve, "tensor_scalar", eq[:, :], lgt[:, :],
                                rstat[:, 1:2], None, ALU.is_ge, inc=(sDVE, 1))
                        self.op(ve, "scalar_tensor_tensor", eq[:, :], eq[:, :],
                                -100000.0, lgt[:, :], ALU.mult, ALU.add,
                                inc=(sDVE, 1))
                        self.op(ve, "reduce_max", rstat[:, 2:3], eq[:, :],
                                axis=AX.X, inc=(sDVE, 1))
                        self.op(ve, "tensor_scalar", eqa[:, tt * 8:(tt + 1) * 8],
                                lgt[:, :], rstat[:, 2:3], None, ALU.is_ge,
                                inc=(sDVE, 1), mark=f"B.eqa.{b}.{tt}"
                                if tt == 15 else None)
                        self.op(ve, "tensor_tensor", ee[:, :], ee[:, :],
                                eqa[:, tt * 8:(tt + 1) * 8], ALU.mult,
                                inc=(sDVE, 1), waits=[(self.sems["sACT"], f"B.ee.{b}.{tt}")])
                        self.op(ve, "reduce_sum", rstat[:, 3:4], ee[:, :],
                                axis=AX.X, inc=(sDVE, 1))
                        self.op(ve, "reciprocal", rstat[:, 3:4], rstat[:, 3:4],
                                inc=(sDVE, 1))
                        self.op(ve, "tensor_scalar_mul", cwa[:, tt * 8:(tt + 1) * 8],
                                ee[:, :], rstat[:, 3:4], inc=(sDVE, 1),
                                mark=[f"B.cw.{b}.{tt}", f"B.cwa.{b}.{tt}"])
                    elif k == "xstg":
                        b, tt = it["b"], it["tt"]
                        semt = self.sems["sT"] if tt % 2 == 0 else self.sems["sT2"]
                        self.wait(ve, semt, f"B.xstg.{b}.{tt}")
                        self.op(ve, "tensor_copy",
                                xs[:, (tt % 2) * ROWW + D:(tt % 2) * ROWW + D + 8],
                                cwa[:, tt * 8:(tt + 1) * 8],
                                inc=(sDVE, 1), mark=f"B.xsrdy.{b}.{tt}")
                    elif k == "pos":
                        b, tt = it["b"], it["tt"]
                        self.wait(ve, sACT, f"B.posf.{b}.{tt}")
                        self.op(ve, "tensor_tensor", offw[:, :], posf[:, :],
                                eqa[:, tt * 8:(tt + 1) * 8], ALU.mult,
                                inc=(sDVE, 1))
                        self.op(ve, "scalar_tensor_tensor", posf[:, :],
                                eqa[:, tt * 8:(tt + 1) * 8], -100000.0,
                                offw[:, :], ALU.mult, ALU.add, inc=(sDVE, 1))
                        self.op(ve, "tensor_scalar_add", offw[:, :], posf[:, :],
                                100000.0, inc=(sDVE, 1))
                        self.op(ve, "tensor_copy", offs[:, tt * 8:(tt + 1) * 8],
                                offw[:, :], inc=(sDVE, 1),
                                mark=[f"B.offs.{b}.{tt}", f"B.godone.{it['god']}"])
                    elif k == "gu":
                        gg = it["gid"]
                        par = gg % 2
                        tok0, tw = it["tok0"], it["tw"]
                        self.wait(ve, sPE, f"B.u.{gg}")
                        if gg >= 2:
                            self.wait(ve, sPOOL, f"B.hm.{gg - 2}")
                        self.op(ve, "tensor_tensor", ub[:, par * 512:par * 512 + tw],
                                pu(gg)[:, 0:tw], cwb[:, tok0:tok0 + tw], ALU.mult,
                                inc=(sDVE, 1),
                                mark=[f"B.ub.{gg}", f"B.ubdone.{gg}"])
                    elif k == "down":
                        if it["s"] == 0:
                            continue
                        b, e, tl, g = it["b"], it["e"], it["tl"], it["god"]
                        self.wait(ve, sPE, f"B.god.{g}")
                        self.op(ve, "tensor_tensor", stg[:, tl * 1024:(tl + 1) * 1024],
                                stg[:, tl * 1024:(tl + 1) * 1024], po(g)[:, :],
                                ALU.add, inc=(sDVE, 1),
                                mark=[f"B.godone.{g}"] +
                                     ([f"B.stg.{b}.{e}"] if (it["s"] == NS - 1
                                                             and tl == NTL - 1) else []))
                    elif k == "epilog":
                        b, tt, g = it["b"], it["tt"], it["g"]
                        par = g % 2
                        self.wait(ve, sACT, f"B.racc.{g}")
                        if g >= 2:
                            semz = sZ if g % 2 == 0 else sZ2
                            self.wait(ve, semz, f"B.zw.{g - 2}")
                        self.op(ve, "reduce_sum", stat[:, par * 4:par * 4 + 1],
                                rb[:, par * D:(par + 1) * D], axis=AX.X,
                                negate=True, inc=(sDVE, 1))
                        self.op(ve, "tensor_scalar_mul", stat[:, par * 4 + 1:par * 4 + 2],
                                stat[:, par * 4:par * 4 + 1], 1.0 / D, inc=(sDVE, 1))
                        self.op(ve, "tensor_scalar_add", cb[:, 0:D],
                                rb[:, par * D:(par + 1) * D],
                                stat[:, par * 4 + 1:par * 4 + 2], inc=(sDVE, 1))
                        self.op(ve, "tensor_mul", rb[:, par * D:(par + 1) * D],
                                cb[:, 0:D],
                                cb[:, 0:D], inc=(sDVE, 1))
                        self.op(ve, "reduce_sum", stat[:, par * 4 + 2:par * 4 + 3],
                                rb[:, par * D:(par + 1) * D], axis=AX.X,
                                inc=(sDVE, 1), mark=f"B.var.{g}")
                        self.wait(ve, sACT, f"B.sd.{g}")
                        self.op(ve, "reciprocal", stat[:, par * 4 + 3:par * 4 + 4],
                                stat[:, par * 4 + 3:par * 4 + 4], inc=(sDVE, 1))
                        self.op(ve, "tensor_scalar_mul", rb[:, par * D:(par + 1) * D],
                                cb[:, 0:D],
                                stat[:, par * 4 + 3:par * 4 + 4],
                                inc=(sDVE, 1), mark=f"B.z.{g}")


            def f_gpsimd_merged(ge):
                if ge is not None:
                    ge.reg_mov(bcreg, C - 1)
                for it in self.B_items:
                    k = it["kind"]
                    if k == "accinit":
                        b = it["b"]
                        if b > 0:
                            self.wait(ge, sDVE, f"B.z.{(b - 1) * 16 + 15}")
                        self.dma(ge, bass.AP(acc, 0, [[16 * 1024, 128], [1024, 16], [1, 1024]]),
                                 t["z1_16"][b * S:(b + 1) * S, :]
                                 .rearrange("(tt p) d -> p tt d", p=128),
                                 sV2, mark=f"B.acc0.{b}")
                        self.wait(ge, sV2, f"B.acc0.{b}")
                    elif k == "xscat":
                        b, tt = it["b"], it["tt"]
                        self.wait(ge, sDVE, f"B.xsrdy.{b}.{tt}")
                        self.wait(ge, sDVE, f"B.offs.{b}.{tt}")
                        if tt == 0 and b > 0:
                            self.wait(ge, sW, f"B.xgl.{b - 1}.{NEXP - 1}")
                        par = tt % 2
                        for e in range(NEXP):
                            self.op(ge, "indirect_dma_start",
                                    xg_d[e][:, :],
                                    IndirectOffsetOnAxis(offs[:, tt * 8 + e:tt * 8 + e + 1], 0),
                                    xs[:, par * ROWW:(par + 1) * ROWW],
                                    None,
                                    bounds_check=bcreg, oob_is_err=False,
                                    inc=(sV, 16),
                                    mark=[f"B.xscat.{b}.{tt}"] +
                                         ([f"B.xscatall.{b}"] if it["last"] else [])
                                    if e == NEXP - 1 else None)
                        self.wait(ge, sV, f"B.xscat.{b}.{tt}")
                    elif k == "accglast":
                        b = it["b"]
                        e = NEXP - 1
                        self.wait(ge, sW2, f"B.dst.{b}.{e}")
                        for tt in range(16):
                            self.op(ge, "indirect_dma_start",
                                    acc[:, tt * 1024:(tt + 1) * 1024],
                                    None,
                                    dn_d[e][:, :],
                                    IndirectOffsetOnAxis(offs[:, tt * 8 + e:tt * 8 + e + 1], 0),
                                    bounds_check=bcreg, oob_is_err=False,
                                    compute_op=ALU.add,
                                    inc=(sV, 16),
                                    mark=f"B.accg.{b}.{e}.{tt}")
                            self.wait(ge, sV, f"B.accg.{b}.{e}.{tt}")
                    elif k == "gu":
                        gg = it["gid"]
                        b, e = it["b"], it["e"]
                        self.wait(ge, sACT, f"B.sg.{gg}")
                        self.wait(ge, sDVE, f"B.ub.{gg}")
                        sid = it["sid"]
                        if gg % 16 == 0 and sid >= 1:
                            self.wait(ge, sPE, f"B.hfree.{sid - 1}")
                        fc, tp = it["fc"], it["tp"]
                        tok0, tw = it["tok0"], it["tw"]
                        par = gg % 2
                        self.op(ge, "tensor_mul",
                                hb[:, fc * C + tok0: fc * C + tok0 + tw],
                                sg[:, par * 512:par * 512 + tw],
                                ub[:, par * 512:par * 512 + tw],
                                inc=(sPOOL, 1),
                                mark=[f"B.hm.{gg}"] +
                                     ([f"B.h.{sid}"] if gg % 16 == 15 else []))
                        if it["s"] == 0 and e >= 1:
                            tt = fc * 2 + tp
                            pe_ = e - 1
                            if tt == 0:
                                self.wait(ge, sW2, f"B.dst.{b}.{pe_}")
                            self.op(ge, "indirect_dma_start",
                                    acc[:, tt * 1024:(tt + 1) * 1024],
                                    None,
                                    dn_d[pe_][:, :],
                                    IndirectOffsetOnAxis(offs[:, tt * 8 + pe_:tt * 8 + pe_ + 1], 0),
                                    bounds_check=bcreg, oob_is_err=False,
                                    compute_op=ALU.add,
                                    inc=(sV, 16),
                                    mark=f"B.accg.{b}.{pe_}.{tt}")
                            self.wait(ge, sV, f"B.accg.{b}.{pe_}.{tt}")

            streams = {"sync": f_sync, "gpsimd": f_gpsimd_merged, "tensor": f_tensor,
                       "vector": f_vector, "scalar": f_scalar}
            self.run_phase(streams)

    def emit_phaseC(self):
        t = self.t
        sX = self.sems["sX"]
        sW, sW2 = self.sems["sW"], self.sems["sW2"]
        sZ, sZ2 = self.sems["sZ"], self.sems["sZ2"]
        sPE, sACT, sDVE = self.sems["sPE"], self.sems["sACT"], self.sems["sDVE"]
        sD = self.sems["sD2DC"]

        items = []
        ga = 0

        def G(kind, **kw):
            items.append(dict(kind=kind, **kw))

        for b in range(NB):
            gs = 0
            gy = 0
            gt = 0
            G("xh", b=b)
            for g in range(NKV):
                for j, kind in ((0, "kproj"), (1, "vproj")):
                    for qg in range(4):
                        G(kind, b=b, g=g, qg=qg, ga=ga, esem="sACT"); ga += 1
                for kt in range(16):
                    G("vtrans", b=b, g=g, kt=kt, gt=gt); gt += 1
                G("krope", b=b, g=g)
                G("sq1", b=b, g=g)
                for qg in range(4):
                    G("knorm", b=b, g=g, qg=qg, ga=ga, esem="sACT"); ga += 1
                G("m2c", b=b, g=g)
                for hq in range(4):
                    h = g * 4 + hq
                    for qg in range(4):
                        G("qproj", b=b, g=g, h=h, qg=qg, ga=ga, esem="sDVE"); ga += 1
                    G("qrope", b=b, g=g, h=h)
                    G("sq0", b=b, g=g, h=h)
                    for qg in range(4):
                        G("qnorm", b=b, g=g, h=h, qg=qg, ga=ga, esem="sDVE"); ga += 1
                    G("aug", b=b, g=g, h=h)
                    for qg in range(4):
                        for kt in range(16):
                            G("score", b=b, g=g, h=h, qg=qg, kt=kt, gs=gs,
                              i=kt, gy=gy, nk=16)
                            gs += 1
                        G("recip", b=b, h=h, qg=qg, gy=gy)
                        G("bcast", b=b, h=h, qg=qg, ga=ga, gy=gy, esem="sACT"); ga += 1
                        G("ymul", b=b, h=h, qg=qg, gy=gy,
                          last=(h == NH - 1 and qg == 3))
                        gy += 1
            for tt in range(16):
                G("outproj", b=b, tt=tt, g2=b * 16 + tt)
                G("epilog", b=b, tt=tt, g2=b * 16 + tt)
        self.C_items = items
        self.C_ga_item = {it["ga"]: it for it in items if "ga" in it}
        self.C_last_qnorm = {}
        for it in items:
            if it["kind"] == "qnorm":
                self.C_last_qnorm[(it["b"], it["h"])] = it["ga"]
            if it["kind"] == "knorm":
                self.C_last_qnorm[(it["b"], "k", it["g"])] = it["ga"]

        with ExitStack() as st:
            xh = self.sb(st, "C_xh", [128, NCHUNK * S], BF16)
            wqa = self.sb(st, "C_wq", [128, NCHUNK * D], BF16)
            wka = self.sb(st, "C_wk", [128, NCHUNK * 256], BF16)
            wva = self.sb(st, "C_wv", [128, NCHUNK * 256], BF16)
            woa = self.sb(st, "C_wo", [128, NCHUNK * D], BF16)
            qh = self.sb(st, "C_qh", [96, S], BF16)
            kh = self.sb(st, "C_kh", [96, S], BF16)
            qP = self.sb(st, "C_qP", [64, S], BF16)
            kP = self.sb(st, "C_kP", [64, S], BF16)
            vt = self.sb(st, "C_vt", [64, S], F32)
            va = self.sb(st, "C_va", [128, 16 * 96], BF16)
            q2 = self.sb(st, "C_q2", [64, S], BF16)
            rta = self.sb(st, "C_rta", [32, S], BF16)
            rtb = self.sb(st, "C_rtb", [32, S], BF16)
            rtc = self.sb(st, "C_rtc", [32, S], BF16)
            cosT = self.sb(st, "C_cos", [32, S], BF16)
            sinT = self.sb(st, "C_sin", [32, S], BF16)
            n_row = self.sb(st, "C_nrow", [1, S], BF16)
            b_row = self.sb(st, "C_brow", [1, S], F32)
            m2c = self.sb(st, "C_m2c", [1, 2], F32)
            pbuf = self.sb(st, "C_pbuf", [128, 2 * 1024], BF16)
            rzrow = self.sb(st, "C_rzrow", [1, 512], F32)
            rzb = self.sb(st, "C_rzb", [64, 512], F32)
            yt = self.sb(st, "C_yt", [128, NCHUNK * S], BF16)
            id16 = self.sb(st, "C_id16", [128, 128], BF16)
            id32 = self.sb(st, "C_id32", [128, 128], F32)
            o128_32 = self.sb(st, "C_o128_32", [1, 128], F32)
            o64_16 = self.sb(st, "C_o64_16", [64, 1], BF16)
            z2t = self.sb(st, "C_z2t", [128, 2 * D], F32)
            rb = self.sb(st, "C_rb", [128, 2 * D], F32)
            cb = self.sb(st, "C_cb", [128, D], F32)
            zb16 = self.sb(st, "C_zb16", [128, 2 * D], BF16)
            stat = self.sb(st, "C_stat", [128, 8], F32)
            epsc = self.sb(st, "C_epsc", [128, 1], F32)

            psA = self.ps(st, "C_psA", [64, 2 * 512])
            psS = self.ps(st, "C_psS", [128, 4 * 512])
            psY = self.ps(st, "C_psY", [96, 512])
            psO = self.ps(st, "C_psO", [128, 512])

            def pa(sl):
                return psA[:, (sl % 2) * 512:(sl % 2 + 1) * 512]

            def pss(gs):
                sp, half = (gs // 2) % 2, gs % 2
                return psS[:, sp * 1024 + half * 512: sp * 1024 + (half + 1) * 512]

            def pssp(gp):
                sp = gp % 2
                return psS[:, sp * 1024:(sp + 1) * 1024]

            def pb(gs):
                sp, half = (gs // 2) % 2, gs % 2
                return pbuf[:, sp * 1024 + half * 512: sp * 1024 + (half + 1) * 512]

            def pbp(gp):
                sp = gp % 2
                return pbuf[:, sp * 1024:(sp + 1) * 1024]

            def ga_wait(eng, g):
                if g >= 2:
                    it2 = self.C_ga_item[g - 2]
                    self.wait(eng, self.sems[it2["esem"]], f"C.gadone.{g - 2}")

            def f_sync(sy):
                self.dma(sy, id16[:, :], self.c_ident16[:, :], sX)
                self.dma(sy, id32[:, :], self.c_ident32[:, :], sX)
                self.dma(sy, o128_32[:, :], self.c_ones1x128_32[:, :], sX)
                self.dma(sy, o64_16[:, :], self.c_ones64x1_16[:, :], sX)
                self.dma(sy, cosT[:, :], self.c_ropecos[:, :], sX)
                self.dma(sy, sinT[:, :], self.c_ropesin[:, :], sX)
                self.wait(sy, sD, "d2dC")
                self.dma(sy, bass.AP(wqa, 0, [[NCHUNK * D, 128], [D, NCHUNK], [1, D]]),
                         t["wq1_16"][:, :].rearrange("(c p) e -> p c e", p=128), sX)
                self.dma(sy, bass.AP(wka, 0, [[NCHUNK * 256, 128], [256, NCHUNK], [1, 256]]),
                         t["wk1_16"][:, :].rearrange("(c p) e -> p c e", p=128), sX)
                self.dma(sy, bass.AP(wva, 0, [[NCHUNK * 256, 128], [256, NCHUNK], [1, 256]]),
                         t["wv1_16"][:, :].rearrange("(c p) e -> p c e", p=128), sX)
                self.dma(sy, bass.AP(woa, 0, [[NCHUNK * D, 128], [D, NCHUNK], [1, D]]),
                         t["wo1_16"][:, :].rearrange("(c p) e -> p c e", p=128), sX,
                         mark="C.setup")
                self.wait(sy, sX, "C.setup")
                for b in range(NB):
                    if b > 0:
                        self.wait(sy, sPE, f"C.xhfree.{b - 1}")
                    for c in range(NCHUNK):
                        self.dma(sy, xh[:, c * S:(c + 1) * S],
                                 t["z2_16"][b * S:(b + 1) * S,
                                            c * 128:(c + 1) * 128], sX,
                                 transpose=True,
                                 mark=f"C.xh.{b}" if c == NCHUNK - 1 else None)
                    for g2 in range(b * 16, b * 16 + 16):
                        tt = g2 % 16
                        semx = sW if g2 % 2 == 0 else sW2
                        if g2 >= 2:
                            self.wait(sy, sDVE, f"C.zused.{g2 - 2}")
                        r0 = b * S + tt * 128
                        self.dma(sy, z2t[:, (g2 % 2) * D:(g2 % 2 + 1) * D],
                                 t["z2_32"][r0:r0 + 128, :], semx,
                                 mark=f"C.z2t.{g2}")
                        if g2 >= 2:
                            gw = g2 - 2
                            semz = sZ if gw % 2 == 0 else sZ2
                            self.wait(sy, sDVE, f"C.z.{gw}")
                            self.wait(sy, sACT, f"C.z16.{gw}")
                            rw0 = (gw // 16) * S + (gw % 16) * 128
                            self.dma(sy, t["z3_32"][rw0:rw0 + 128, :],
                                     rb[:, (gw % 2) * D:(gw % 2 + 1) * D], semz)
                            self.dma(sy, t["z3_16"][rw0:rw0 + 128, :],
                                     zb16[:, (gw % 2) * D:(gw % 2 + 1) * D], semz,
                                     mark=f"C.zw.{gw}")
                for gw in (NB * 16 - 2, NB * 16 - 1):
                    semz = sZ if gw % 2 == 0 else sZ2
                    self.wait(sy, sDVE, f"C.z.{gw}")
                    self.wait(sy, sACT, f"C.z16.{gw}")
                    rw0 = (gw // 16) * S + (gw % 16) * 128
                    self.dma(sy, t["z3_32"][rw0:rw0 + 128, :],
                             rb[:, (gw % 2) * D:(gw % 2 + 1) * D], semz)
                    self.dma(sy, t["z3_16"][rw0:rw0 + 128, :],
                             zb16[:, (gw % 2) * D:(gw % 2 + 1) * D], semz,
                             mark=f"C.zw.{gw}")
                self.wait(sy, sZ, sZ.n)
                self.wait(sy, sZ2, sZ2.n)

            def f_tensor(te):
                self.wait(te, sX, "C.setup")
                for it in self.C_items:
                    k = it["kind"]
                    if k in ("kproj", "vproj", "qproj"):
                        b, g, qg, gg = it["b"], it["g"], it["qg"], it["ga"]
                        if k == "kproj" and qg == 0 and g == 0:
                            self.wait(te, sX, f"C.xh.{b}")
                        ga_wait(te, gg)
                        if k == "qproj":
                            woff = it["h"] * 64
                            warr, nw = wqa, D
                        elif k == "kproj":
                            woff = g * 64
                            warr, nw = wka, 256
                        else:
                            woff = g * 64
                            warr, nw = wva, 256
                        marks = [f"C.ga.{gg}"]
                        if (k == "qproj" and qg == 3 and it["h"] == NH - 1):
                            marks.append(f"C.xhfree.{b}")
                        for c in range(NCHUNK):
                            self.op(te, "matmul", pa(gg)[0:64, :],
                                    warr[:, c * nw + woff: c * nw + woff + 64],
                                    xh[:, c * S + qg * 512: c * S + (qg + 1) * 512],
                                    start=(c == 0), stop=(c == NCHUNK - 1),
                                    inc=(sPE, 1) if c == NCHUNK - 1 else None,
                                    mark=marks if c == NCHUNK - 1 else None)
                    elif k in ("qnorm", "knorm"):
                        b, g, qg, gg = it["b"], it["g"], it["qg"], it["ga"]
                        key = (f"C.sq0.{b}.{it['h']}" if k == "qnorm"
                               else f"C.sq1.{b}.{g}")
                        self.wait(te, sDVE, key)
                        ga_wait(te, gg)
                        self.op(te, "matmul", pa(gg)[0:1, :],
                                o64_16[:, :], q2[:, qg * 512:(qg + 1) * 512],
                                start=True, stop=True, inc=(sPE, 1),
                                mark=f"C.ga.{gg}")
                    elif k == "vtrans":
                        b, g, kt, gt = it["b"], it["g"], it["kt"], it["gt"]
                        if kt == 0:
                            self.wait(te, sACT, f"C.vt.{b}.{g}")
                            if g == 0 and b > 0:
                                self.wait(te, sDVE, f"C.psO.done.{b * 16 - 1}.1")
                        if gt >= 1:
                            self.wait(te, sDVE, f"C.psT.done.{b}.{gt - 1}")
                        self.op(te, "transpose", psO[:, 0:64],
                                vt[:, kt * 128:(kt + 1) * 128], id32[0:64, 0:64],
                                inc=(sPE, 1), mark=f"C.psT.{b}.{gt}")
                    elif k == "score":
                        b, h, qg, kt, gs, i = (it["b"], it["h"], it["qg"],
                                               it["kt"], it["gs"], it["i"])
                        if i == 0 and qg == 0:
                            self.wait(te, sDVE, f"C.aug.{b}.{h}")
                            if it["h"] % 4 == 0:
                                self.wait(te, sDVE, f"C.krope.{b}.{it['g']}")
                        if gs % 2 == 0 and gs >= 4:
                            self.wait(te, sACT, f"C.expdone.{b}.{gs // 2 - 2}")
                        self.op(te, "matmul", pss(gs),
                                kh[:, kt * 128:(kt + 1) * 128],
                                qh[:, qg * 512:(qg + 1) * 512],
                                start=True, stop=True,
                                inc=(sPE, 1), mark=f"C.psS.{b}.{gs}")
                        if gs % 2 == 1:
                            self._yC_mm(te, b, gs - 3)
                            self._yC_mm(te, b, gs - 2)
                        if i == it["nk"] - 1:
                            self._yC_mm(te, b, gs - 1)
                            self._yC_mm(te, b, gs)
                    elif k == "bcast":
                        b, h, qg, gg = it["b"], it["h"], it["qg"], it["ga"]
                        self.wait(te, sDVE, f"C.rz.{b}.{h}.{qg}")
                        ga_wait(te, gg)
                        self.op(te, "matmul", pa(gg)[0:64, :],
                                o128_32[0:1, 0:64], rzrow[0:1, :],
                                start=True, stop=True, inc=(sPE, 1),
                                mark=f"C.ga.{gg}")
                    elif k == "outproj":
                        b, tt, g2 = it["b"], it["tt"], it["g2"]
                        if tt == 0:
                            self.wait(te, sDVE, f"C.ytdone.{b}")
                        for dg in range(2):
                            if not (g2 == 0 and dg == 0):
                                self.wait(te, sDVE,
                                          f"C.psO.done.{g2}.0" if dg == 1
                                          else f"C.psO.done.{g2 - 1}.1")
                            for ci in range(NCHUNK):
                                self.op(te, "matmul",
                                        psO[:, 0:512],
                                        yt[:, ci * S + tt * 128: ci * S + (tt + 1) * 128],
                                        woa[:, ci * D + dg * 512: ci * D + (dg + 1) * 512],
                                        start=(ci == 0), stop=(ci == NCHUNK - 1),
                                        inc=(sPE, 1) if ci == NCHUNK - 1 else None,
                                        mark=f"C.psO.{g2}.{dg}" if ci == NCHUNK - 1 else None)

            self._C_ymeta = {}
            for it in items:
                if it["kind"] == "score":
                    self._C_ymeta[(it["b"], it["gs"])] = it

            def yC_mm(te, b, gs):
                if gs < 0:
                    return
                itm = self._C_ymeta.get((b, gs))
                if itm is None or itm.get("ydone"):
                    return
                itm["ydone"] = True
                kt, i, gy = itm["kt"], itm["i"], itm["gy"]
                self.wait(te, sACT, f"C.expp.{b}.{gs // 2}")
                if i == 0 and gy >= 1:
                    self.wait(te, sDVE, f"C.ymul.{b}.{gy - 1}")
                last = (i == itm["nk"] - 1)
                self.op(te, "matmul", psY[:, :],
                        va[:, kt * 96:(kt + 1) * 96], pb(gs),
                        start=(i == 0), stop=last,
                        inc=(sPE, 1) if last else None,
                        mark=f"C.psY.{b}.{gy}" if last else None)

            self._yC_mm = lambda te, b, gs: yC_mm(te, b, gs)

            def f_scalar(se):
                for it in self.C_items:
                    k = it["kind"]
                    if k in ("kproj", "vproj"):
                        b, g, qg, gg = it["b"], it["g"], it["qg"], it["ga"]
                        self.wait(se, sPE, f"C.ga.{gg}")
                        tgt = kP if k == "kproj" else vt
                        marks = [f"C.gadone.{gg}"]
                        if qg == 3:
                            marks.append(f"C.{'kv'[k == 'vproj']}evac.{b}.{g}")
                            if k == "vproj":
                                marks.append(f"C.vt.{b}.{g}")
                        self.op(se, "copy", tgt[0:64, qg * 512:(qg + 1) * 512],
                                pa(gg)[0:64, :], inc=(sACT, 1), mark=marks)
                    elif k == "knorm":
                        b, g, qg, gg = it["b"], it["g"], it["qg"], it["ga"]
                        self.wait(se, sPE, f"C.ga.{gg}")
                        self.op(se, "copy",
                                n_row[0:1, qg * 512:(qg + 1) * 512],
                                pa(gg)[0:1, :], inc=(sACT, 1),
                                mark=[f"C.gadone.{gg}"] +
                                     ([f"C.nrow1.{b}.{g}"] if qg == 3 else []))
                    elif k == "score":
                        b, gs = it["b"], it["gs"]
                        if gs % 2 == 0:
                            continue
                        gp = gs // 2
                        self.wait(se, sPE, f"C.psS.{b}.{gs}")
                        self.op(se, "activation", pbp(gp), pssp(gp), AF.Exp,
                                inc=(sACT, 1),
                                mark=[f"C.expp.{b}.{gp}", f"C.expdone.{b}.{gp}"])
                    elif k == "bcast":
                        b, h, qg, gg = it["b"], it["h"], it["qg"], it["ga"]
                        self.wait(se, sPE, f"C.ga.{gg}")
                        self.op(se, "copy", rzb[:, :], pa(gg)[0:64, :],
                                inc=(sACT, 1),
                                mark=[f"C.gadone.{gg}", f"C.rzb.{b}.{h}.{qg}"])
                    elif k == "epilog":
                        g2 = it["g2"]
                        par = g2 % 2
                        self.wait(se, sDVE, f"C.var.{g2}")
                        self.op(se, "activation", stat[:, par * 4 + 3:par * 4 + 4],
                                stat[:, par * 4 + 2:par * 4 + 3], AF.Sqrt,
                                scale=1.0 / D, bias=epsc[:, :],
                                inc=(sACT, 1), mark=f"C.sd.{g2}")
                        self.wait(se, sDVE, f"C.z.{g2}")
                        self.op(se, "activation", zb16[:, par * D:(par + 1) * D],
                                rb[:, par * D:(par + 1) * D], AF.Copy,
                                inc=(sACT, 1), mark=f"C.z16.{g2}")

            def f_vector(ve):
                self.op(ve, "memset", epsc[:, :], EPS, inc=(sDVE, 1))
                self.op(ve, "memset", qh[64:96, :], 0.0, inc=(sDVE, 1))
                self.op(ve, "memset", kh[64:96, :], 0.0, inc=(sDVE, 1))
                self.op(ve, "memset", va[:, :], 0.0, inc=(sDVE, 1))
                self.op(ve, "memset",
                        bass.AP(va, 64, [[16 * 96, 128], [96, 16], [1, 1]]), 1.0,
                        inc=(sDVE, 1))

                def rope(dst, srcP):
                    self.op(ve, "tensor_copy", rtc[:, :], srcP[32:64, :],
                            inc=(sDVE, 1))
                    self.op(ve, "tensor_mul", rta[:, :], srcP[0:32, :], cosT[:, :],
                            inc=(sDVE, 1))
                    self.op(ve, "tensor_mul", rtb[:, :], rtc[:, :], sinT[:, :],
                            inc=(sDVE, 1))
                    self.op(ve, "tensor_sub", dst[0:32, :], rta[:, :], rtb[:, :],
                            inc=(sDVE, 1))
                    self.op(ve, "tensor_mul", rta[:, :], srcP[0:32, :], sinT[:, :],
                            inc=(sDVE, 1))
                    self.op(ve, "tensor_mul", rtb[:, :], rtc[:, :], cosT[:, :],
                            inc=(sDVE, 1))
                    self.op(ve, "tensor_add", rta[:, :], rta[:, :], rtb[:, :],
                            inc=(sDVE, 1))
                    self.op(ve, "tensor_copy", dst[32:64, :], rta[:, :],
                            inc=(sDVE, 1))

                for it in self.C_items:
                    k = it["kind"]
                    if k == "qproj":
                        b, qg, gg = it["b"], it["qg"], it["ga"]
                        self.wait(ve, sPE, f"C.ga.{gg}")
                        self.op(ve, "tensor_scalar_mul",
                                qP[0:64, qg * 512:(qg + 1) * 512],
                                pa(gg)[0:64, :], 0.125,
                                inc=(sDVE, 1), mark=[f"C.gadone.{gg}"])
                    elif k == "qrope":
                        b, h = it["b"], it["h"]
                        rope(qh, qP)
                    elif k == "krope":
                        b, g = it["b"], it["g"]
                        self.wait(ve, sACT, f"C.kevac.{b}.{g}")
                        rope(kh, kP)
                        self.op(ve, "memset", kh[64:65, :], 1.0,
                                inc=(sDVE, 1), mark=f"C.krope.{b}.{g}")
                    elif k == "sq0":
                        b, h = it["b"], it["h"]
                        if h % 4 == 0:
                            key = self.C_last_qnorm[(b, "k", it["g"])]
                        else:
                            key = self.C_last_qnorm[(b, h - 1)]
                        self.wait(ve, sPE, f"C.ga.{key}")
                        self.op(ve, "tensor_mul", q2[:, :], qh[0:64, :], qh[0:64, :],
                                inc=(sDVE, 1), mark=f"C.sq0.{b}.{h}")
                    elif k == "sq1":
                        b, g = it["b"], it["g"]
                        if g > 0:
                            self.wait(ve, sPE, f"C.ga.{self.C_last_qnorm[(b, g * 4 - 1)]}")
                        elif b > 0:
                            self.wait(ve, sPE, f"C.ga.{self.C_last_qnorm[(b - 1, NH - 1)]}")
                        self.op(ve, "tensor_mul", q2[:, :], kh[0:64, :], kh[0:64, :],
                                inc=(sDVE, 1), mark=f"C.sq1.{b}.{g}")
                    elif k == "m2c":
                        b, g = it["b"], it["g"]
                        self.wait(ve, sACT, f"C.nrow1.{b}.{g}")
                        self.op(ve, "reduce_max", m2c[0:1, 0:1], n_row[0:1, 0:S],
                                axis=AX.X, inc=(sDVE, 1))
                        self.op(ve, "tensor_scalar_mul", m2c[0:1, 1:2], m2c[0:1, 0:1],
                                0.5 / LAM, inc=(sDVE, 1), mark=f"C.m2c.{b}.{g}")
                    elif k == "qnorm":
                        b, qg, gg = it["b"], it["qg"], it["ga"]
                        self.wait(ve, sPE, f"C.ga.{gg}")
                        self.op(ve, "tensor_scalar_mul",
                                b_row[0:1, qg * 512:(qg + 1) * 512],
                                pa(gg)[0:1, :], 0.5 * LAM,
                                inc=(sDVE, 1), mark=[f"C.gadone.{gg}"])
                    elif k == "aug":
                        b, h = it["b"], it["h"]
                        self.op(ve, "tensor_scalar_add", b_row[0:1, :], b_row[0:1, :],
                                m2c[0:1, 1:2], inc=(sDVE, 1))
                        self.op(ve, "tensor_scalar_mul", qh[64:65, :], b_row[0:1, :],
                                -1.0, inc=(sDVE, 1), mark=f"C.aug.{b}.{h}")
                    elif k == "vtrans":
                        b, gt, kt = it["b"], it["gt"], it["kt"]
                        self.wait(ve, sPE, f"C.psT.{b}.{gt}")
                        self.op(ve, "tensor_copy", va[:, kt * 96:kt * 96 + 64],
                                psO[:, 0:64], inc=(sDVE, 1),
                                mark=f"C.psT.done.{b}.{gt}")
                    elif k == "recip":
                        b, h, qg, gy = it["b"], it["h"], it["qg"], it["gy"]
                        self.wait(ve, sPE, f"C.psY.{b}.{gy}")
                        self.op(ve, "reciprocal", rzrow[0:1, :], psY[64:65, :],
                                inc=(sDVE, 1), mark=f"C.rz.{b}.{h}.{qg}")
                    elif k == "ymul":
                        b, h, qg, gy = it["b"], it["h"], it["qg"], it["gy"]
                        self.wait(ve, sACT, f"C.rzb.{b}.{h}.{qg}")
                        strip, row = h // 2, (h % 2) * 64
                        self.op(ve, "tensor_mul",
                                yt[row:row + 64,
                                   strip * S + qg * 512: strip * S + (qg + 1) * 512],
                                psY[0:64, :], rzb[:, :],
                                inc=(sDVE, 1),
                                mark=[f"C.ymul.{b}.{gy}"] +
                                     ([f"C.ytdone.{b}"] if it["last"] else []))
                    elif k == "epilog":
                        b, tt, g2 = it["b"], it["tt"], it["g2"]
                        par = g2 % 2
                        semx = sW if g2 % 2 == 0 else sW2
                        self.wait(ve, semx, f"C.z2t.{g2}")
                        if g2 >= 2:
                            semz = sZ if g2 % 2 == 0 else sZ2
                            self.wait(ve, semz, f"C.zw.{g2 - 2}")
                        self.wait(ve, sPE, f"C.psO.{g2}.0")
                        self.op(ve, "tensor_add", rb[:, par * D:par * D + 512],
                                psO[:, 0:512], z2t[:, par * D:par * D + 512],
                                inc=(sDVE, 1), mark=f"C.psO.done.{g2}.0")
                        self.wait(ve, sPE, f"C.psO.{g2}.1")
                        self.op(ve, "tensor_add", rb[:, par * D + 512:(par + 1) * D],
                                psO[:, 0:512], z2t[:, par * D + 512:(par + 1) * D],
                                inc=(sDVE, 1),
                                mark=[f"C.psO.done.{g2}.1", f"C.zused.{g2}"])
                        self.op(ve, "reduce_sum", stat[:, par * 4:par * 4 + 1],
                                rb[:, par * D:(par + 1) * D], axis=AX.X,
                                negate=True, inc=(sDVE, 1))
                        self.op(ve, "tensor_scalar_mul", stat[:, par * 4 + 1:par * 4 + 2],
                                stat[:, par * 4:par * 4 + 1], 1.0 / D, inc=(sDVE, 1))
                        self.op(ve, "tensor_scalar_add", cb[:, 0:D],
                                rb[:, par * D:(par + 1) * D],
                                stat[:, par * 4 + 1:par * 4 + 2], inc=(sDVE, 1))
                        self.op(ve, "tensor_mul", rb[:, par * D:(par + 1) * D],
                                cb[:, 0:D],
                                cb[:, 0:D], inc=(sDVE, 1))
                        self.op(ve, "reduce_sum", stat[:, par * 4 + 2:par * 4 + 3],
                                rb[:, par * D:(par + 1) * D], axis=AX.X,
                                inc=(sDVE, 1), mark=f"C.var.{g2}")
                        self.wait(ve, sACT, f"C.sd.{g2}")
                        self.op(ve, "reciprocal", stat[:, par * 4 + 3:par * 4 + 4],
                                stat[:, par * 4 + 3:par * 4 + 4], inc=(sDVE, 1))
                        self.op(ve, "tensor_scalar_mul", rb[:, par * D:(par + 1) * D],
                                cb[:, 0:D],
                                stat[:, par * 4 + 3:par * 4 + 4],
                                inc=(sDVE, 1), mark=f"C.z.{g2}")

            def reset_items():
                for it in items:
                    it.pop("ydone", None)

            streams = {"sync": f_sync,
                       "tensor": lambda e: (reset_items(), f_tensor(e)),
                       "vector": f_vector, "scalar": f_scalar}
            self.run_phase(streams)


    def emit_phaseD(self):
        t = self.t
        nc = self.nc
        sX = self.sems["sX"]
        sW, sW2 = self.sems["sW"], self.sems["sW2"]
        sV, sV2 = self.sems["sV"], self.sems["sV2"]
        sZ, sZ2 = self.sems["sZ"], self.sems["sZ2"]
        sPE, sACT = self.sems["sPE"], self.sems["sACT"]
        sDVE, sPOOL = self.sems["sDVE"], self.sems["sPOOL"]
        sD = self.sems["sD2DD"]
        NCYC = 3

        dtmp = nc.dram_tensor("D_tmp16", [S, D], BF16, kind="Internal")

        items = []
        gd = 0

        def G(kind, **kw):
            items.append(dict(kind=kind, **kw))

        G("biasprep")
        for b in range(NB):
            G("xh", b=b, cyc=0)
            G("gctx", b=b)
            G("gnet", b=b)
            G("oload", b=b)
            for cyc in range(NCYC):
                for tg in range(4):
                    for oc in range(NCHUNK):
                        G("h1", b=b, cyc=cyc, tg=tg, oc=oc, gd=gd, esem="sACT"); gd += 1
                    for oc in range(4):
                        G("h2", b=b, cyc=cyc, tg=tg, oc=oc, gd=gd, esem="sACT"); gd += 1
                    for oc in range(2):
                        G("h3", b=b, cyc=cyc, tg=tg, oc=oc, gd=gd, esem="sACT"); gd += 1
                    for j in range(4):
                        G("tf", b=b, cyc=cyc, tg=tg, j=j, tt=tg * 4 + j)
                        G("twist", b=b, cyc=cyc, tg=tg, j=j, tt=tg * 4 + j)
                if cyc < NCYC - 1:
                    for tt in range(16):
                        G("oout", b=b, cyc=cyc, tt=tt)
                    G("xh", b=b, cyc=cyc + 1)
            for tt in range(16):
                G("final", b=b, tt=tt, g2=b * 16 + tt)
        self.D_items = items
        self.D_gd_item = {it["gd"]: it for it in items if "gd" in it}

        with ExitStack() as st:
            xh = self.sb(st, "D_xh", [128, NCHUNK * S], BF16)
            out32 = self.sb(st, "D_out32", [128, 16 * D], F32)
            h1a = self.sb(st, "D_h1", [128, NCHUNK * 512], BF16)
            h2a = self.sb(st, "D_h2", [128, 4 * 512], BF16)
            h3a = self.sb(st, "D_h3", [128, 2 * 512], BF16)
            scr = self.sb(st, "D_scr", [128, D], F32)
            w1a = self.sb(st, "D_w1", [128, NCHUNK * D], BF16)
            w2a = self.sb(st, "D_w2", [128, NCHUNK * 512], BF16)
            w3a = self.sb(st, "D_w3", [128, 4 * 256], BF16)
            w4a = self.sb(st, "D_w4", [128, 2], BF16)
            gw1a = self.sb(st, "D_gw1", [128, NCHUNK * 512], BF16)
            gw2a = self.sb(st, "D_gw2", [128, 4 * 256], BF16)
            gw3a = self.sb(st, "D_gw3", [128, 2], BF16)
            brow = self.sb(st, "D_brow", [1, D], F32)
            b1c = self.sb(st, "D_b1c", [128, NCHUNK], F32)
            b2c = self.sb(st, "D_b2c", [128, 4], F32)
            b3c = self.sb(st, "D_b3c", [128, 2], F32)
            b4c = self.sb(st, "D_b4c", [128, 1], F32)
            g1bc = self.sb(st, "D_g1bc", [128, 4], F32)
            g2bc = self.sb(st, "D_g2bc", [128, 2], F32)
            g3b = self.sb(st, "D_g3b", [1, 1], F32)
            arv = self.sb(st, "D_ar", [1, 1], F32)
            sc1 = self.sb(st, "D_sc1", [1, 8], F32)
            gcT = self.sb(st, "D_gc", [128, NCHUNK], F32)
            gc16 = self.sb(st, "D_gc16", [128, NCHUNK], BF16)
            g1_16 = self.sb(st, "D_g1x", [128, 4], BF16)
            g2_16 = self.sb(st, "D_g2x", [128, 2], BF16)
            acol = self.sb(st, "D_acol", [128, 2], F32)
            cpl = self.sb(st, "D_cpl", [128, 2], F32)
            scol = self.sb(st, "D_scol", [128, 1], F32)
            ob16 = self.sb(st, "D_ob16", [128, 2 * D], BF16)
            id32 = self.sb(st, "D_id32", [128, 128], F32)
            o128_32 = self.sb(st, "D_o128_32", [1, 128], F32)
            z3t = self.sb(st, "D_z3t", [128, 2 * D], F32)
            rb = self.sb(st, "D_rb", [128, 2 * D], F32)
            cb = self.sb(st, "D_cb", [128, 2 * D], F32)
            stat = self.sb(st, "D_stat", [128, 8], F32)
            epsc = self.sb(st, "D_epsc", [128, 1], F32)

            psH = self.ps(st, "D_psH", [128, 2 * 512])
            psM = self.ps(st, "D_psM", [128, 2])
            psB = self.ps(st, "D_psB", [128, 16])

            def ph(i):
                return psH[:, (i % 2) * 512:(i % 2 + 1) * 512]

            def gd_wait(eng, g):
                if g >= 2:
                    it2 = self.D_gd_item[g - 2]
                    self.wait(eng, self.sems[it2["esem"]], f"D.gdone.{g - 2}")

            def f_sync(sy):
                self.dma(sy, id32[:, :], self.c_ident32[:, :], sX)
                self.dma(sy, o128_32[:, :], self.c_ones1x128_32[:, :], sX)
                self.wait(sy, sD, "d2dD")
                self.dma(sy, bass.AP(w1a, 0, [[NCHUNK * D, 128], [D, NCHUNK], [1, D]]),
                         t["cn_w1_16"][:, :].rearrange("(c p) e -> p c e", p=128), sX)
                self.dma(sy, bass.AP(w2a, 0, [[NCHUNK * 512, 128], [512, NCHUNK], [1, 512]]),
                         t["cn_w2_16"][:, :].rearrange("(c p) e -> p c e", p=128), sX)
                self.dma(sy, bass.AP(w3a, 0, [[4 * 256, 128], [256, 4], [1, 256]]),
                         t["cn_w3_16"][:, :].rearrange("(c p) e -> p c e", p=128), sX)
                self.dma(sy, bass.AP(w4a, 0, [[2, 128], [1, 2], [1, 1]]),
                         t["cn_w4_16"][:, :].rearrange("(c p) e -> p c e", p=128), sX,
                         allow_slow_non_contiguous=True)
                self.dma(sy, bass.AP(gw1a, 0, [[NCHUNK * 512, 128], [512, NCHUNK], [1, 512]]),
                         t["gc_w1_16"][:, :].rearrange("(c p) e -> p c e", p=128), sX)
                self.dma(sy, bass.AP(gw2a, 0, [[4 * 256, 128], [256, 4], [1, 256]]),
                         t["gc_w2_16"][:, :].rearrange("(c p) e -> p c e", p=128), sX)
                self.dma(sy, bass.AP(gw3a, 0, [[2, 128], [1, 2], [1, 1]]),
                         t["gc_w3_16"][:, :].rearrange("(c p) e -> p c e", p=128), sX,
                         allow_slow_non_contiguous=True)
                self.dma(sy, arv[:, :], t["adaptive_range"][:].rearrange("(a d) -> a d", a=1), sX)
                self.dma(sy, g3b[:, :], t["gc_b3"][:].rearrange("(a d) -> a d", a=1), sX,
                         mark="D.setup")
                self.wait(sy, sX, "D.setup")
                for i, (nm, ln) in enumerate((("cn_b1", D), ("cn_b2", 512), ("cn_b3", 256),
                                              ("cn_b4", 1), ("gc_b1", 512), ("gc_b2", 256))):
                    if i > 0:
                        self.wait(sy, sPE, f"D.btr.{i - 1}")
                        self.wait(sy, sX, f"D.brow.{i - 1}")
                    self.dma(sy, brow[0:1, 0:ln],
                             t[nm][:].rearrange("(a d) -> a d", a=1), sX,
                             mark=f"D.brow.{i}")
                self.wait(sy, sX, "D.brow.5")
                for it in self.D_items:
                    k = it["kind"]
                    if k == "xh":
                        b, cyc = it["b"], it["cyc"]
                        if cyc == 0:
                            if b > 0:
                                self.wait(sy, sPE, f"D.xhfree.{b - 1}.2")
                            src = t["z3_16"]
                            r0 = b * S
                        else:
                            self.wait(sy, sZ, f"D.ow.{b}.{cyc - 1}")
                            self.wait(sy, sZ2, f"D.ow2.{b}.{cyc - 1}")
                            self.wait(sy, sPE, f"D.xhfree.{b}.{cyc - 1}")
                            src = dtmp
                            r0 = 0
                        for c in range(NCHUNK):
                            self.dma(sy, xh[:, c * S:(c + 1) * S],
                                     src[r0:r0 + S, c * 128:(c + 1) * 128], sX,
                                     transpose=True,
                                     mark=f"D.xh.{b}.{cyc}" if c == NCHUNK - 1 else None)
                    elif k == "oload":
                        b = it["b"]
                        if b > 0:
                            self.wait(sy, sDVE, f"D.zused.{(b - 1) * 16 + 15}")
                        for tt in range(16):
                            self.dma(sy, out32[:, tt * D:(tt + 1) * D],
                                     t["z3_32"][b * S + tt * 128: b * S + (tt + 1) * 128, :],
                                     sV, mark=f"D.ol.{b}.{tt}")
                    elif k == "oout":
                        b, cyc, tt = it["b"], it["cyc"], it["tt"]
                        semz = sZ if tt % 2 == 0 else sZ2
                        self.wait(sy, sACT, f"D.ob16.{b}.{cyc}.{tt}")
                        mk = [f"D.od.{b}.{cyc}.{tt}"]
                        if tt == 14:
                            mk.append(f"D.ow.{b}.{cyc}")
                        if tt == 15:
                            mk.append(f"D.ow2.{b}.{cyc}")
                        self.dma(sy, dtmp[tt * 128:(tt + 1) * 128, :],
                                 ob16[:, (tt % 2) * D:(tt % 2 + 1) * D], semz,
                                 mark=mk)
                    elif k == "final":
                        b, tt, g2 = it["b"], it["tt"], it["g2"]
                        semx = sW if g2 % 2 == 0 else sW2
                        if g2 >= 2:
                            self.wait(sy, sDVE, f"D.zused.{g2 - 2}")
                        r0 = b * S + tt * 128
                        self.dma(sy, z3t[:, (g2 % 2) * D:(g2 % 2 + 1) * D],
                                 t["z3_32"][r0:r0 + 128, :], semx,
                                 mark=f"D.z3t.{g2}")
                        if g2 >= 2:
                            gw = g2 - 2
                            semz = sZ if gw % 2 == 0 else sZ2
                            self.wait(sy, sDVE, f"D.z.{gw}")
                            bb, ttw = gw // 16, gw % 16
                            self.dma(sy, t["out"][bb, ttw * 128:(ttw + 1) * 128, :],
                                     rb[:, (gw % 2) * D:(gw % 2 + 1) * D], semz,
                                     mark=f"D.zw.{gw}")
                for gw in (NB * 16 - 2, NB * 16 - 1):
                    semz = sZ if gw % 2 == 0 else sZ2
                    self.wait(sy, sDVE, f"D.z.{gw}")
                    bb, ttw = gw // 16, gw % 16
                    self.dma(sy, t["out"][bb, ttw * 128:(ttw + 1) * 128, :],
                             rb[:, (gw % 2) * D:(gw % 2 + 1) * D], semz,
                             mark=f"D.zw.{gw}")
                self.wait(sy, sZ, sZ.n)
                self.wait(sy, sZ2, sZ2.n)

            def f_tensor(te):
                for it in self.D_items:
                    k = it["kind"]
                    if k == "biasprep":
                        for i, ln in enumerate((D, 512, 256, 1, 512, 256)):
                            self.wait(te, sX, f"D.brow.{i}")
                            if ln == 1:
                                self.wait(te, sACT, "D.b4s")
                                self.op(te, "nop", inc=(sPE, 1), mark=f"D.btr.{i}")
                                continue
                            nch = ln // 128
                            for cc in range(nch):
                                if (i, cc) != (0, 0):
                                    prev_i = i - 1 if i - 1 != 3 else 2
                                    self.wait(te, sACT, f"D.btec.{i}.{cc - 1}"
                                              if cc > 0 else f"D.btelast.{prev_i}")
                                self.op(te, "transpose", psB[0:128, 0:1],
                                        brow[0:1, cc * 128: cc * 128 + 128],
                                        id32[0:1, 0:1],
                                        inc=(sPE, 1),
                                        mark=f"D.btpe.{i}.{cc}")
                            self.wait(te, sACT, f"D.btelast.{i}")
                            self.op(te, "nop", inc=(sPE, 1), mark=f"D.btr.{i}")
                        self.op(te, "matmul", psM[:, 0:1], o128_32[0:1, :],
                                sc1[0:1, 5:6], start=True, stop=True,
                                inc=(sPE, 1), mark="D.b4bc")
                    elif k == "gctx":
                        pass
                    elif k == "gnet":
                        b = it["b"]
                        self.wait(te, sDVE, f"D.gc16.{b}")
                        self.wait(te, sACT, "D.b4ce" if b == 0 else f"D.acol.{b - 1}")
                        for oc in range(4):
                            for dc in range(NCHUNK):
                                self.op(te, "matmul", psM[:, 0:1],
                                        gw1a[:, dc * 512 + oc * 128: dc * 512 + oc * 128 + 128],
                                        gc16[:, dc:dc + 1],
                                        start=(dc == 0), stop=(dc == NCHUNK - 1),
                                        inc=(sPE, 1) if dc == NCHUNK - 1 else None,
                                        mark=f"D.g1.{b}.{oc}" if dc == NCHUNK - 1 else None)
                            self.wait(te, sACT, f"D.g1e.{b}.{oc}")
                        for oc in range(2):
                            for dc in range(4):
                                self.op(te, "matmul", psM[:, 1:2],
                                        gw2a[:, dc * 256 + oc * 128: dc * 256 + oc * 128 + 128],
                                        g1_16[:, dc:dc + 1],
                                        start=(dc == 0), stop=(dc == 3),
                                        inc=(sPE, 1) if dc == 3 else None,
                                        mark=f"D.g2.{b}.{oc}" if dc == 3 else None)
                            self.wait(te, sACT, f"D.g2e.{b}.{oc}")
                        for dc in range(2):
                            self.op(te, "matmul", psM[0:1, 0:1],
                                    gw3a[:, dc:dc + 1], g2_16[:, dc:dc + 1],
                                    start=(dc == 0), stop=(dc == 1),
                                    inc=(sPE, 1) if dc == 1 else None,
                                    mark=f"D.g3.{b}" if dc == 1 else None)
                        self.wait(te, sDVE, f"D.AB.{b}")
                        self.op(te, "matmul", psM[:, 0:1], o128_32[0:1, :],
                                sc1[0:1, 3:4], start=True, stop=True, inc=(sPE, 1),
                                mark=f"D.Abc.{b}")
                        self.wait(te, sACT, f"D.Ae.{b}")
                        self.op(te, "matmul", psM[:, 0:1], o128_32[0:1, :],
                                sc1[0:1, 4:5], start=True, stop=True, inc=(sPE, 1),
                                mark=f"D.Bbc.{b}")
                    elif k in ("h1", "h2", "h3"):
                        b, cyc, tg, oc, g = it["b"], it["cyc"], it["tg"], it["oc"], it["gd"]
                        if k == "h1" and oc == 0 and tg == 0:
                            self.wait(te, sX, f"D.xh.{b}.{cyc}")
                            if cyc == 0:
                                self.wait(te, sACT, f"D.acol.{b}")
                        gd_wait(te, g)
                        if k == "h1":
                            ndc, warr, wstr, rhs, rstr = NCHUNK, w1a, D, xh, S
                            rview = lambda dc: xh[:, dc * S + tg * 512:
                                                  dc * S + (tg + 1) * 512]
                        elif k == "h2":
                            ndc, warr, wstr = NCHUNK, w2a, 512
                            rview = lambda dc: h1a[:, dc * 512:(dc + 1) * 512]
                        else:
                            ndc, warr, wstr = 4, w3a, 256
                            rview = lambda dc: h2a[:, dc * 512:(dc + 1) * 512]
                        if k == "h2" and oc == 0:
                            self.wait(te, sACT, f"D.h1all.{b}.{cyc}.{tg}")
                        if k == "h3" and oc == 0:
                            self.wait(te, sACT, f"D.h2all.{b}.{cyc}.{tg}")
                        for dc in range(ndc):
                            self.op(te, "matmul", ph(g),
                                    warr[:, dc * wstr + oc * 128: dc * wstr + oc * 128 + 128],
                                    rview(dc),
                                    start=(dc == 0), stop=(dc == ndc - 1),
                                    inc=(sPE, 1) if dc == ndc - 1 else None,
                                    mark=f"D.gd.{g}" if dc == ndc - 1 else None)
                    elif k == "tf":
                        b, cyc, tg, j, tt = it["b"], it["cyc"], it["tg"], it["j"], it["tt"]
                        if j == 0:
                            self.wait(te, sACT, f"D.h3all.{b}.{cyc}.{tg}")
                        prev = None
                        if tt > 0:
                            prev = (b, cyc, tt - 1)
                        elif cyc > 0:
                            prev = (b, cyc - 1, 15)
                        elif b > 0:
                            prev = (b - 1, NCYC - 1, 15)
                        else:
                            self.wait(te, sACT, "D.b4ce")
                        if prev is not None:
                            self.wait(te, sACT, "D.scol.%d.%d.%d" % prev)
                        if cyc == 0 and tg == 0 and j == 0:
                            self.wait(te, sACT, f"D.acol.{b}")
                        for fc in range(2):
                            mk = [f"D.tf.{b}.{cyc}.{tt}"]
                            if tt == 15:
                                mk.append(f"D.xhfree.{b}.{cyc}")
                            self.op(te, "matmul", psM[:, 1:2],
                                    h3a[:, fc * 512 + j * 128: fc * 512 + j * 128 + 128],
                                    w4a[:, fc:fc + 1],
                                    start=(fc == 0), stop=(fc == 1),
                                    inc=(sPE, 1) if fc == 1 else None,
                                    mark=mk if fc == 1 else None)

            def f_scalar(se):
                for it in self.D_items:
                    k = it["kind"]
                    if k == "biasprep":
                        for i, (tgt, ln) in enumerate(((b1c, D), (b2c, 512), (b3c, 256),
                                                       (None, 1), (g1bc, 512), (g2bc, 256))):
                            if ln == 1:
                                self.wait(se, sX, f"D.brow.{i}")
                                self.op(se, "copy", sc1[0:1, 5:6], brow[0:1, 0:1],
                                        inc=(sACT, 1), mark="D.b4s")
                                continue
                            nch = ln // 128
                            for cc in range(nch):
                                self.wait(se, sPE, f"D.btpe.{i}.{cc}")
                                self.op(se, "copy", tgt[0:128, cc:cc + 1],
                                        psB[0:128, 0:1], inc=(sACT, 1),
                                        mark=[f"D.btec.{i}.{cc}"] +
                                             ([f"D.btelast.{i}"] if cc == nch - 1 else []))
                        self.wait(se, sPE, "D.b4bc")
                        self.op(se, "copy", b4c[:, :], psM[:, 0:1],
                                inc=(sACT, 1), mark="D.b4ce")
                    elif k == "gnet":
                        b = it["b"]
                        for oc in range(4):
                            self.wait(se, sPE, f"D.g1.{b}.{oc}")
                            self.op(se, "activation", g1_16[:, oc:oc + 1],
                                    psM[:, 0:1], AF.Gelu, bias=g1bc[:, oc:oc + 1],
                                    inc=(sACT, 1), mark=f"D.g1e.{b}.{oc}")
                        for oc in range(2):
                            self.wait(se, sPE, f"D.g2.{b}.{oc}")
                            self.op(se, "activation", g2_16[:, oc:oc + 1],
                                    psM[:, 1:2], AF.Gelu, bias=g2bc[:, oc:oc + 1],
                                    inc=(sACT, 1), mark=f"D.g2e.{b}.{oc}")
                        self.wait(se, sPE, f"D.g3.{b}")
                        self.op(se, "activation", sc1[0:1, 0:1], psM[0:1, 0:1],
                                AF.Sigmoid, bias=g3b[0:1, 0:1],
                                inc=(sACT, 1), mark=f"D.gf.{b}")
                        self.wait(se, sPE, f"D.Abc.{b}")
                        self.op(se, "copy", acol[:, 0:1], psM[:, 0:1],
                                inc=(sACT, 1), mark=f"D.Ae.{b}")
                        self.wait(se, sPE, f"D.Bbc.{b}")
                        self.op(se, "copy", acol[:, 1:2], psM[:, 0:1],
                                inc=(sACT, 1), mark=f"D.acol.{b}")
                    elif k in ("h1", "h2", "h3"):
                        b, cyc, tg, oc, g = it["b"], it["cyc"], it["tg"], it["oc"], it["gd"]
                        self.wait(se, sPE, f"D.gd.{g}")
                        tgt, bc, nall = {"h1": (h1a, b1c, NCHUNK),
                                         "h2": (h2a, b2c, 4),
                                         "h3": (h3a, b3c, 2)}[k]
                        self.op(se, "activation", tgt[:, oc * 512:(oc + 1) * 512],
                                ph(g), AF.Gelu, bias=bc[:, oc:oc + 1],
                                inc=(sACT, 1),
                                mark=[f"D.gdone.{g}"] +
                                     ([f"D.{k}all.{b}.{cyc}.{tg}"] if oc == nall - 1 else []))
                    elif k == "tf":
                        b, cyc, tt = it["b"], it["cyc"], it["tt"]
                        self.wait(se, sPE, f"D.tf.{b}.{cyc}.{tt}")
                        prev = ((b, cyc, tt - 1) if tt > 0 else
                                ((b, cyc - 1, 15) if cyc > 0 else
                                 ((b - 1, NCYC - 1, 15) if b > 0 else None)))
                        if prev is not None:
                            self.wait(se, sDVE, "D.cpl.%d.%d.%d" % prev)
                        self.op(se, "activation", scol[:, :], psM[:, 1:2],
                                AF.Sigmoid, bias=b4c[:, :],
                                inc=(sACT, 1), mark=f"D.scol.{b}.{cyc}.{tt}")
                    elif k == "oout":
                        b, cyc, tt = it["b"], it["cyc"], it["tt"]
                        self.wait(se, sDVE, f"D.tw.{b}.{cyc}.{tt}")
                        if tt >= 2:
                            semz = sZ if tt % 2 == 0 else sZ2
                            self.wait(se, semz, f"D.od.{b}.{cyc}.{tt - 2}")
                        self.op(se, "activation", ob16[:, (tt % 2) * D:(tt % 2 + 1) * D],
                                out32[:, tt * D:(tt + 1) * D], AF.Copy,
                                inc=(sACT, 1), mark=f"D.ob16.{b}.{cyc}.{tt}")
                    elif k == "final":
                        g2 = it["g2"]
                        par = g2 % 2
                        self.wait(se, sDVE, f"D.var.{g2}")
                        self.op(se, "activation", stat[:, par * 4 + 3:par * 4 + 4],
                                stat[:, par * 4 + 2:par * 4 + 3], AF.Sqrt,
                                scale=1.0 / D, bias=epsc[:, :],
                                inc=(sACT, 1), mark=f"D.sd.{g2}")

            def f_vector(ve):
                self.op(ve, "memset", epsc[:, :], EPS, inc=(sDVE, 1))
                for it in self.D_items:
                    k = it["kind"]
                    if k == "gctx":
                        b = it["b"]
                        self.wait(ve, sX, f"D.xh.{b}.0")
                        for c in range(NCHUNK):
                            self.op(ve, "reduce_sum", gcT[:, c:c + 1],
                                    xh[:, c * S:(c + 1) * S], axis=AX.X,
                                    inc=(sDVE, 1))
                        self.op(ve, "tensor_scalar_mul", gc16[:, :], gcT[:, :],
                                1.0 / S, inc=(sDVE, 1), mark=f"D.gc16.{b}")
                    elif k == "gnet":
                        b = it["b"]
                        self.wait(ve, sACT, f"D.gf.{b}")
                        self.op(ve, "tensor_scalar", sc1[0:1, 1:2], sc1[0:1, 0:1],
                                0.7, -0.5, ALU.mult, ALU.add, inc=(sDVE, 1))
                        self.op(ve, "tensor_mul", sc1[0:1, 2:3], sc1[0:1, 1:2],
                                arv[0:1, 0:1], inc=(sDVE, 1))
                        self.op(ve, "tensor_scalar", sc1[0:1, 3:4], sc1[0:1, 2:3],
                                2.0, 0.1, ALU.mult, ALU.add, inc=(sDVE, 1))
                        self.op(ve, "tensor_scalar_mul", sc1[0:1, 4:5], arv[0:1, 0:1],
                                0.6, inc=(sDVE, 1), mark=f"D.AB.{b}")
                    elif k == "tf":
                        b, cyc, tt = it["b"], it["cyc"], it["tt"]
                        self.wait(ve, sACT, f"D.scol.{b}.{cyc}.{tt}")
                        self.op(ve, "tensor_scalar", cpl[:, 0:1], scol[:, :],
                                acol[:, 1:2], acol[:, 0:1], ALU.mult, ALU.add,
                                inc=(sDVE, 1))
                        self.op(ve, "tensor_scalar_mul", cpl[:, 1:2], cpl[:, 0:1],
                                -1.0, inc=(sDVE, 1), mark=f"D.cpl.{b}.{cyc}.{tt}")
                    elif k == "twist":
                        b, cyc, tt = it["b"], it["cyc"], it["tt"]
                        if cyc == 0 and tt == 0:
                            self.wait(ve, sV, f"D.ol.{b}.15")
                        Q = [out32[:, tt * D + q * 256: tt * D + (q + 1) * 256]
                             for q in range(4)]
                        Sc = [scr[:, q * 256:(q + 1) * 256] for q in range(4)]
                        c0, c1 = cpl[:, 0:1], cpl[:, 1:2]
                        self.op(ve, "scalar_tensor_tensor", Sc[0], Q[2], c0, Q[0],
                                ALU.mult, ALU.add, inc=(sDVE, 1))
                        self.op(ve, "scalar_tensor_tensor", Sc[1], Q[3], c1, Q[1],
                                ALU.mult, ALU.add, inc=(sDVE, 1))
                        self.op(ve, "scalar_tensor_tensor", Sc[2], Q[0], c1, Q[2],
                                ALU.mult, ALU.add, inc=(sDVE, 1))
                        self.op(ve, "scalar_tensor_tensor", Sc[3], Q[1], c0, Q[3],
                                ALU.mult, ALU.add, inc=(sDVE, 1))
                        self.op(ve, "tensor_copy", out32[:, tt * D:(tt + 1) * D],
                                scr[:, :], inc=(sDVE, 1),
                                mark=f"D.tw.{b}.{cyc}.{tt}")
                    elif k == "final":
                        b, tt, g2 = it["b"], it["tt"], it["g2"]
                        par = g2 % 2
                        semx = sW if g2 % 2 == 0 else sW2
                        self.wait(ve, semx, f"D.z3t.{g2}")
                        if g2 >= 2:
                            semz = sZ if g2 % 2 == 0 else sZ2
                            self.wait(ve, semz, f"D.zw.{g2 - 2}")
                        self.op(ve, "tensor_add", rb[:, par * D:(par + 1) * D],
                                out32[:, tt * D:(tt + 1) * D],
                                z3t[:, par * D:(par + 1) * D],
                                inc=(sDVE, 1), mark=[f"D.zused.{g2}"])
                        self.op(ve, "reduce_sum", stat[:, par * 4:par * 4 + 1],
                                rb[:, par * D:(par + 1) * D], axis=AX.X,
                                negate=True, inc=(sDVE, 1))
                        self.op(ve, "tensor_scalar_mul", stat[:, par * 4 + 1:par * 4 + 2],
                                stat[:, par * 4:par * 4 + 1], 1.0 / D, inc=(sDVE, 1))
                        self.op(ve, "tensor_scalar_add", cb[:, par * D:(par + 1) * D],
                                rb[:, par * D:(par + 1) * D],
                                stat[:, par * 4 + 1:par * 4 + 2], inc=(sDVE, 1))
                        self.op(ve, "tensor_mul", rb[:, par * D:(par + 1) * D],
                                cb[:, par * D:(par + 1) * D],
                                cb[:, par * D:(par + 1) * D], inc=(sDVE, 1))
                        self.op(ve, "reduce_sum", stat[:, par * 4 + 2:par * 4 + 3],
                                rb[:, par * D:(par + 1) * D], axis=AX.X,
                                inc=(sDVE, 1), mark=f"D.var.{g2}")
                        self.wait(ve, sACT, f"D.sd.{g2}")
                        self.op(ve, "reciprocal", stat[:, par * 4 + 3:par * 4 + 4],
                                stat[:, par * 4 + 3:par * 4 + 4], inc=(sDVE, 1))
                        self.op(ve, "tensor_scalar_mul", rb[:, par * D:(par + 1) * D],
                                cb[:, par * D:(par + 1) * D],
                                stat[:, par * 4 + 3:par * 4 + 4],
                                inc=(sDVE, 1), mark=f"D.z.{g2}")

            streams = {"sync": f_sync, "tensor": f_tensor,
                       "vector": f_vector, "scalar": f_scalar}
            self.run_phase(streams)

    def build(self):
        self.emit_prologue()
        if "A" in self.phases:
            self.emit_phaseA()
            self.barrier()
        if "B" in self.phases:
            self.emit_phaseB()
            self.barrier()
        if "C" in self.phases:
            self.emit_phaseC()
            self.barrier()
        if "D" in self.phases:
            self.emit_phaseD()
            self.barrier()
        return self.nc


def _build(phases):
    import time
    t0 = time.time()
    mk = MK(phases)
    nc = mk.build()
    print(f"[kernel] built phases={phases} n_instructions={nc.n_instructions()} "
          f"in {time.time() - t0:.1f}s", flush=True)
    return mk, nc


def _run(mk, nc, inputs, shard_c):
    in_maps = []
    for c in range(8):
        m = {}
        for n in mk.ext_in:
            if n == "L0_attn_x":
                m[n] = np.ascontiguousarray(inputs[n][2 * c:2 * c + 2])
            elif n.startswith("z") and n[2:] in ("_32", "_16"):
                m[n] = inputs[n][c]
            else:
                m[n] = inputs[n]
        in_maps.append(m)
    return run_bass_kernel_spmd(nc, in_maps, core_ids=list(range(8)))


def kernel(**inputs):
    phases = PHASES
    mk, nc = _build(phases)
    res = _run(mk, nc, inputs, None)
    outs = res.results
    if "D" in phases:
        return np.concatenate([o["out"] for o in outs], axis=0)
    last = [n for n in mk.ext_out if n.endswith("_32")][-1]
    return np.stack([o[last] for o in outs], axis=0)


# version 14 (final)
# speedup vs baseline: 1.0142x; 1.0142x over previous
import os as _os


import numpy as np
import ml_dtypes
from contextlib import ExitStack

import concourse.bass as bass
import concourse.mybir as mybir
from concourse.bass_utils import run_bass_kernel_spmd

F32 = mybir.dt.float32
BF16 = mybir.dt.bfloat16
AF = mybir.ActivationFunctionType
ALU = mybir.AluOpType
AX = mybir.AxisListType

D = 1024
S = 2048
NB = 2
NTOK = NB * S
NCHUNK = D // 128
HD = 64
NH = 16
NKV = 4
DFF = 4096
NEXP = 8
LAM = 10.0
EPS = 1e-5

PHASES = ("A", "B", "C", "D")
UNFUSED = False


class Sem:
    def __init__(self, h, name):
        self.h = h
        self.name = name
        self.n = 0
        self.pts = {}


class MK:
    def __init__(self, phases=PHASES):
        self.phases = phases
        self.cur_eng = None
        self.nc = bass.Bass("TRN2", target_bir_lowering=False)
        self.dry = True
        self.sems = {}
        self.stack = ExitStack()
        self._mk_io()
        self._mk_consts()
        for n in ("sD2DA", "sD2DB", "sD2DC", "sD2DD",
                  "sX", "sW", "sW2", "sV", "sV2", "sU", "sU2", "sT", "sT2", "sZ", "sZ2",
                  "sGI", "sGA",
                  "sPE", "sACT", "sDVE", "sPOOL", "sPH"):
            self.S(n)
        self.pidx = 0

    def S(self, name):
        if name not in self.sems:
            h = self.stack.enter_context(self.nc.semaphore(name))
            self.sems[name] = Sem(h, name)
        return self.sems[name]

    SELF_EDGE = {"vector": "sDVE", "scalar": "sACT", "gpsimd": "sPOOL"}

    def op(self, eng, name, *args, waits=(), inc=None, mark=None, **kw):
        own = None
        if "dma" not in name and self.cur_eng in self.SELF_EDGE:
            own = self.sems[self.SELF_EDGE[self.cur_eng]]
            if eng is not None and own.n > 0:
                eng.wait_ge(own.h, own.n)
        if eng is not None:
            for sem, key in waits:
                tgt = sem.pts[key] if isinstance(key, str) else key
                eng.wait_ge(sem.h, tgt)
        ins = None
        if eng is not None:
            ins = getattr(eng, name)(*args, **kw)
        if inc is None and own is not None:
            inc = (own, 1)
        if inc is not None:
            sem, k = inc
            assert own is None or sem is own, (name, sem.name)
            sem.n += k
            if ins is not None:
                ins.then_inc(sem.h, k)
            if mark is not None:
                for key in ([mark] if isinstance(mark, str) else mark):
                    if self.dry:
                        sem.pts[key] = sem.n
                    else:
                        assert sem.pts.get(key) == sem.n, (key, sem.pts.get(key), sem.n)
        else:
            assert mark is None
        return ins

    def dma(self, eng, out, in_, sem, mark=None, waits=(), **kw):
        return self.op(eng, "dma_start", out, in_, waits=waits,
                       inc=(sem, 16), mark=mark, **kw)

    def wait(self, eng, sem, key):
        if eng is not None:
            tgt = sem.pts[key] if isinstance(key, str) else key
            eng.wait_ge(sem.h, tgt)

    def barrier(self):
        sem = self.sems["sPH"]
        self.pidx += 1
        tgt = 5 * self.pidx
        with self.nc.Block() as blk:
            for attr in ("tensor", "vector", "scalar", "gpsimd", "sync"):
                def mk(eng):
                    eng.nop().then_inc(sem.h, 1)
                    eng.wait_ge(sem.h, tgt)
                getattr(blk, attr)(mk)
        sem.n += 5

    def run_phase(self, streams):
        snap = {k: v.n for k, v in self.sems.items()}
        self.dry = True
        for name in ("sync", "gpsimd", "tensor", "vector", "scalar"):
            if name in streams:
                self.cur_eng = name
                streams[name](None)
        for k, v in snap.items():
            self.sems[k].n = v
        self.dry = False
        with self.nc.Block() as blk:
            for name in ("sync", "gpsimd", "tensor", "vector", "scalar"):
                if name in streams:
                    def wrapped(eng, fn=streams[name], nm=name):
                        self.cur_eng = nm
                        fn(eng)
                    getattr(blk, name)(wrapped)

    def sb(self, st, name, shape, dt):
        return st.enter_context(self.nc.sbuf_tensor(name, shape, dt))

    def ps(self, st, name, shape, dt=F32):
        return st.enter_context(self.nc.psum_tensor(name, shape, dt))

    def _mk_io(self):
        nc = self.nc
        ph = self.phases
        self.ext_in = []
        self.ext_out = []

        def ein(name, shape, dt=F32):
            self.ext_in.append(name)
            return nc.dram_tensor(name, shape, dt, kind="ExternalInput")

        def eout(name, shape, dt=F32):
            self.ext_out.append(name)
            return nc.dram_tensor(name, shape, dt, kind="ExternalOutput")

        def stream(name, producer, consumers):
            prod = producer in ph
            cons = any(p in ph for p in consumers)
            if prod and cons:
                return (nc.dram_tensor(name + "_32", [NTOK, D], F32, kind="Internal"),
                        nc.dram_tensor(name + "_16", [NTOK, D], BF16, kind="Internal"))
            if prod:
                return eout(name + "_32", [NTOK, D]), eout(name + "_16", [NTOK, D], BF16)
            if cons:
                return ein(name + "_32", [NTOK, D]), ein(name + "_16", [NTOK, D], BF16)
            return None, None

        t = {}
        if "A" in ph:
            t["x"] = ein("L0_attn_x", [NB, S, D])
            t["Wqkv"] = ein("L0_attn_Wqkv", [NH, D, 3 * HD])
            t["bqkv"] = ein("L0_attn_bqkv", [NH, 3 * HD])
            t["Wout"] = ein("L0_attn_Wout", [D, D])
            t["bout"] = ein("L0_attn_bout", [D])
            t["Wqkv16"] = nc.dram_tensor("Wqkv16", [NH, D, 3 * HD], BF16, kind="Internal")
            t["Wout16d"] = nc.dram_tensor("Wout16d", [D, D], BF16, kind="Internal")
            t["S16_0"] = nc.dram_tensor("S16_0", [NB, S, D], BF16, kind="Internal")
        t["z1_32"], t["z1_16"] = stream("z1", "A", ("B", "C", "D"))
        if "B" in ph:
            t["rw"] = ein("L0_ffn_router_w", [NEXP, D])
            t["wg"] = ein("L0_ffn_w_gate", [NEXP, D, DFF])
            t["wu"] = ein("L0_ffn_w_up", [NEXP, D, DFF])
            t["wd"] = ein("L0_ffn_w_down", [NEXP, DFF, D])
            t["wg16"] = nc.dram_tensor("wg16d", [NEXP, D, DFF], BF16, kind="Internal")
            t["wu16"] = nc.dram_tensor("wu16d", [NEXP, D, DFF], BF16, kind="Internal")
            t["wd16"] = nc.dram_tensor("wd16d", [NEXP, DFF, D], BF16, kind="Internal")
        t["z2_32"], t["z2_16"] = stream("z2", "B", ("C", "D"))
        if "C" in ph:
            t["wq1"] = ein("L1_attn_w_q", [D, D])
            t["wk1"] = ein("L1_attn_w_k", [D, NKV * HD])
            t["wv1"] = ein("L1_attn_w_v", [D, NKV * HD])
            t["wo1"] = ein("L1_attn_w_o", [D, D])
            for n in ("wq1", "wk1", "wv1", "wo1"):
                t[n + "_16"] = nc.dram_tensor(n + "_16d", list(t[n].shape), BF16, kind="Internal")
        t["z3_32"], t["z3_16"] = stream("z3", "C", ("D",))
        if "D" in ph:
            for n, shp in (("cn_w1", [D, D]), ("cn_w2", [D, 512]), ("cn_w3", [512, 256]),
                           ("cn_w4", [256, 1]), ("gc_w1", [D, 512]), ("gc_w2", [512, 256]),
                           ("gc_w3", [256, 1])):
                t[n] = ein("L1_ffn_" + n, shp)
                t[n + "_16"] = nc.dram_tensor(n + "_16d", shp, BF16, kind="Internal")
            for n, shp in (("cn_b1", [D]), ("cn_b2", [512]), ("cn_b3", [256]), ("cn_b4", [1]),
                           ("gc_b1", [512]), ("gc_b2", [256]), ("gc_b3", [1]),
                           ("adaptive_range", [1])):
                t[n] = ein("L1_ffn_" + n, shp)
            t["out"] = eout("out", [NB, S, D])
        self.t = t

    def _mk_consts(self):
        nc = self.nc
        self.c_ident32 = nc.inline_tensor(np.eye(128, dtype=np.float32), name="ident32")
        self.c_ident16 = nc.inline_tensor(np.eye(128, dtype=ml_dtypes.bfloat16), name="ident16")
        ones = np.ones((1, 128), dtype=np.float32)
        self.c_ones1x128_32 = nc.inline_tensor(ones, name="ones1x128_32")
        self.c_ones1x128_16 = nc.inline_tensor(ones.astype(ml_dtypes.bfloat16), name="ones1x128_16")
        self.c_ones64x1_16 = nc.inline_tensor(np.ones((64, 1), dtype=ml_dtypes.bfloat16), name="ones64x1_16")
        masks = np.concatenate(
            [(np.arange(512)[None, :] - np.arange(128)[:, None] >= 128 * m)
             for m in range(4)], axis=1).astype(ml_dtypes.bfloat16)
        self.c_cmask = nc.inline_tensor(masks, name="cmask")
        half = HD // 2
        inv = 1.0 / (10000.0 ** (np.arange(half, dtype=np.float64) * 2.0 / HD))
        ang = np.arange(S, dtype=np.float64)[None, :] * inv[:, None]
        self.c_ropecos = nc.inline_tensor(np.cos(ang).astype(ml_dtypes.bfloat16), name="ropecos")
        self.c_ropesin = nc.inline_tensor(np.sin(ang).astype(ml_dtypes.bfloat16), name="ropesin")
        sel = np.zeros((8, 8 * 128), dtype=ml_dtypes.bfloat16)
        for e in range(8):
            sel[e, e * 128:(e + 1) * 128] = 1.0
        self.c_sel8 = nc.inline_tensor(sel, name="sel8")
        self.c_allones = nc.inline_tensor(
            np.ones((128, 128), dtype=ml_dtypes.bfloat16), name="allones")
        self.c_strictut = nc.inline_tensor(
            np.triu(np.ones((128, 128), dtype=np.float32), 1)
            .astype(ml_dtypes.bfloat16), name="strictut")

    def emit_prologue(self):
        t = self.t

        def g_stream(g):
            if "A" in self.phases:
                sD = self.sems["sD2DA"]
                self.dma(g, t["S16_0"][:, :, :], t["x"][:, :, :], sD)
                self.dma(g, t["Wqkv16"][:, :, :], t["Wqkv"][:, :, :], sD)
                self.dma(g, t["Wout16d"][:, :], t["Wout"][:, :], sD, mark="d2dA")
            if "C" in self.phases:
                sD = self.sems["sD2DC"]
                for n in ("wq1", "wk1", "wv1", "wo1"):
                    self.dma(g, t[n + "_16"][:, :], t[n][:, :], sD,
                             mark="d2dC" if n == "wo1" else None)
            if "D" in self.phases:
                sD = self.sems["sD2DD"]
                for n in ("cn_w1", "cn_w2", "cn_w3", "cn_w4",
                          "gc_w1", "gc_w2", "gc_w3"):
                    self.dma(g, t[n + "_16"][:, :], t[n][:, :], sD,
                             mark="d2dD" if n == "gc_w3" else None)
            if "B" in self.phases:
                sD = self.sems["sD2DB"]
                for e in range(NEXP):
                    self.dma(g, t["wg16"][e, :, :], t["wg"][e, :, :], sD)
                    self.dma(g, t["wu16"][e, :, :], t["wu"][e, :, :], sD)
                    self.dma(g, t["wd16"][e, :, :], t["wd"][e, :, :], sD,
                             mark="d2dB" if e == NEXP - 1 else None)

        self.run_phase({"gpsimd": g_stream})

    def emit_phaseA(self):
        t = self.t
        sX = self.sems["sX"]
        sW, sW2 = self.sems["sW"], self.sems["sW2"]
        sV, sV2 = self.sems["sV"], self.sems["sV2"]
        sZ, sZ2 = self.sems["sZ"], self.sems["sZ2"]
        sPE, sACT, sDVE = self.sems["sPE"], self.sems["sACT"], self.sems["sDVE"]
        sD = self.sems["sD2DA"]

        items = []
        ga = 0

        def G(kind, **kw):
            items.append(dict(kind=kind, **kw))

        G("biasT", ga=ga, esem="sACT"); ga += 1
        G("coutrow", ga=ga, nh2=0, esem="sACT"); ga += 1
        G("coutrow", ga=ga, nh2=1, esem="sACT"); ga += 1
        G("coutb")
        for b in range(NB):
            gs = 0
            gy = 0
            gt = 0
            for h in range(NH):
                for j in range(3):
                    for qg in range(4):
                        G("proj", b=b, h=h, j=j, qg=qg, ga=ga,
                          esem="sACT" if j == 2 else "sDVE")
                        ga += 1
                G("sq0", b=b, h=h)
                for qg in range(4):
                    G("qnorm", b=b, h=h, qg=qg, ga=ga, esem="sDVE"); ga += 1
                G("sq1", b=b, h=h)
                for kt in range(16):
                    G("vtrans", b=b, h=h, kt=kt, gt=gt); gt += 1
                    if kt < 8:
                        if kt % 2 == 1:
                            G("knorm", b=b, h=h, qg=kt // 2, ga=ga, esem="sACT"); ga += 1
                G("maxb", b=b, h=h)
                G("aug", b=b, h=h)
                G("kones", b=b, h=h)
                for qg in range(4):
                    nk = 4 * qg + 4
                    for i, kt in enumerate(range(nk)):
                        G("score", b=b, h=h, qg=qg, kt=kt, gs=gs, i=i,
                          straddle=(kt >= 4 * qg), gy=gy, nk=nk)
                        gs += 1
                    G("recip", b=b, h=h, qg=qg, gy=gy)
                    G("bcast", b=b, h=h, qg=qg, ga=ga, gy=gy, esem="sACT"); ga += 1
                    G("ymul", b=b, h=h, qg=qg, gy=gy,
                      last=(h == NH - 1 and qg == 3))
                    gy += 1
            for tt in range(16):
                G("outproj", b=b, tt=tt, g=b * 16 + tt)
                G("epilog", b=b, tt=tt, g=b * 16 + tt)
        self.A_items = items
        ga_item = {}
        for it in items:
            if "ga" in it:
                ga_item[it["ga"]] = it
        self.A_ga_item = ga_item

        with ExitStack() as st:
            xh = self.sb(st, "A_xh", [128, NCHUNK * S], BF16)
            wqkv = self.sb(st, "A_wqkv", [128, 2 * NCHUNK * 192], BF16)
            wo = self.sb(st, "A_wo", [128, NCHUNK * D], BF16)
            qh = self.sb(st, "A_qh", [96, S], BF16)
            kh = self.sb(st, "A_kh", [96, S], BF16)
            vt = self.sb(st, "A_vt", [64, S], F32)
            va = self.sb(st, "A_va", [128, 16 * 96], BF16)
            q2 = self.sb(st, "A_q2", [64, S], BF16)
            n_row = self.sb(st, "A_nrow", [1, S], F32)
            b_row = self.sb(st, "A_brow", [1, S], F32)
            m2c = self.sb(st, "A_m2c", [1, 2], F32)
            pbuf = self.sb(st, "A_pbuf", [128, 2 * 1024], BF16)
            rzrow = self.sb(st, "A_rzrow", [1, 512], F32)
            rzb = self.sb(st, "A_rzb", [64, 512], F32)
            yt = self.sb(st, "A_yt", [128, NCHUNK * S], BF16)
            bq = self.sb(st, "A_bq", [16, 192], F32)
            bqT = self.sb(st, "A_bqT", [64, 3 * 16], F32)
            bqTs = self.sb(st, "A_bqTs", [64, 16], F32)
            brow = self.sb(st, "A_boutrow", [1, D], F32)
            brow16 = self.sb(st, "A_boutrow16", [1, D], BF16)
            bvrep = self.sb(st, "A_bvrep", [128, 8], F32)
            bvrep16 = self.sb(st, "A_bvrep16", [128, 8], BF16)
            coutrow = self.sb(st, "A_coutrow", [1, D], F32)
            coutb = self.sb(st, "A_coutb", [128, D], F32)
            id16 = self.sb(st, "A_id16", [128, 128], BF16)
            id32 = self.sb(st, "A_id32", [128, 128], F32)
            o128_32 = self.sb(st, "A_o128_32", [1, 128], F32)
            o128_16 = self.sb(st, "A_o128_16", [1, 128], BF16)
            o64_16 = self.sb(st, "A_o64_16", [64, 1], BF16)
            cmask = self.sb(st, "A_cmask", [128, 4 * 512], BF16)
            x0b = self.sb(st, "A_x0b", [128, 2 * D], F32)
            rb = self.sb(st, "A_rb", [128, 2 * D], F32)
            cb = self.sb(st, "A_cb", [128, 2 * D], F32)
            sqb = rb
            zb = rb
            zb16 = self.sb(st, "A_zb16", [128, 2 * D], BF16)
            stat = self.sb(st, "A_stat", [128, 8], F32)
            epsc = self.sb(st, "A_epsc", [128, 1], F32)

            psA = self.ps(st, "A_psA", [64, 2 * 512])
            psS = self.ps(st, "A_psS", [128, 4 * 512])
            psY = self.ps(st, "A_psY", [96, 512])
            psO = self.ps(st, "A_psO", [128, 512])

            def pa(sl):
                return psA[:, (sl % 2) * 512:(sl % 2 + 1) * 512]

            def pss(gs):
                sp, half = (gs // 2) % 2, gs % 2
                return psS[:, sp * 1024 + half * 512: sp * 1024 + (half + 1) * 512]

            def pssp(gp):
                sp = gp % 2
                return psS[:, sp * 1024:(sp + 1) * 1024]

            def pb(gs):
                sp, half = (gs // 2) % 2, gs % 2
                return pbuf[:, sp * 1024 + half * 512: sp * 1024 + (half + 1) * 512]

            def pbp(gp):
                sp = gp % 2
                return pbuf[:, sp * 1024:(sp + 1) * 1024]

            def f_sync(sy):
                self.wait(sy, sD, "d2dA")
                self.dma(sy, bass.AP(wo, 0, [[NCHUNK * D, 128], [D, NCHUNK], [1, D]]),
                         t["Wout16d"][:, :].rearrange("(c p) e -> p c e", p=128), sX)
                self.dma(sy, bq[:, :], t["bqkv"][:, :], sX)
                self.dma(sy, brow[:, :], t["bout"][:].rearrange("(a d) -> a d", a=1), sX)
                self.dma(sy, id16[:, :], self.c_ident16[:, :], sX)
                self.dma(sy, id32[:, :], self.c_ident32[:, :], sX)
                self.dma(sy, o128_32[:, :], self.c_ones1x128_32[:, :], sX)
                self.dma(sy, o128_16[:, :], self.c_ones1x128_16[:, :], sX)
                self.dma(sy, o64_16[:, :], self.c_ones64x1_16[:, :], sX)
                self.dma(sy, cmask[:, :], self.c_cmask[:, :], sX, mark="A.setup")
                self.wait(sy, sX, "A.setup")
                for b in range(NB):
                    if b > 0:
                        self.wait(sy, sPE, f"A.xhfree.{b - 1}")
                    for c in range(NCHUNK):
                        self.dma(sy, xh[:, c * S:(c + 1) * S],
                                 t["S16_0"][b, :, c * 128:(c + 1) * 128], sX,
                                 transpose=True,
                                 mark=f"A.xh.{b}" if c == NCHUNK - 1 else None)
                    for h in range(NH):
                        semw = sV if h % 2 == 0 else sV2
                        if b * NH + h >= 2:
                            hp, bp = (h - 2) % NH, b if h >= 2 else b - 1
                            self.wait(sy, sPE, f"A.wfree.{bp}.{hp}")
                        self.dma(sy, bass.AP(wqkv, (h % 2) * NCHUNK * 192,
                                             [[2 * NCHUNK * 192, 128], [192, NCHUNK], [1, 192]]),
                                 t["Wqkv16"][h, :, :].rearrange("(c p) e -> p c e", p=128),
                                 semw, mark=f"A.wh.{b}.{h}")
                    for g in range(b * 16, b * 16 + 16):
                        tt = g % 16
                        semx = sW if g % 2 == 0 else sW2
                        if g >= 2:
                            self.wait(sy, sDVE, f"A.x0used.{g - 2}")
                        self.dma(sy, x0b[:, (g % 2) * D:(g % 2 + 1) * D],
                                 t["x"][b, tt * 128:(tt + 1) * 128, :], semx,
                                 mark=f"A.x0.{g}")
                        if g >= 2:
                            gw = g - 2
                            semz = sZ if gw % 2 == 0 else sZ2
                            self.wait(sy, sDVE, f"A.z.{gw}")
                            self.wait(sy, sACT, f"A.z16.{gw}")
                            r0 = (gw // 16) * S + (gw % 16) * 128
                            self.dma(sy, t["z1_32"][r0:r0 + 128, :],
                                     zb[:, (gw % 2) * D:(gw % 2 + 1) * D], semz)
                            self.dma(sy, t["z1_16"][r0:r0 + 128, :],
                                     zb16[:, (gw % 2) * D:(gw % 2 + 1) * D], semz,
                                     mark=f"A.zw.{gw}")
                for gw in (NB * 16 - 2, NB * 16 - 1):
                    semz = sZ if gw % 2 == 0 else sZ2
                    self.wait(sy, sDVE, f"A.z.{gw}")
                    self.wait(sy, sACT, f"A.z16.{gw}")
                    r0 = (gw // 16) * S + (gw % 16) * 128
                    self.dma(sy, t["z1_32"][r0:r0 + 128, :],
                             zb[:, (gw % 2) * D:(gw % 2 + 1) * D], semz)
                    self.dma(sy, t["z1_16"][r0:r0 + 128, :],
                             zb16[:, (gw % 2) * D:(gw % 2 + 1) * D], semz,
                             mark=f"A.zw.{gw}")
                self.wait(sy, sZ, sZ.n)
                self.wait(sy, sZ2, sZ2.n)

            def f_tensor(te):
                self.wait(te, sX, "A.setup")
                for it in self.A_items:
                    k = it["kind"]
                    if k == "biasT":
                        for j in range(3):
                            self.op(te, "transpose", psA[0:64, j * 16:(j + 1) * 16],
                                    bq[:, j * 64:(j + 1) * 64], id32[0:16, 0:16],
                                    inc=(sPE, 1), mark=f"A.ga.{it['ga']}" if j == 2 else None)
                    elif k == "coutrow":
                        nh2 = it["nh2"]
                        self.wait(te, sACT, "A.bvrep16")
                        self.wait(te, sACT, "A.brow16")
                        if it["ga"] >= 2:
                            it2 = self.A_ga_item[it["ga"] - 2]
                            self.wait(te, self.sems[it2["esem"]], f"A.gadone.{it['ga'] - 2}")
                        sl = pa(it["ga"])[0:1, :]
                        for ci in range(NCHUNK):
                            self.op(te, "matmul", sl, bvrep16[:, ci:ci + 1],
                                    wo[:, ci * D + nh2 * 512: ci * D + (nh2 + 1) * 512],
                                    start=(ci == 0), stop=False)
                        self.op(te, "matmul", sl, o128_16[0:1, 0:1],
                                brow16[0:1, nh2 * 512:(nh2 + 1) * 512],
                                start=False, stop=True, inc=(sPE, 1),
                                mark=f"A.ga.{it['ga']}")
                    elif k == "coutb":
                        self.wait(te, sACT, "A.coutrow")
                        for nh2 in range(2):
                            if nh2 == 1:
                                self.wait(te, sACT, "A.coutbe.0")
                            self.op(te, "matmul", psO[:, 0:512],
                                    o128_32[0:1, :], coutrow[0:1, nh2 * 512:(nh2 + 1) * 512],
                                    start=True, stop=True, inc=(sPE, 1),
                                    mark=f"A.coutb.{nh2}")
                    elif k == "proj":
                        b, h, j, qg, g = it["b"], it["h"], it["j"], it["qg"], it["ga"]
                        if h == 0 and j == 0 and qg == 0:
                            self.wait(te, sX, f"A.xh.{b}")
                        if j == 0 and qg == 0:
                            semw = sV if h % 2 == 0 else sV2
                            self.wait(te, semw, f"A.wh.{b}.{h}")
                        if g >= 2:
                            it2 = self.A_ga_item[g - 2]
                            self.wait(te, self.sems[it2["esem"]], f"A.gadone.{g - 2}")
                        woff = (h % 2) * NCHUNK * 192 + j * 64
                        marks = [f"A.ga.{g}"]
                        if j == 2 and qg == 3:
                            marks.append(f"A.wfree.{b}.{h}")
                            if h == NH - 1:
                                marks.append(f"A.xhfree.{b}")
                        for c in range(NCHUNK):
                            self.op(te, "matmul", pa(g)[0:64, :],
                                    wqkv[:, woff + c * 192: woff + c * 192 + 64],
                                    xh[:, c * S + qg * 512: c * S + (qg + 1) * 512],
                                    start=(c == 0), stop=(c == NCHUNK - 1),
                                    inc=(sPE, 1) if c == NCHUNK - 1 else None,
                                    mark=marks if c == NCHUNK - 1 else None)
                    elif k in ("qnorm", "knorm"):
                        b, h, g = it["b"], it["h"], it["ga"]
                        self.wait(te, sDVE, f"A.sq{0 if k == 'qnorm' else 1}.{b}.{h}")
                        if g >= 2:
                            it2 = self.A_ga_item[g - 2]
                            self.wait(te, self.sems[it2["esem"]], f"A.gadone.{g - 2}")
                        qg = it["qg"]
                        self.op(te, "matmul", pa(g)[0:1, :],
                                o64_16[:, :], q2[:, qg * 512:(qg + 1) * 512],
                                start=True, stop=True, inc=(sPE, 1),
                                mark=f"A.ga.{g}")
                    elif k == "vtrans":
                        b, h, kt, gt = it["b"], it["h"], it["kt"], it["gt"]
                        if kt == 0:
                            self.wait(te, sACT, f"A.vt.{b}.{h}")
                            if h == 0:
                                if b == 0:
                                    self.wait(te, sACT, "A.coutbe")
                                else:
                                    self.wait(te, sDVE, f"A.psO.done.{b * 16 - 1}.1")
                        if gt >= 1:
                            self.wait(te, sDVE, f"A.psT.done.{b}.{gt - 1}")
                        self.op(te, "transpose", psO[:, 0:64],
                                vt[:, kt * 128:(kt + 1) * 128], id32[0:64, 0:64],
                                inc=(sPE, 1), mark=f"A.psT.{b}.{gt}")
                    elif k == "score":
                        b, h, qg, kt, gs, i = (it["b"], it["h"], it["qg"],
                                               it["kt"], it["gs"], it["i"])
                        if i == 0 and qg == 0:
                            self.wait(te, sDVE, f"A.aug.{b}.{h}")
                            self.wait(te, sDVE, f"A.kones.{b}.{h}")
                        if gs % 2 == 0 and gs >= 4:
                            self.wait(te, sACT, f"A.expdone.{b}.{gs // 2 - 2}")
                        self.op(te, "matmul", pss(gs),
                                kh[:, kt * 128:(kt + 1) * 128],
                                qh[:, qg * 512:(qg + 1) * 512],
                                start=True, stop=True,
                                inc=(sPE, 1), mark=f"A.psS.{b}.{gs}")
                        if gs % 2 == 1:
                            self._y_mm(te, b, it, gs - 3)
                            self._y_mm(te, b, it, gs - 2)
                        if i == it["nk"] - 1:
                            self._y_mm(te, b, it, gs - 1)
                            self._y_mm(te, b, it, gs)
                    elif k == "bcast":
                        b, h, qg, g = it["b"], it["h"], it["qg"], it["ga"]
                        self.wait(te, sDVE, f"A.rz.{b}.{h}.{qg}")
                        if g >= 2:
                            it2 = self.A_ga_item[g - 2]
                            self.wait(te, self.sems[it2["esem"]], f"A.gadone.{g - 2}")
                        self.op(te, "matmul", pa(g)[0:64, :],
                                o128_32[0:1, 0:64], rzrow[0:1, :],
                                start=True, stop=True, inc=(sPE, 1),
                                mark=f"A.ga.{g}")
                    elif k == "outproj":
                        b, tt, g = it["b"], it["tt"], it["g"]
                        if g == 0:
                            self.wait(te, sACT, "A.coutbe")
                        if tt == 0:
                            self.wait(te, sDVE, f"A.ytdone.{b}")
                        for dg in range(2):
                            if not (g == 0 and dg == 0):
                                self.wait(te, sDVE,
                                          f"A.psO.done.{g}.0" if dg == 1
                                          else f"A.psO.done.{g - 1}.1")
                            for ci in range(NCHUNK):
                                self.op(te, "matmul",
                                        psO[:, 0:512],
                                        yt[:, ci * S + tt * 128: ci * S + (tt + 1) * 128],
                                        wo[:, ci * D + dg * 512: ci * D + (dg + 1) * 512],
                                        start=(ci == 0), stop=(ci == NCHUNK - 1),
                                        inc=(sPE, 1) if ci == NCHUNK - 1 else None,
                                        mark=f"A.psO.{g}.{dg}" if ci == NCHUNK - 1 else None)

            self._A_ymeta = {}
            for it in items:
                if it["kind"] == "score":
                    self._A_ymeta[(it["b"], it["gs"])] = it

            def y_mm(te, b, cur, gs):
                if gs < 0:
                    return
                itm = self._A_ymeta.get((b, gs))
                if itm is None or itm.get("ydone"):
                    return
                itm["ydone"] = True
                h, qg, kt, i = itm["h"], itm["qg"], itm["kt"], itm["i"]
                gy = itm["gy"]
                sem, key = ((sDVE, f"A.pmask.{b}.{gs}") if itm["straddle"]
                            else (sACT, f"A.expp.{b}.{gs // 2}"))
                self.wait(te, sem, key)
                if i == 0 and gy >= 1:
                    self.wait(te, sDVE, f"A.ymul.{b}.{gy - 1}")
                last = (i == itm["nk"] - 1)
                self.op(te, "matmul", psY[:, :],
                        va[:, kt * 96:(kt + 1) * 96], pb(gs),
                        start=(i == 0), stop=last,
                        inc=(sPE, 1) if last else None,
                        mark=f"A.psY.{b}.{gy}" if last else None)

            self._y_mm = lambda te, b, cur, gs: y_mm(te, b, cur, gs)

            def f_scalar(se):
                for it in self.A_items:
                    k = it["kind"]
                    if k == "biasT":
                        g = it["ga"]
                        self.wait(se, sPE, f"A.ga.{g}")
                        for j in range(3):
                            self.op(se, "copy", bqT[:, j * 16:(j + 1) * 16],
                                    psA[0:64, j * 16:(j + 1) * 16],
                                    inc=(sACT, 1),
                                    mark=[f"A.gadone.{g}", "A.bqTe"] if j == 2 else None)
                        self.wait(se, sDVE, "A.bvrep")
                        self.op(se, "copy", bvrep16[:, :], bvrep[:, :],
                                inc=(sACT, 1), mark="A.bvrep16")
                        self.wait(se, sX, "A.setup")
                        self.op(se, "copy", brow16[:, :], brow[:, :],
                                inc=(sACT, 1), mark="A.brow16")
                    elif k == "coutrow":
                        g, nh2 = it["ga"], it["nh2"]
                        self.wait(se, sPE, f"A.ga.{g}")
                        self.op(se, "copy", coutrow[0:1, nh2 * 512:(nh2 + 1) * 512],
                                pa(g)[0:1, :], inc=(sACT, 1),
                                mark=[f"A.gadone.{g}"] + (["A.coutrow"] if nh2 == 1 else []))
                    elif k == "coutb":
                        for nh2 in range(2):
                            self.wait(se, sPE, f"A.coutb.{nh2}")
                            self.op(se, "copy", coutb[:, nh2 * 512:(nh2 + 1) * 512],
                                    psO[:, 0:512],
                                    inc=(sACT, 1),
                                    mark="A.coutbe" if nh2 == 1 else "A.coutbe.0")
                    elif k == "proj":
                        if it["j"] != 2:
                            continue
                        b, h, qg, g = it["b"], it["h"], it["qg"], it["ga"]
                        self.wait(se, sPE, f"A.ga.{g}")
                        marks = [f"A.gadone.{g}"]
                        if qg == 3:
                            marks.append(f"A.vt.{b}.{h}")
                        self.op(se, "copy", vt[0:64, qg * 512:(qg + 1) * 512],
                                pa(g)[0:64, :], inc=(sACT, 1), mark=marks)
                    elif k == "knorm":
                        b, h, qg, g = it["b"], it["h"], it["qg"], it["ga"]
                        self.wait(se, sPE, f"A.ga.{g}")
                        self.op(se, "copy",
                                n_row[0:1, qg * 512:(qg + 1) * 512],
                                pa(g)[0:1, :], inc=(sACT, 1),
                                mark=[f"A.gadone.{g}"] +
                                     ([f"A.nrow1.{b}.{h}"] if qg == 3 else []))
                    elif k == "score":
                        b, gs = it["b"], it["gs"]
                        if gs % 2 == 0:
                            continue
                        gp = gs // 2
                        self.wait(se, sPE, f"A.psS.{b}.{gs}")
                        self.op(se, "activation", pbp(gp), pssp(gp), AF.Exp,
                                inc=(sACT, 1),
                                mark=[f"A.expp.{b}.{gp}", f"A.expdone.{b}.{gp}"])
                    elif k == "bcast":
                        b, h, qg, g = it["b"], it["h"], it["qg"], it["ga"]
                        self.wait(se, sPE, f"A.ga.{g}")
                        self.op(se, "copy", rzb[:, :], pa(g)[0:64, :],
                                inc=(sACT, 1),
                                mark=[f"A.gadone.{g}", f"A.rzb.{b}.{h}.{qg}"])
                    elif k == "epilog":
                        g = it["g"]
                        par = g % 2
                        self.wait(se, sDVE, f"A.var.{g}")
                        self.op(se, "activation", stat[:, par * 4 + 3:par * 4 + 4],
                                stat[:, par * 4 + 2:par * 4 + 3], AF.Sqrt,
                                scale=1.0 / D, bias=epsc[:, :],
                                inc=(sACT, 1), mark=f"A.sd.{g}")
                        self.wait(se, sDVE, f"A.z.{g}")
                        self.op(se, "activation", zb16[:, par * D:(par + 1) * D],
                                zb[:, par * D:(par + 1) * D], AF.Copy,
                                inc=(sACT, 1), mark=f"A.z16.{g}")

            def f_vector(ve):
                for it in self.A_items:
                    k = it["kind"]
                    if k == "biasT":
                        self.wait(ve, sACT, f"A.gadone.{it['ga']}")
                        self.op(ve, "tensor_scalar_mul", bqTs[:, :], bqT[:, 0:16],
                                0.125, inc=(sDVE, 1))
                        self.op(ve, "memset", epsc[:, :], EPS, inc=(sDVE, 1))
                        self.op(ve, "memset", qh[64:96, :], 0.0, inc=(sDVE, 1))
                        self.op(ve, "memset", kh[64:96, :], 0.0, inc=(sDVE, 1))
                        self.op(ve, "memset", va[:, :], 0.0, inc=(sDVE, 1))
                        for c in range(8):
                            for r in range(2):
                                h = 2 * c + r
                                self.op(ve, "tensor_copy",
                                        bvrep[r * 64:(r + 1) * 64, c:c + 1],
                                        bqT[:, 32 + h:32 + h + 1],
                                        inc=(sDVE, 1),
                                        mark="A.bvrep" if h == 15 else None)
                        self.op(ve, "memset",
                                bass.AP(va, 64, [[16 * 96, 128], [96, 16], [1, 1]]), 1.0,
                                inc=(sDVE, 1), mark="A.vones")
                    elif k == "proj":
                        if it["j"] == 2:
                            continue
                        b, h, j, qg, g = it["b"], it["h"], it["j"], it["qg"], it["ga"]
                        self.wait(ve, sPE, f"A.ga.{g}")
                        marks = [f"A.gadone.{g}"]
                        if qg == 3:
                            marks.append(f"A.{'qk'[j]}evac.{b}.{h}")
                        if j == 0:
                            self.op(ve, "tensor_scalar",
                                    qh[0:64, qg * 512:(qg + 1) * 512],
                                    pa(g)[0:64, :], 0.125, bqTs[:, h:h + 1],
                                    ALU.mult, ALU.add,
                                    inc=(sDVE, 1), mark=marks)
                        else:
                            self.op(ve, "tensor_scalar_add",
                                    kh[0:64, qg * 512:(qg + 1) * 512],
                                    pa(g)[0:64, :], bqT[:, 16 + h:16 + h + 1],
                                    inc=(sDVE, 1), mark=marks)
                    elif k == "qnorm":
                        b, h, qg, g = it["b"], it["h"], it["qg"], it["ga"]
                        self.wait(ve, sPE, f"A.ga.{g}")
                        self.op(ve, "tensor_scalar_mul",
                                b_row[0:1, qg * 512:(qg + 1) * 512],
                                pa(g)[0:1, :], 0.5 * LAM,
                                inc=(sDVE, 1),
                                mark=[f"A.gadone.{g}"] +
                                     ([f"A.nrow0.{b}.{h}"] if qg == 3 else []))
                    elif k == "aug":
                        b, h = it["b"], it["h"]
                        self.op(ve, "tensor_scalar_mul", qh[64:65, :], b_row[0:1, :],
                                -1.0, inc=(sDVE, 1), mark=f"A.aug.{b}.{h}")
                    elif k == "sq0":
                        b, h = it["b"], it["h"]
                        self.op(ve, "tensor_mul", q2[:, :], qh[0:64, :], qh[0:64, :],
                                inc=(sDVE, 1), mark=f"A.sq0.{b}.{h}")
                    elif k == "sq1":
                        b, h = it["b"], it["h"]
                        self.wait(ve, sPE, f"A.ga.{it_ga_last_qnorm[(b, h)]}")
                        self.op(ve, "tensor_mul", q2[:, :], kh[0:64, :], kh[0:64, :],
                                inc=(sDVE, 1), mark=f"A.sq1.{b}.{h}")
                    elif k == "vtrans":
                        b, gt, kt = it["b"], it["gt"], it["kt"]
                        self.wait(ve, sPE, f"A.psT.{b}.{gt}")
                        self.op(ve, "tensor_copy", va[:, kt * 96:kt * 96 + 64],
                                psO[:, 0:64], inc=(sDVE, 1),
                                mark=f"A.psT.done.{b}.{gt}")
                    elif k == "maxb":
                        b, h = it["b"], it["h"]
                        self.wait(ve, sACT, f"A.nrow1.{b}.{h}")
                        self.op(ve, "reduce_max", m2c[0:1, 0:1], n_row[0:1, 0:S],
                                axis=AX.X, inc=(sDVE, 1))
                        self.op(ve, "tensor_scalar_mul", m2c[0:1, 1:2], m2c[0:1, 0:1],
                                0.5 / LAM, inc=(sDVE, 1))
                        self.op(ve, "tensor_scalar_add", b_row[0:1, :], b_row[0:1, :],
                                m2c[0:1, 1:2],
                                inc=(sDVE, 1), mark=f"A.brow.{b}.{h}")
                    elif k == "kones":
                        b, h = it["b"], it["h"]
                        self.op(ve, "memset", kh[64:65, :], 1.0,
                                inc=(sDVE, 1), mark=f"A.kones.{b}.{h}")
                    elif k == "score":
                        if it["straddle"]:
                            b, gs = it["b"], it["gs"]
                            m = it["kt"] - 4 * it["qg"]
                            self.wait(ve, sACT, f"A.expp.{b}.{gs // 2}")
                            self.op(ve, "tensor_mul", pb(gs), pb(gs),
                                    cmask[:, m * 512:(m + 1) * 512],
                                    inc=(sDVE, 1), mark=f"A.pmask.{b}.{gs}")
                    elif k == "recip":
                        b, h, qg, gy = it["b"], it["h"], it["qg"], it["gy"]
                        self.wait(ve, sPE, f"A.psY.{b}.{gy}")
                        self.op(ve, "reciprocal", rzrow[0:1, :], psY[64:65, :],
                                inc=(sDVE, 1), mark=f"A.rz.{b}.{h}.{qg}")
                    elif k == "ymul":
                        b, h, qg, gy = it["b"], it["h"], it["qg"], it["gy"]
                        self.wait(ve, sACT, f"A.rzb.{b}.{h}.{qg}")
                        strip, row = h // 2, (h % 2) * 64
                        self.op(ve, "tensor_mul",
                                yt[row:row + 64,
                                   strip * S + qg * 512: strip * S + (qg + 1) * 512],
                                psY[0:64, :], rzb[:, :],
                                inc=(sDVE, 1),
                                mark=[f"A.ymul.{b}.{gy}"] +
                                     ([f"A.ytdone.{b}"] if it["last"] else []))
                    elif k == "epilog":
                        b, tt, g = it["b"], it["tt"], it["g"]
                        par = g % 2
                        semx = sW if g % 2 == 0 else sW2
                        self.wait(ve, semx, f"A.x0.{g}")
                        if g >= 2:
                            semz = sZ if g % 2 == 0 else sZ2
                            self.wait(ve, semz, f"A.zw.{g - 2}")
                        self.wait(ve, sPE, f"A.psO.{g}.0")
                        self.op(ve, "tensor_add", rb[:, par * D:par * D + 512],
                                psO[:, 0:512], x0b[:, par * D:par * D + 512],
                                inc=(sDVE, 1), mark=f"A.psO.done.{g}.0")
                        self.wait(ve, sPE, f"A.psO.{g}.1")
                        self.op(ve, "tensor_add", rb[:, par * D + 512:(par + 1) * D],
                                psO[:, 0:512], x0b[:, par * D + 512:(par + 1) * D],
                                inc=(sDVE, 1),
                                mark=[f"A.psO.done.{g}.1", f"A.x0used.{g}"])
                        self.op(ve, "tensor_add", rb[:, par * D:(par + 1) * D],
                                rb[:, par * D:(par + 1) * D], coutb[:, :],
                                inc=(sDVE, 1))
                        self.op(ve, "reduce_sum", stat[:, par * 4:par * 4 + 1],
                                rb[:, par * D:(par + 1) * D], axis=AX.X,
                                negate=True, inc=(sDVE, 1))
                        self.op(ve, "tensor_scalar_mul", stat[:, par * 4 + 1:par * 4 + 2],
                                stat[:, par * 4:par * 4 + 1], 1.0 / D, inc=(sDVE, 1))
                        self.op(ve, "tensor_scalar_add", cb[:, par * D:(par + 1) * D],
                                rb[:, par * D:(par + 1) * D],
                                stat[:, par * 4 + 1:par * 4 + 2], inc=(sDVE, 1))
                        self.op(ve, "tensor_mul", sqb[:, par * D:(par + 1) * D],
                                cb[:, par * D:(par + 1) * D],
                                cb[:, par * D:(par + 1) * D], inc=(sDVE, 1))
                        self.op(ve, "reduce_sum", stat[:, par * 4 + 2:par * 4 + 3],
                                sqb[:, par * D:(par + 1) * D], axis=AX.X,
                                inc=(sDVE, 1), mark=f"A.var.{g}")
                        self.wait(ve, sACT, f"A.sd.{g}")
                        self.op(ve, "reciprocal", stat[:, par * 4 + 3:par * 4 + 4],
                                stat[:, par * 4 + 3:par * 4 + 4], inc=(sDVE, 1))
                        self.op(ve, "tensor_scalar_mul", zb[:, par * D:(par + 1) * D],
                                cb[:, par * D:(par + 1) * D],
                                stat[:, par * 4 + 3:par * 4 + 4],
                                inc=(sDVE, 1), mark=f"A.z.{g}")

            it_ga_last_qnorm = {}
            for it in items:
                if it["kind"] == "qnorm":
                    it_ga_last_qnorm[(it["b"], it["h"])] = it["ga"]

            def reset_items():
                for it in items:
                    it.pop("ydone", None)

            streams = {"sync": f_sync, "tensor": lambda e: (reset_items(), f_tensor(e)),
                       "vector": f_vector, "scalar": f_scalar}
            self.run_phase(streams)

    def emit_phaseB(self):
        t = self.t
        nc = self.nc
        sX = self.sems["sX"]
        sW, sW2 = self.sems["sW"], self.sems["sW2"]
        sV, sV2 = self.sems["sGI"], self.sems["sGA"]
        sZ, sZ2 = self.sems["sZ"], self.sems["sZ2"]
        sPE, sACT = self.sems["sPE"], self.sems["sACT"]
        sDVE, sPOOL = self.sems["sDVE"], self.sems["sPOOL"]
        sD = self.sems["sD2DB"]
        I32 = mybir.dt.int32
        from concourse.bass_core_rust import IndirectOffsetOnAxis

        C = 768
        NTL = C // 128
        TPW = ((0, 512), (512, C - 512))
        ROWW = D + 128
        NS = 4
        FS = DFF // NS
        NFC = FS // 128

        xg_d = [nc.dram_tensor(f"B_xg{e}", [C, ROWW], BF16, kind="Internal")
                for e in range(NEXP)]
        dn_d = [nc.dram_tensor(f"B_dn{e}", [C, D], BF16, kind="Internal")
                for e in range(NEXP)]

        items = []
        god = 0

        def G(kind, **kw):
            items.append(dict(kind=kind, **kw))

        for c in range(NCHUNK):
            G("rwT", c=c, god=god, esem="sACT"); god += 1
        for b in range(NB):
            for h2 in range(4):
                G("xh", b=b, h2=h2)
                for tt in range(h2 * 4, h2 * 4 + 4):
                    G("logit", b=b, tt=tt, h2=h2, god=god, esem="sACT"); god += 1
                    G("smax", b=b, tt=tt)
            for tt in range(16):
                G("pos", b=b, tt=tt, god=god, esem="sDVE"); god += 1
            G("accinit", b=b)
            for tt in range(16):
                G("xstg", b=b, tt=tt)
                G("xscat", b=b, tt=tt, last=(tt == 15))
            sid = 0
            for e in range(NEXP):
                G("xgload", b=b, e=e)
                G("cwbE", b=b, e=e, god=god, esem="sACT"); god += 1
                G("cwbE2", b=b, e=e, god=god, esem="sACT"); god += 1
                if e >= 1:
                    G("dstore", b=b, e=e - 1)
                for s in range(NS):
                    G("wload", b=b, e=e, s=s, sid=b * NEXP * NS + e * NS + s)
                    gid0 = (b * NEXP * NS + e * NS + s) * 16
                    for fc in range(NFC):
                        for tp in range(2):
                            G("gu", b=b, e=e, s=s, fc=fc, tp=tp,
                              tok0=TPW[tp][0], tw=TPW[tp][1],
                              sid=b * NEXP * NS + e * NS + s,
                              gid=gid0 + fc * 2 + tp)
                    for tl in range(NTL):
                        G("down", b=b, e=e, s=s, tl=tl,
                          sid=b * NEXP * NS + e * NS + s,
                          god=god, esem=("sACT" if s == 0 else "sDVE"))
                        god += 1
                G("accg", b=b, e=e)
            G("dstore", b=b, e=NEXP - 1)
            G("accglast", b=b)
            for tt in range(16):
                G("epilog", b=b, tt=tt, g=b * 16 + tt)
        self.B_items = items
        self.B_god_item = {it["god"]: it for it in items if "god" in it}
        self.B_last_gu = {}
        for it in items:
            if it["kind"] == "gu":
                self.B_last_gu[(it["b"], it["e"])] = it["gid"]

        with ExitStack() as st:
            xh = self.sb(st, "B_xh", [128, NCHUNK * 512], BF16)
            xs = self.sb(st, "B_xs", [128, 2 * ROWW], BF16)
            xgT = self.sb(st, "B_xgT", [128, NCHUNK * C], BF16)
            cwTg = self.sb(st, "B_cwTg", [128, C], BF16)
            cwb = self.sb(st, "B_cwb", [128, C], BF16)
            hb = self.sb(st, "B_h", [128, NFC * C], BF16)
            acc = self.sb(st, "B_acc", [128, 16 * 1024], BF16)
            stg = self.sb(st, "B_stg", [128, NTL * 1024], BF16)
            wga = self.sb(st, "B_wg", [128, 2 * NCHUNK * 1024], BF16)
            wua = self.sb(st, "B_wu", [128, 2 * NCHUNK * 1024], BF16)
            wda = self.sb(st, "B_wd", [128, NFC * 1024], BF16)
            sg = self.sb(st, "B_sg", [128, 2 * 512], BF16)
            ub = self.sb(st, "B_ub", [128, 2 * 512], BF16)
            rwf = self.sb(st, "B_rwf", [8, D], F32)
            rwT = self.sb(st, "B_rwT", [128, NCHUNK * 8], BF16)
            lgt = self.sb(st, "B_lgt", [128, 8], F32)
            ee = self.sb(st, "B_ee", [128, 8], F32)
            eq = self.sb(st, "B_eq", [128, 8], F32)
            eqa = self.sb(st, "B_eqa", [128, 16 * 8], BF16)
            cwa = self.sb(st, "B_cwa", [128, 16 * 8], BF16)
            posf = self.sb(st, "B_posf", [128, 8], F32)
            offw = self.sb(st, "B_offw", [128, 8], F32)
            offs = self.sb(st, "B_offs", [128, 16 * 8], I32)
            rstat = self.sb(st, "B_rstat", [128, 4], F32)
            sel8 = self.sb(st, "B_sel8", [8, 8 * 128], BF16)
            allo = self.sb(st, "B_allo", [128, 128], BF16)
            sut = self.sb(st, "B_sut", [128, 128], BF16)
            id32 = self.sb(st, "B_id32", [128, 128], F32)
            rb = self.sb(st, "B_rb", [128, 2 * D], F32)
            cb = self.sb(st, "B_cb", [128, D], F32)
            zb16 = self.sb(st, "B_zb16", [128, 2 * D], BF16)
            stat = self.sb(st, "B_stat", [128, 8], F32)
            epsc = self.sb(st, "B_epsc", [128, 1], F32)

            bcreg = st.enter_context(nc.gpsimd.register("B_bc"))
            psG = self.ps(st, "B_psG", [128, 2 * 512])
            psU = self.ps(st, "B_psU", [128, 2 * 512])
            psO = self.ps(st, "B_psO", [128, 2 * 1024])

            def pg(i):
                return psG[:, (i % 2) * 512:(i % 2 + 1) * 512]

            def pu(i):
                return psU[:, (i % 2) * 512:(i % 2 + 1) * 512]

            def po(i):
                return psO[:, (i % 2) * 1024:(i % 2 + 1) * 1024]

            def god_wait(eng, g):
                if g >= 2:
                    it2 = self.B_god_item[g - 2]
                    self.wait(eng, self.sems[it2["esem"]], f"B.godone.{g - 2}")

            def f_sync(sy):
                self.dma(sy, id32[:, :], self.c_ident32[:, :], sX)
                self.dma(sy, sel8[:, :], self.c_sel8[:, :], sX)
                self.dma(sy, allo[:, :], self.c_allones[:, :], sX)
                self.dma(sy, sut[:, :], self.c_strictut[:, :], sX)
                self.dma(sy, rwf[:, :], t["rw"][:, :], sX, mark="B.setup")
                self.wait(sy, sX, "B.setup")
                self.wait(sy, sD, "d2dB")
                self.wait(sy, sDVE, "B.xs0")
                for e in range(NEXP):
                    for r8 in range(C // 128):
                        self.dma(sy, xg_d[e][r8 * 128:(r8 + 1) * 128, :],
                                 xs[:, 0:ROWW], sX,
                                 mark="B.xgz" if (e == NEXP - 1 and r8 == C // 128 - 1)
                                 else None)
                self.wait(sy, sX, "B.xgz")
                for it in self.B_items:
                    k = it["kind"]
                    if k == "xh":
                        b, h2 = it["b"], it["h2"]
                        if b > 0 or h2 > 0:
                            self.wait(sy, sPE, f"B.xhfree.{b}.{h2 - 1}"
                                      if h2 > 0 else f"B.xhfree.{b - 1}.3")
                        for c in range(NCHUNK):
                            self.dma(sy, xh[:, c * 512:(c + 1) * 512],
                                     t["z1_16"][b * S + h2 * 512:
                                                b * S + (h2 + 1) * 512,
                                                c * 128:(c + 1) * 128], sX,
                                     transpose=True,
                                     mark=f"B.xh.{b}.{h2}" if c == NCHUNK - 1 else None)
                    elif k == "xstg":
                        b, tt = it["b"], it["tt"]
                        semt = self.sems["sT"] if tt % 2 == 0 else self.sems["sT2"]
                        if b * 16 + tt >= 2:
                            pb, pt = (b, tt - 2) if tt >= 2 else (b - 1, tt + 14)
                            self.wait(sy, sV, f"B.xscat.{pb}.{pt}")
                        self.dma(sy, xs[:, (tt % 2) * ROWW:(tt % 2) * ROWW + D],
                                 t["z1_16"][b * S + tt * 128:
                                            b * S + (tt + 1) * 128, :], semt,
                                 mark=f"B.xstg.{b}.{tt}")
                    elif k == "xgload":
                        b, e = it["b"], it["e"]
                        self.wait(sy, sV, f"B.xscatall.{b}")
                        if (b, e) != (0, 0):
                            pb, pe = (b, e - 1) if e > 0 else (b - 1, NEXP - 1)
                            self.wait(sy, sPE, f"B.gufree.{pb}.{pe}")
                        for c in range(NCHUNK):
                            self.dma(sy, xgT[:, c * C:(c + 1) * C],
                                     xg_d[e][:, c * 128:(c + 1) * 128], sW,
                                     transpose=True)
                        self.dma(sy, cwTg[:, :], xg_d[e][:, D:D + 128], sW,
                                 transpose=True, mark=f"B.xgl.{b}.{e}")
                    elif k == "wload":
                        e, s, sid = it["e"], it["s"], it["sid"]
                        semw = self.sems["sU"] if sid % 2 == 0 else self.sems["sU2"]
                        if sid >= 2:
                            self.wait(sy, sPE, f"B.wfree.{sid - 2}")
                        w = sid % 2
                        self.dma(sy, bass.AP(wga, w * NCHUNK * 1024,
                                             [[2 * NCHUNK * 1024, 128], [1024, NCHUNK], [1, 1024]]),
                                 t["wg16"][e, :, s * FS:(s + 1) * FS]
                                 .rearrange("(c p) f -> p c f", p=128), semw)
                        self.dma(sy, bass.AP(wua, w * NCHUNK * 1024,
                                             [[2 * NCHUNK * 1024, 128], [1024, NCHUNK], [1, 1024]]),
                                 t["wu16"][e, :, s * FS:(s + 1) * FS]
                                 .rearrange("(c p) f -> p c f", p=128), semw)
                        if sid >= 1:
                            self.wait(sy, sPE, f"B.dfree.{sid - 1}")
                        self.dma(sy, bass.AP(wda, 0,
                                             [[NFC * 1024, 128], [1024, NFC], [1, 1024]]),
                                 t["wd16"][e, s * FS:(s + 1) * FS, :]
                                 .rearrange("(c p) f -> p c f", p=128), semw,
                                 mark=f"B.wl.{sid}")
                    elif k == "dstore":
                        b, e = it["b"], it["e"]
                        self.wait(sy, sDVE, f"B.stg.{b}.{e}")
                        self.dma(sy,
                                 dn_d[e][:, :].rearrange("(tl p) d -> p tl d", p=128),
                                 bass.AP(stg, 0, [[NTL * 1024, 128], [1024, NTL], [1, 1024]]),
                                 sW2, mark=f"B.dst.{b}.{e}")
                        self.wait(sy, sW2, f"B.dst.{b}.{e}")
                    elif k == "epilog":
                        b, tt, g = it["b"], it["tt"], it["g"]
                        if g >= 2:
                            gw = g - 2
                            semz = sZ if gw % 2 == 0 else sZ2
                            self.wait(sy, sDVE, f"B.z.{gw}")
                            self.wait(sy, sACT, f"B.z16.{gw}")
                            rw0 = (gw // 16) * S + (gw % 16) * 128
                            self.dma(sy, t["z2_32"][rw0:rw0 + 128, :],
                                     rb[:, (gw % 2) * D:(gw % 2 + 1) * D], semz)
                            self.dma(sy, t["z2_16"][rw0:rw0 + 128, :],
                                     zb16[:, (gw % 2) * D:(gw % 2 + 1) * D], semz,
                                     mark=f"B.zw.{gw}")
                for gw in (NB * 16 - 2, NB * 16 - 1):
                    semz = sZ if gw % 2 == 0 else sZ2
                    self.wait(sy, sDVE, f"B.z.{gw}")
                    self.wait(sy, sACT, f"B.z16.{gw}")
                    rw0 = (gw // 16) * S + (gw % 16) * 128
                    self.dma(sy, t["z2_32"][rw0:rw0 + 128, :],
                             rb[:, (gw % 2) * D:(gw % 2 + 1) * D], semz)
                    self.dma(sy, t["z2_16"][rw0:rw0 + 128, :],
                             zb16[:, (gw % 2) * D:(gw % 2 + 1) * D], semz,
                             mark=f"B.zw.{gw}")
                self.wait(sy, sZ, sZ.n)
                self.wait(sy, sZ2, sZ2.n)

            def f_tensor(te):
                self.wait(te, sX, "B.setup")
                for it in self.B_items:
                    k = it["kind"]
                    if k == "rwT":
                        c, g = it["c"], it["god"]
                        god_wait(te, g)
                        self.op(te, "transpose", po(g)[0:128, 0:8],
                                rwf[:, c * 128:(c + 1) * 128], id32[0:8, 0:8],
                                inc=(sPE, 1), mark=f"B.god.{g}")
                    elif k == "logit":
                        b, tt, h2, g = it["b"], it["tt"], it["h2"], it["god"]
                        if tt % 4 == 0:
                            self.wait(te, sX, f"B.xh.{b}.{h2}")
                            if b == 0 and h2 == 0:
                                self.wait(te, sACT, "B.rwTe")
                        god_wait(te, g)
                        tl = tt % 4
                        for dc in range(NCHUNK):
                            self.op(te, "matmul", po(g)[:, 0:8],
                                    xh[:, dc * 512 + tl * 128: dc * 512 + (tl + 1) * 128],
                                    rwT[:, dc * 8:(dc + 1) * 8],
                                    start=(dc == 0), stop=(dc == NCHUNK - 1),
                                    inc=(sPE, 1) if dc == NCHUNK - 1 else None,
                                    mark=[f"B.god.{g}"] +
                                         ([f"B.xhfree.{b}.{h2}"] if tl == 3 and dc == NCHUNK - 1 else [])
                                    if dc == NCHUNK - 1 else None)
                    elif k == "pos":
                        b, tt, g = it["b"], it["tt"], it["god"]
                        self.wait(te, sDVE, f"B.eqa.{b}.15")
                        god_wait(te, g)
                        nk = tt + 1
                        for kt in range(nk):
                            lt = sut if kt == tt else allo
                            self.op(te, "matmul", po(g)[:, 0:8],
                                    lt[:, :], eqa[:, kt * 8:(kt + 1) * 8],
                                    start=(kt == 0), stop=(kt == nk - 1),
                                    inc=(sPE, 1) if kt == nk - 1 else None,
                                    mark=f"B.god.{g}" if kt == nk - 1 else None)
                    elif k in ("cwbE", "cwbE2"):
                        b, e, g = it["b"], it["e"], it["god"]
                        half = 0 if k == "cwbE" else 1
                        hw = 512 if half == 0 else C - 512
                        if half == 0:
                            self.wait(te, sW, f"B.xgl.{b}.{e}")
                        god_wait(te, g)
                        self.op(te, "matmul", po(g)[:, 0:hw],
                                sel8[:, e * 128:(e + 1) * 128],
                                cwTg[0:8, half * 512:half * 512 + hw],
                                start=True, stop=True,
                                inc=(sPE, 1), mark=f"B.god.{g}")
                    elif k == "gu":
                        b, e, s, fc, tp = it["b"], it["e"], it["s"], it["fc"], it["tp"]
                        sid, gg = it["sid"], it["gid"]
                        if fc == 0 and tp == 0:
                            semw = self.sems["sU"] if sid % 2 == 0 else self.sems["sU2"]
                            self.wait(te, semw, f"B.wl.{sid}")
                            if s == 0:
                                self.wait(te, sACT, f"B.cwbe.{b}.{e}.1")
                        if gg >= 2:
                            self.wait(te, sACT, f"B.sgdone.{gg - 2}")
                            self.wait(te, sDVE, f"B.ubdone.{gg - 2}")
                        w = sid % 2
                        tok0, tw = it["tok0"], it["tw"]
                        for dc in range(NCHUNK):
                            self.op(te, "matmul", pg(gg)[:, 0:tw],
                                    wga[:, w * NCHUNK * 1024 + dc * 1024 + fc * 128:
                                         w * NCHUNK * 1024 + dc * 1024 + fc * 128 + 128],
                                    xgT[:, dc * C + tok0: dc * C + tok0 + tw],
                                    start=(dc == 0), stop=(dc == NCHUNK - 1),
                                    inc=(sPE, 1) if dc == NCHUNK - 1 else None,
                                    mark=f"B.g.{gg}" if dc == NCHUNK - 1 else None)
                        umarks = [f"B.u.{gg}"]
                        if s == NS - 1 and fc == NFC - 1 and tp == 1:
                            umarks.append(f"B.gufree.{b}.{e}")
                        for dc in range(NCHUNK):
                            self.op(te, "matmul", pu(gg)[:, 0:tw],
                                    wua[:, w * NCHUNK * 1024 + dc * 1024 + fc * 128:
                                         w * NCHUNK * 1024 + dc * 1024 + fc * 128 + 128],
                                    xgT[:, dc * C + tok0: dc * C + tok0 + tw],
                                    start=(dc == 0), stop=(dc == NCHUNK - 1),
                                    inc=(sPE, 1) if dc == NCHUNK - 1 else None,
                                    mark=umarks if dc == NCHUNK - 1 else None)
                    elif k == "down":
                        b, e, s, tl = it["b"], it["e"], it["s"], it["tl"]
                        sid, g = it["sid"], it["god"]
                        if tl == 0:
                            self.wait(te, sPOOL, f"B.h.{sid}")
                        god_wait(te, g)
                        marks = [f"B.god.{g}"]
                        if tl == NTL - 1:
                            marks.append(f"B.dfree.{sid}")
                            marks.append(f"B.wfree.{sid}")
                            marks.append(f"B.hfree.{sid}")
                        nmm = 2 * NFC
                        i = 0
                        for dg in range(2):
                            for fc in range(NFC):
                                i += 1
                                self.op(te, "matmul",
                                        po(g)[:, dg * 512:(dg + 1) * 512],
                                        hb[:, fc * C + tl * 128: fc * C + tl * 128 + 128],
                                        wda[:, fc * 1024 + dg * 512: fc * 1024 + (dg + 1) * 512],
                                        start=(fc == 0), stop=(fc == NFC - 1),
                                        inc=(sPE, 1) if i == nmm else None,
                                        mark=marks if i == nmm else None)

            def f_scalar(se):
                for it in self.B_items:
                    k = it["kind"]
                    if k == "rwT":
                        c, g = it["c"], it["god"]
                        self.wait(se, sPE, f"B.god.{g}")
                        self.op(se, "copy", rwT[:, c * 8:(c + 1) * 8],
                                po(g)[0:128, 0:8], inc=(sACT, 1),
                                mark=[f"B.godone.{g}"] +
                                     (["B.rwTe"] if c == NCHUNK - 1 else []))
                    elif k == "logit":
                        b, tt, g = it["b"], it["tt"], it["god"]
                        self.wait(se, sPE, f"B.god.{g}")
                        if (b, tt) != (0, 0):
                            pb, pt = (b, tt - 1) if tt > 0 else (b - 1, 15)
                            self.wait(se, sDVE, f"B.cw.{pb}.{pt}")
                        self.op(se, "copy", lgt[:, :], po(g)[:, 0:8],
                                inc=(sACT, 1),
                                mark=[f"B.godone.{g}", f"B.lgt.{b}.{tt}"])
                    elif k == "smax":
                        b, tt = it["b"], it["tt"]
                        self.wait(se, sDVE, f"B.nrmax.{b}.{tt}")
                        self.op(se, "activation", ee[:, :], lgt[:, :], AF.Exp,
                                bias=rstat[:, 0:1], inc=(sACT, 1),
                                mark=f"B.ee.{b}.{tt}")
                    elif k == "pos":
                        b, tt, g = it["b"], it["tt"], it["god"]
                        self.wait(se, sPE, f"B.god.{g}")
                        if (b, tt) != (0, 0):
                            pb, pt = (b, tt - 1) if tt > 0 else (b - 1, 15)
                            self.wait(se, sDVE, f"B.offs.{pb}.{pt}")
                        self.op(se, "copy", posf[:, :], po(g)[:, 0:8],
                                inc=(sACT, 1),
                                mark=[f"B.posf.{b}.{tt}"])
                    elif k in ("cwbE", "cwbE2"):
                        b, e, g = it["b"], it["e"], it["god"]
                        half = 0 if k == "cwbE" else 1
                        hw = 512 if half == 0 else C - 512
                        self.wait(se, sPE, f"B.god.{g}")
                        if half == 0 and (b, e) != (0, 0):
                            pb, pe = (b, e - 1) if e > 0 else (b - 1, NEXP - 1)
                            self.wait(se, sDVE, f"B.ub.{self.B_last_gu[(pb, pe)]}")
                        self.op(se, "copy", cwb[:, half * 512:half * 512 + hw],
                                po(g)[:, 0:hw], inc=(sACT, 1),
                                mark=[f"B.godone.{g}", f"B.cwbe.{b}.{e}.{half}"])
                    elif k == "gu":
                        gg = it["gid"]
                        par = gg % 2
                        tw = it["tw"]
                        self.wait(se, sPE, f"B.g.{gg}")
                        if gg >= 2:
                            self.wait(se, sPOOL, f"B.hm.{gg - 2}")
                        self.op(se, "activation", sg[:, par * 512:par * 512 + tw],
                                pg(gg)[:, 0:tw], AF.Silu, inc=(sACT, 1),
                                mark=[f"B.sg.{gg}", f"B.sgdone.{gg}"])
                    elif k == "down":
                        if it["s"] != 0:
                            continue
                        b, e, tl, g = it["b"], it["e"], it["tl"], it["god"]
                        self.wait(se, sPE, f"B.god.{g}")
                        if tl == 0 and (b, e) != (0, 0):
                            pb, pe = (b, e - 1) if e > 0 else (b - 1, NEXP - 1)
                            self.wait(se, sW2, f"B.dst.{pb}.{pe}")
                        self.op(se, "copy", stg[:, tl * 1024:(tl + 1) * 1024],
                                po(g)[:, :], inc=(sACT, 1),
                                mark=[f"B.godone.{g}"])
                    elif k == "epilog":
                        b, tt, g = it["b"], it["tt"], it["g"]
                        par = g % 2
                        if tt == 0:
                            self.wait(se, sV, f"B.accg.{b}.{NEXP - 1}.15")
                        if g >= 2:
                            semz = sZ if g % 2 == 0 else sZ2
                            self.wait(se, semz, f"B.zw.{g - 2}")
                        self.op(se, "copy", rb[:, par * D:(par + 1) * D],
                                acc[:, tt * 1024:(tt + 1) * 1024],
                                inc=(sACT, 1), mark=f"B.racc.{g}")
                        self.wait(se, sDVE, f"B.var.{g}")
                        self.op(se, "activation", stat[:, par * 4 + 3:par * 4 + 4],
                                stat[:, par * 4 + 2:par * 4 + 3], AF.Sqrt,
                                scale=1.0 / D, bias=epsc[:, :],
                                inc=(sACT, 1), mark=f"B.sd.{g}")
                        self.wait(se, sDVE, f"B.z.{g}")
                        self.op(se, "activation", zb16[:, par * D:(par + 1) * D],
                                rb[:, par * D:(par + 1) * D], AF.Copy,
                                inc=(sACT, 1), mark=f"B.z16.{g}")

            def f_vector(ve):
                self.op(ve, "memset", epsc[:, :], EPS, inc=(sDVE, 1))
                self.op(ve, "memset", xs[:, :], 0.0, inc=(sDVE, 1), mark="B.xs0")
                for it in self.B_items:
                    k = it["kind"]
                    if k == "smax":
                        b, tt = it["b"], it["tt"]
                        self.wait(ve, sACT, f"B.lgt.{b}.{tt}")
                        self.op(ve, "reduce_max", rstat[:, 0:1], lgt[:, :],
                                axis=AX.X, negate=True, inc=(sDVE, 1),
                                mark=f"B.nrmax.{b}.{tt}")
                        self.op(ve, "tensor_scalar_mul", rstat[:, 1:2],
                                rstat[:, 0:1], -1.0, inc=(sDVE, 1))
                        self.op(ve, "tensor_scalar", eq[:, :], lgt[:, :],
                                rstat[:, 1:2], None, ALU.is_ge, inc=(sDVE, 1))
                        self.op(ve, "scalar_tensor_tensor", eq[:, :], eq[:, :],
                                -100000.0, lgt[:, :], ALU.mult, ALU.add,
                                inc=(sDVE, 1))
                        self.op(ve, "reduce_max", rstat[:, 2:3], eq[:, :],
                                axis=AX.X, inc=(sDVE, 1))
                        self.op(ve, "tensor_scalar", eqa[:, tt * 8:(tt + 1) * 8],
                                lgt[:, :], rstat[:, 2:3], None, ALU.is_ge,
                                inc=(sDVE, 1), mark=f"B.eqa.{b}.{tt}"
                                if tt == 15 else None)
                        self.op(ve, "tensor_tensor", ee[:, :], ee[:, :],
                                eqa[:, tt * 8:(tt + 1) * 8], ALU.mult,
                                inc=(sDVE, 1), waits=[(self.sems["sACT"], f"B.ee.{b}.{tt}")])
                        self.op(ve, "reduce_sum", rstat[:, 3:4], ee[:, :],
                                axis=AX.X, inc=(sDVE, 1))
                        self.op(ve, "reciprocal", rstat[:, 3:4], rstat[:, 3:4],
                                inc=(sDVE, 1))
                        self.op(ve, "tensor_scalar_mul", cwa[:, tt * 8:(tt + 1) * 8],
                                ee[:, :], rstat[:, 3:4], inc=(sDVE, 1),
                                mark=[f"B.cw.{b}.{tt}", f"B.cwa.{b}.{tt}"])
                    elif k == "xstg":
                        b, tt = it["b"], it["tt"]
                        semt = self.sems["sT"] if tt % 2 == 0 else self.sems["sT2"]
                        self.wait(ve, semt, f"B.xstg.{b}.{tt}")
                        self.op(ve, "tensor_copy",
                                xs[:, (tt % 2) * ROWW + D:(tt % 2) * ROWW + D + 8],
                                cwa[:, tt * 8:(tt + 1) * 8],
                                inc=(sDVE, 1), mark=f"B.xsrdy.{b}.{tt}")
                    elif k == "pos":
                        b, tt = it["b"], it["tt"]
                        self.wait(ve, sACT, f"B.posf.{b}.{tt}")
                        self.op(ve, "tensor_tensor", offw[:, :], posf[:, :],
                                eqa[:, tt * 8:(tt + 1) * 8], ALU.mult,
                                inc=(sDVE, 1))
                        self.op(ve, "scalar_tensor_tensor", posf[:, :],
                                eqa[:, tt * 8:(tt + 1) * 8], -100000.0,
                                offw[:, :], ALU.mult, ALU.add, inc=(sDVE, 1))
                        self.op(ve, "tensor_scalar_add", offw[:, :], posf[:, :],
                                100000.0, inc=(sDVE, 1))
                        self.op(ve, "tensor_copy", offs[:, tt * 8:(tt + 1) * 8],
                                offw[:, :], inc=(sDVE, 1),
                                mark=[f"B.offs.{b}.{tt}", f"B.godone.{it['god']}"])
                    elif k == "gu":
                        gg = it["gid"]
                        par = gg % 2
                        tok0, tw = it["tok0"], it["tw"]
                        self.wait(ve, sPE, f"B.u.{gg}")
                        if gg >= 2:
                            self.wait(ve, sPOOL, f"B.hm.{gg - 2}")
                        self.op(ve, "tensor_tensor", ub[:, par * 512:par * 512 + tw],
                                pu(gg)[:, 0:tw], cwb[:, tok0:tok0 + tw], ALU.mult,
                                inc=(sDVE, 1),
                                mark=[f"B.ub.{gg}", f"B.ubdone.{gg}"])
                    elif k == "down":
                        if it["s"] == 0:
                            continue
                        b, e, tl, g = it["b"], it["e"], it["tl"], it["god"]
                        self.wait(ve, sPE, f"B.god.{g}")
                        self.op(ve, "tensor_tensor", stg[:, tl * 1024:(tl + 1) * 1024],
                                stg[:, tl * 1024:(tl + 1) * 1024], po(g)[:, :],
                                ALU.add, inc=(sDVE, 1),
                                mark=[f"B.godone.{g}"] +
                                     ([f"B.stg.{b}.{e}"] if (it["s"] == NS - 1
                                                             and tl == NTL - 1) else []))
                    elif k == "epilog":
                        b, tt, g = it["b"], it["tt"], it["g"]
                        par = g % 2
                        self.wait(ve, sACT, f"B.racc.{g}")
                        if g >= 2:
                            semz = sZ if g % 2 == 0 else sZ2
                            self.wait(ve, semz, f"B.zw.{g - 2}")
                        self.op(ve, "reduce_sum", stat[:, par * 4:par * 4 + 1],
                                rb[:, par * D:(par + 1) * D], axis=AX.X,
                                negate=True, inc=(sDVE, 1))
                        self.op(ve, "tensor_scalar_mul", stat[:, par * 4 + 1:par * 4 + 2],
                                stat[:, par * 4:par * 4 + 1], 1.0 / D, inc=(sDVE, 1))
                        self.op(ve, "tensor_scalar_add", cb[:, 0:D],
                                rb[:, par * D:(par + 1) * D],
                                stat[:, par * 4 + 1:par * 4 + 2], inc=(sDVE, 1))
                        self.op(ve, "tensor_mul", rb[:, par * D:(par + 1) * D],
                                cb[:, 0:D],
                                cb[:, 0:D], inc=(sDVE, 1))
                        self.op(ve, "reduce_sum", stat[:, par * 4 + 2:par * 4 + 3],
                                rb[:, par * D:(par + 1) * D], axis=AX.X,
                                inc=(sDVE, 1), mark=f"B.var.{g}")
                        self.wait(ve, sACT, f"B.sd.{g}")
                        self.op(ve, "reciprocal", stat[:, par * 4 + 3:par * 4 + 4],
                                stat[:, par * 4 + 3:par * 4 + 4], inc=(sDVE, 1))
                        self.op(ve, "tensor_scalar_mul", rb[:, par * D:(par + 1) * D],
                                cb[:, 0:D],
                                stat[:, par * 4 + 3:par * 4 + 4],
                                inc=(sDVE, 1), mark=f"B.z.{g}")


            def f_gpsimd_merged(ge):
                if ge is not None:
                    ge.reg_mov(bcreg, C - 1)
                for it in self.B_items:
                    k = it["kind"]
                    if k == "accinit":
                        b = it["b"]
                        if b > 0:
                            self.wait(ge, sDVE, f"B.z.{(b - 1) * 16 + 15}")
                        self.dma(ge, bass.AP(acc, 0, [[16 * 1024, 128], [1024, 16], [1, 1024]]),
                                 t["z1_16"][b * S:(b + 1) * S, :]
                                 .rearrange("(tt p) d -> p tt d", p=128),
                                 sV2, mark=f"B.acc0.{b}")
                        self.wait(ge, sV2, f"B.acc0.{b}")
                    elif k == "xscat":
                        b, tt = it["b"], it["tt"]
                        self.wait(ge, sDVE, f"B.xsrdy.{b}.{tt}")
                        self.wait(ge, sDVE, f"B.offs.{b}.{tt}")
                        if tt == 0 and b > 0:
                            self.wait(ge, sW, f"B.xgl.{b - 1}.{NEXP - 1}")
                        par = tt % 2
                        for e in range(NEXP):
                            self.op(ge, "indirect_dma_start",
                                    xg_d[e][:, :],
                                    IndirectOffsetOnAxis(offs[:, tt * 8 + e:tt * 8 + e + 1], 0),
                                    xs[:, par * ROWW:(par + 1) * ROWW],
                                    None,
                                    bounds_check=bcreg, oob_is_err=False,
                                    inc=(sV, 16),
                                    mark=[f"B.xscat.{b}.{tt}"] +
                                         ([f"B.xscatall.{b}"] if it["last"] else [])
                                    if e == NEXP - 1 else None)
                        self.wait(ge, sV, f"B.xscat.{b}.{tt}")
                    elif k == "accglast":
                        b = it["b"]
                        e = NEXP - 1
                        self.wait(ge, sW2, f"B.dst.{b}.{e}")
                        for tt in range(16):
                            self.op(ge, "indirect_dma_start",
                                    acc[:, tt * 1024:(tt + 1) * 1024],
                                    None,
                                    dn_d[e][:, :],
                                    IndirectOffsetOnAxis(offs[:, tt * 8 + e:tt * 8 + e + 1], 0),
                                    bounds_check=bcreg, oob_is_err=False,
                                    compute_op=ALU.add,
                                    inc=(sV, 16),
                                    mark=f"B.accg.{b}.{e}.{tt}")
                            self.wait(ge, sV, f"B.accg.{b}.{e}.{tt}")
                    elif k == "gu":
                        gg = it["gid"]
                        b, e = it["b"], it["e"]
                        self.wait(ge, sACT, f"B.sg.{gg}")
                        self.wait(ge, sDVE, f"B.ub.{gg}")
                        sid = it["sid"]
                        if gg % 16 == 0 and sid >= 1:
                            self.wait(ge, sPE, f"B.hfree.{sid - 1}")
                        fc, tp = it["fc"], it["tp"]
                        tok0, tw = it["tok0"], it["tw"]
                        par = gg % 2
                        self.op(ge, "tensor_mul",
                                hb[:, fc * C + tok0: fc * C + tok0 + tw],
                                sg[:, par * 512:par * 512 + tw],
                                ub[:, par * 512:par * 512 + tw],
                                inc=(sPOOL, 1),
                                mark=[f"B.hm.{gg}"] +
                                     ([f"B.h.{sid}"] if gg % 16 == 15 else []))
                        if it["s"] == 0 and e >= 1:
                            tt = fc * 2 + tp
                            pe_ = e - 1
                            if tt == 0:
                                self.wait(ge, sW2, f"B.dst.{b}.{pe_}")
                            self.op(ge, "indirect_dma_start",
                                    acc[:, tt * 1024:(tt + 1) * 1024],
                                    None,
                                    dn_d[pe_][:, :],
                                    IndirectOffsetOnAxis(offs[:, tt * 8 + pe_:tt * 8 + pe_ + 1], 0),
                                    bounds_check=bcreg, oob_is_err=False,
                                    compute_op=ALU.add,
                                    inc=(sV, 16),
                                    mark=f"B.accg.{b}.{pe_}.{tt}")
                            self.wait(ge, sV, f"B.accg.{b}.{pe_}.{tt}")

            streams = {"sync": f_sync, "gpsimd": f_gpsimd_merged, "tensor": f_tensor,
                       "vector": f_vector, "scalar": f_scalar}
            self.run_phase(streams)

    def emit_phaseC(self):
        t = self.t
        sX = self.sems["sX"]
        sW, sW2 = self.sems["sW"], self.sems["sW2"]
        sZ, sZ2 = self.sems["sZ"], self.sems["sZ2"]
        sPE, sACT, sDVE = self.sems["sPE"], self.sems["sACT"], self.sems["sDVE"]
        sD = self.sems["sD2DC"]

        items = []
        ga = 0

        def G(kind, **kw):
            items.append(dict(kind=kind, **kw))

        for b in range(NB):
            gs = 0
            gy = 0
            gt = 0
            G("xh", b=b)
            for g in range(NKV):
                for j, kind in ((0, "kproj"), (1, "vproj")):
                    for qg in range(4):
                        G(kind, b=b, g=g, qg=qg, ga=ga, esem="sACT"); ga += 1
                for kt in range(16):
                    G("vtrans", b=b, g=g, kt=kt, gt=gt); gt += 1
                G("krope", b=b, g=g)
                G("sq1", b=b, g=g)
                for qg in range(4):
                    G("knorm", b=b, g=g, qg=qg, ga=ga, esem="sACT"); ga += 1
                G("m2c", b=b, g=g)
                for hq in range(4):
                    h = g * 4 + hq
                    for qg in range(4):
                        G("qproj", b=b, g=g, h=h, qg=qg, ga=ga, esem="sDVE"); ga += 1
                    G("qrope", b=b, g=g, h=h)
                    G("sq0", b=b, g=g, h=h)
                    for qg in range(4):
                        G("qnorm", b=b, g=g, h=h, qg=qg, ga=ga, esem="sDVE"); ga += 1
                    G("aug", b=b, g=g, h=h)
                    for qg in range(4):
                        for kt in range(16):
                            G("score", b=b, g=g, h=h, qg=qg, kt=kt, gs=gs,
                              i=kt, gy=gy, nk=16)
                            gs += 1
                        G("recip", b=b, h=h, qg=qg, gy=gy)
                        G("bcast", b=b, h=h, qg=qg, ga=ga, gy=gy, esem="sACT"); ga += 1
                        G("ymul", b=b, h=h, qg=qg, gy=gy,
                          last=(h == NH - 1 and qg == 3))
                        gy += 1
            for tt in range(16):
                G("outproj", b=b, tt=tt, g2=b * 16 + tt)
                G("epilog", b=b, tt=tt, g2=b * 16 + tt)
        self.C_items = items
        self.C_ga_item = {it["ga"]: it for it in items if "ga" in it}
        self.C_last_qnorm = {}
        for it in items:
            if it["kind"] == "qnorm":
                self.C_last_qnorm[(it["b"], it["h"])] = it["ga"]
            if it["kind"] == "knorm":
                self.C_last_qnorm[(it["b"], "k", it["g"])] = it["ga"]

        with ExitStack() as st:
            xh = self.sb(st, "C_xh", [128, NCHUNK * S], BF16)
            wqa = self.sb(st, "C_wq", [128, NCHUNK * D], BF16)
            wka = self.sb(st, "C_wk", [128, NCHUNK * 256], BF16)
            wva = self.sb(st, "C_wv", [128, NCHUNK * 256], BF16)
            woa = self.sb(st, "C_wo", [128, NCHUNK * D], BF16)
            qh = self.sb(st, "C_qh", [96, S], BF16)
            kh = self.sb(st, "C_kh", [96, S], BF16)
            qP = self.sb(st, "C_qP", [64, S], BF16)
            kP = self.sb(st, "C_kP", [64, S], BF16)
            vt = self.sb(st, "C_vt", [64, S], F32)
            va = self.sb(st, "C_va", [128, 16 * 96], BF16)
            q2 = self.sb(st, "C_q2", [64, S], BF16)
            rta = self.sb(st, "C_rta", [32, S], BF16)
            rtb = self.sb(st, "C_rtb", [32, S], BF16)
            rtc = self.sb(st, "C_rtc", [32, S], BF16)
            cosT = self.sb(st, "C_cos", [32, S], BF16)
            sinT = self.sb(st, "C_sin", [32, S], BF16)
            n_row = self.sb(st, "C_nrow", [1, S], BF16)
            b_row = self.sb(st, "C_brow", [1, S], F32)
            m2c = self.sb(st, "C_m2c", [1, 2], F32)
            pbuf = self.sb(st, "C_pbuf", [128, 2 * 1024], BF16)
            rzrow = self.sb(st, "C_rzrow", [1, 512], F32)
            rzb = self.sb(st, "C_rzb", [64, 512], F32)
            yt = self.sb(st, "C_yt", [128, NCHUNK * S], BF16)
            id16 = self.sb(st, "C_id16", [128, 128], BF16)
            id32 = self.sb(st, "C_id32", [128, 128], F32)
            o128_32 = self.sb(st, "C_o128_32", [1, 128], F32)
            o64_16 = self.sb(st, "C_o64_16", [64, 1], BF16)
            z2t = self.sb(st, "C_z2t", [128, 2 * D], F32)
            rb = self.sb(st, "C_rb", [128, 2 * D], F32)
            cb = self.sb(st, "C_cb", [128, D], F32)
            zb16 = self.sb(st, "C_zb16", [128, 2 * D], BF16)
            stat = self.sb(st, "C_stat", [128, 8], F32)
            epsc = self.sb(st, "C_epsc", [128, 1], F32)

            psA = self.ps(st, "C_psA", [64, 2 * 512])
            psS = self.ps(st, "C_psS", [128, 4 * 512])
            psY = self.ps(st, "C_psY", [96, 512])
            psO = self.ps(st, "C_psO", [128, 512])

            def pa(sl):
                return psA[:, (sl % 2) * 512:(sl % 2 + 1) * 512]

            def pss(gs):
                sp, half = (gs // 2) % 2, gs % 2
                return psS[:, sp * 1024 + half * 512: sp * 1024 + (half + 1) * 512]

            def pssp(gp):
                sp = gp % 2
                return psS[:, sp * 1024:(sp + 1) * 1024]

            def pb(gs):
                sp, half = (gs // 2) % 2, gs % 2
                return pbuf[:, sp * 1024 + half * 512: sp * 1024 + (half + 1) * 512]

            def pbp(gp):
                sp = gp % 2
                return pbuf[:, sp * 1024:(sp + 1) * 1024]

            def ga_wait(eng, g):
                if g >= 2:
                    it2 = self.C_ga_item[g - 2]
                    self.wait(eng, self.sems[it2["esem"]], f"C.gadone.{g - 2}")

            def f_sync(sy):
                self.dma(sy, id16[:, :], self.c_ident16[:, :], sX)
                self.dma(sy, id32[:, :], self.c_ident32[:, :], sX)
                self.dma(sy, o128_32[:, :], self.c_ones1x128_32[:, :], sX)
                self.dma(sy, o64_16[:, :], self.c_ones64x1_16[:, :], sX)
                self.dma(sy, cosT[:, :], self.c_ropecos[:, :], sX)
                self.dma(sy, sinT[:, :], self.c_ropesin[:, :], sX)
                self.wait(sy, sD, "d2dC")
                self.dma(sy, bass.AP(wqa, 0, [[NCHUNK * D, 128], [D, NCHUNK], [1, D]]),
                         t["wq1_16"][:, :].rearrange("(c p) e -> p c e", p=128), sX)
                self.dma(sy, bass.AP(wka, 0, [[NCHUNK * 256, 128], [256, NCHUNK], [1, 256]]),
                         t["wk1_16"][:, :].rearrange("(c p) e -> p c e", p=128), sX)
                self.dma(sy, bass.AP(wva, 0, [[NCHUNK * 256, 128], [256, NCHUNK], [1, 256]]),
                         t["wv1_16"][:, :].rearrange("(c p) e -> p c e", p=128), sX)
                self.dma(sy, bass.AP(woa, 0, [[NCHUNK * D, 128], [D, NCHUNK], [1, D]]),
                         t["wo1_16"][:, :].rearrange("(c p) e -> p c e", p=128), sX,
                         mark="C.setup")
                self.wait(sy, sX, "C.setup")
                for b in range(NB):
                    if b > 0:
                        self.wait(sy, sPE, f"C.xhfree.{b - 1}")
                    for c in range(NCHUNK):
                        self.dma(sy, xh[:, c * S:(c + 1) * S],
                                 t["z2_16"][b * S:(b + 1) * S,
                                            c * 128:(c + 1) * 128], sX,
                                 transpose=True,
                                 mark=f"C.xh.{b}" if c == NCHUNK - 1 else None)
                    for g2 in range(b * 16, b * 16 + 16):
                        tt = g2 % 16
                        semx = sW if g2 % 2 == 0 else sW2
                        if g2 >= 2:
                            self.wait(sy, sDVE, f"C.zused.{g2 - 2}")
                        r0 = b * S + tt * 128
                        self.dma(sy, z2t[:, (g2 % 2) * D:(g2 % 2 + 1) * D],
                                 t["z2_32"][r0:r0 + 128, :], semx,
                                 mark=f"C.z2t.{g2}")
                        if g2 >= 2:
                            gw = g2 - 2
                            semz = sZ if gw % 2 == 0 else sZ2
                            self.wait(sy, sDVE, f"C.z.{gw}")
                            self.wait(sy, sACT, f"C.z16.{gw}")
                            rw0 = (gw // 16) * S + (gw % 16) * 128
                            self.dma(sy, t["z3_32"][rw0:rw0 + 128, :],
                                     rb[:, (gw % 2) * D:(gw % 2 + 1) * D], semz)
                            self.dma(sy, t["z3_16"][rw0:rw0 + 128, :],
                                     zb16[:, (gw % 2) * D:(gw % 2 + 1) * D], semz,
                                     mark=f"C.zw.{gw}")
                for gw in (NB * 16 - 2, NB * 16 - 1):
                    semz = sZ if gw % 2 == 0 else sZ2
                    self.wait(sy, sDVE, f"C.z.{gw}")
                    self.wait(sy, sACT, f"C.z16.{gw}")
                    rw0 = (gw // 16) * S + (gw % 16) * 128
                    self.dma(sy, t["z3_32"][rw0:rw0 + 128, :],
                             rb[:, (gw % 2) * D:(gw % 2 + 1) * D], semz)
                    self.dma(sy, t["z3_16"][rw0:rw0 + 128, :],
                             zb16[:, (gw % 2) * D:(gw % 2 + 1) * D], semz,
                             mark=f"C.zw.{gw}")
                self.wait(sy, sZ, sZ.n)
                self.wait(sy, sZ2, sZ2.n)

            def f_tensor(te):
                self.wait(te, sX, "C.setup")
                for it in self.C_items:
                    k = it["kind"]
                    if k in ("kproj", "vproj", "qproj"):
                        b, g, qg, gg = it["b"], it["g"], it["qg"], it["ga"]
                        if k == "kproj" and qg == 0 and g == 0:
                            self.wait(te, sX, f"C.xh.{b}")
                        ga_wait(te, gg)
                        if k == "qproj":
                            woff = it["h"] * 64
                            warr, nw = wqa, D
                        elif k == "kproj":
                            woff = g * 64
                            warr, nw = wka, 256
                        else:
                            woff = g * 64
                            warr, nw = wva, 256
                        marks = [f"C.ga.{gg}"]
                        if (k == "qproj" and qg == 3 and it["h"] == NH - 1):
                            marks.append(f"C.xhfree.{b}")
                        for c in range(NCHUNK):
                            self.op(te, "matmul", pa(gg)[0:64, :],
                                    warr[:, c * nw + woff: c * nw + woff + 64],
                                    xh[:, c * S + qg * 512: c * S + (qg + 1) * 512],
                                    start=(c == 0), stop=(c == NCHUNK - 1),
                                    inc=(sPE, 1) if c == NCHUNK - 1 else None,
                                    mark=marks if c == NCHUNK - 1 else None)
                    elif k in ("qnorm", "knorm"):
                        b, g, qg, gg = it["b"], it["g"], it["qg"], it["ga"]
                        key = (f"C.sq0.{b}.{it['h']}" if k == "qnorm"
                               else f"C.sq1.{b}.{g}")
                        self.wait(te, sDVE, key)
                        ga_wait(te, gg)
                        self.op(te, "matmul", pa(gg)[0:1, :],
                                o64_16[:, :], q2[:, qg * 512:(qg + 1) * 512],
                                start=True, stop=True, inc=(sPE, 1),
                                mark=f"C.ga.{gg}")
                    elif k == "vtrans":
                        b, g, kt, gt = it["b"], it["g"], it["kt"], it["gt"]
                        if kt == 0:
                            self.wait(te, sACT, f"C.vt.{b}.{g}")
                            if g == 0 and b > 0:
                                self.wait(te, sDVE, f"C.psO.done.{b * 16 - 1}.1")
                        if gt >= 1:
                            self.wait(te, sDVE, f"C.psT.done.{b}.{gt - 1}")
                        self.op(te, "transpose", psO[:, 0:64],
                                vt[:, kt * 128:(kt + 1) * 128], id32[0:64, 0:64],
                                inc=(sPE, 1), mark=f"C.psT.{b}.{gt}")
                    elif k == "score":
                        b, h, qg, kt, gs, i = (it["b"], it["h"], it["qg"],
                                               it["kt"], it["gs"], it["i"])
                        if i == 0 and qg == 0:
                            self.wait(te, sDVE, f"C.aug.{b}.{h}")
                            if it["h"] % 4 == 0:
                                self.wait(te, sDVE, f"C.krope.{b}.{it['g']}")
                        if gs % 2 == 0 and gs >= 4:
                            self.wait(te, sACT, f"C.expdone.{b}.{gs // 2 - 2}")
                        self.op(te, "matmul", pss(gs),
                                kh[:, kt * 128:(kt + 1) * 128],
                                qh[:, qg * 512:(qg + 1) * 512],
                                start=True, stop=True,
                                inc=(sPE, 1), mark=f"C.psS.{b}.{gs}")
                        if gs % 2 == 1:
                            self._yC_mm(te, b, gs - 3)
                            self._yC_mm(te, b, gs - 2)
                        if i == it["nk"] - 1:
                            self._yC_mm(te, b, gs - 1)
                            self._yC_mm(te, b, gs)
                    elif k == "bcast":
                        b, h, qg, gg = it["b"], it["h"], it["qg"], it["ga"]
                        self.wait(te, sDVE, f"C.rz.{b}.{h}.{qg}")
                        ga_wait(te, gg)
                        self.op(te, "matmul", pa(gg)[0:64, :],
                                o128_32[0:1, 0:64], rzrow[0:1, :],
                                start=True, stop=True, inc=(sPE, 1),
                                mark=f"C.ga.{gg}")
                    elif k == "outproj":
                        b, tt, g2 = it["b"], it["tt"], it["g2"]
                        if tt == 0:
                            self.wait(te, sDVE, f"C.ytdone.{b}")
                        for dg in range(2):
                            if not (g2 == 0 and dg == 0):
                                self.wait(te, sDVE,
                                          f"C.psO.done.{g2}.0" if dg == 1
                                          else f"C.psO.done.{g2 - 1}.1")
                            for ci in range(NCHUNK):
                                self.op(te, "matmul",
                                        psO[:, 0:512],
                                        yt[:, ci * S + tt * 128: ci * S + (tt + 1) * 128],
                                        woa[:, ci * D + dg * 512: ci * D + (dg + 1) * 512],
                                        start=(ci == 0), stop=(ci == NCHUNK - 1),
                                        inc=(sPE, 1) if ci == NCHUNK - 1 else None,
                                        mark=f"C.psO.{g2}.{dg}" if ci == NCHUNK - 1 else None)

            self._C_ymeta = {}
            for it in items:
                if it["kind"] == "score":
                    self._C_ymeta[(it["b"], it["gs"])] = it

            def yC_mm(te, b, gs):
                if gs < 0:
                    return
                itm = self._C_ymeta.get((b, gs))
                if itm is None or itm.get("ydone"):
                    return
                itm["ydone"] = True
                kt, i, gy = itm["kt"], itm["i"], itm["gy"]
                self.wait(te, sACT, f"C.expp.{b}.{gs // 2}")
                if i == 0 and gy >= 1:
                    self.wait(te, sDVE, f"C.ymul.{b}.{gy - 1}")
                last = (i == itm["nk"] - 1)
                self.op(te, "matmul", psY[:, :],
                        va[:, kt * 96:(kt + 1) * 96], pb(gs),
                        start=(i == 0), stop=last,
                        inc=(sPE, 1) if last else None,
                        mark=f"C.psY.{b}.{gy}" if last else None)

            self._yC_mm = lambda te, b, gs: yC_mm(te, b, gs)

            def f_scalar(se):
                for it in self.C_items:
                    k = it["kind"]
                    if k in ("kproj", "vproj"):
                        b, g, qg, gg = it["b"], it["g"], it["qg"], it["ga"]
                        self.wait(se, sPE, f"C.ga.{gg}")
                        tgt = kP if k == "kproj" else vt
                        marks = [f"C.gadone.{gg}"]
                        if qg == 3:
                            marks.append(f"C.{'kv'[k == 'vproj']}evac.{b}.{g}")
                            if k == "vproj":
                                marks.append(f"C.vt.{b}.{g}")
                        self.op(se, "copy", tgt[0:64, qg * 512:(qg + 1) * 512],
                                pa(gg)[0:64, :], inc=(sACT, 1), mark=marks)
                    elif k == "knorm":
                        b, g, qg, gg = it["b"], it["g"], it["qg"], it["ga"]
                        self.wait(se, sPE, f"C.ga.{gg}")
                        self.op(se, "copy",
                                n_row[0:1, qg * 512:(qg + 1) * 512],
                                pa(gg)[0:1, :], inc=(sACT, 1),
                                mark=[f"C.gadone.{gg}"] +
                                     ([f"C.nrow1.{b}.{g}"] if qg == 3 else []))
                    elif k == "score":
                        b, gs = it["b"], it["gs"]
                        if gs % 2 == 0:
                            continue
                        gp = gs // 2
                        self.wait(se, sPE, f"C.psS.{b}.{gs}")
                        self.op(se, "activation", pbp(gp), pssp(gp), AF.Exp,
                                inc=(sACT, 1),
                                mark=[f"C.expp.{b}.{gp}", f"C.expdone.{b}.{gp}"])
                    elif k == "bcast":
                        b, h, qg, gg = it["b"], it["h"], it["qg"], it["ga"]
                        self.wait(se, sPE, f"C.ga.{gg}")
                        self.op(se, "copy", rzb[:, :], pa(gg)[0:64, :],
                                inc=(sACT, 1),
                                mark=[f"C.gadone.{gg}", f"C.rzb.{b}.{h}.{qg}"])
                    elif k == "epilog":
                        g2 = it["g2"]
                        par = g2 % 2
                        self.wait(se, sDVE, f"C.var.{g2}")
                        self.op(se, "activation", stat[:, par * 4 + 3:par * 4 + 4],
                                stat[:, par * 4 + 2:par * 4 + 3], AF.Sqrt,
                                scale=1.0 / D, bias=epsc[:, :],
                                inc=(sACT, 1), mark=f"C.sd.{g2}")
                        self.wait(se, sDVE, f"C.z.{g2}")
                        self.op(se, "activation", zb16[:, par * D:(par + 1) * D],
                                rb[:, par * D:(par + 1) * D], AF.Copy,
                                inc=(sACT, 1), mark=f"C.z16.{g2}")

            def f_vector(ve):
                self.op(ve, "memset", epsc[:, :], EPS, inc=(sDVE, 1))
                self.op(ve, "memset", qh[64:96, :], 0.0, inc=(sDVE, 1))
                self.op(ve, "memset", kh[64:96, :], 0.0, inc=(sDVE, 1))
                self.op(ve, "memset", va[:, :], 0.0, inc=(sDVE, 1))
                self.op(ve, "memset",
                        bass.AP(va, 64, [[16 * 96, 128], [96, 16], [1, 1]]), 1.0,
                        inc=(sDVE, 1))

                def rope(dst, srcP):
                    self.op(ve, "tensor_copy", rtc[:, :], srcP[32:64, :],
                            inc=(sDVE, 1))
                    self.op(ve, "tensor_mul", rta[:, :], srcP[0:32, :], cosT[:, :],
                            inc=(sDVE, 1))
                    self.op(ve, "tensor_mul", rtb[:, :], rtc[:, :], sinT[:, :],
                            inc=(sDVE, 1))
                    self.op(ve, "tensor_sub", dst[0:32, :], rta[:, :], rtb[:, :],
                            inc=(sDVE, 1))
                    self.op(ve, "tensor_mul", rta[:, :], srcP[0:32, :], sinT[:, :],
                            inc=(sDVE, 1))
                    self.op(ve, "tensor_mul", rtb[:, :], rtc[:, :], cosT[:, :],
                            inc=(sDVE, 1))
                    self.op(ve, "tensor_add", rta[:, :], rta[:, :], rtb[:, :],
                            inc=(sDVE, 1))
                    self.op(ve, "tensor_copy", dst[32:64, :], rta[:, :],
                            inc=(sDVE, 1))

                for it in self.C_items:
                    k = it["kind"]
                    if k == "qproj":
                        b, qg, gg = it["b"], it["qg"], it["ga"]
                        self.wait(ve, sPE, f"C.ga.{gg}")
                        self.op(ve, "tensor_scalar_mul",
                                qP[0:64, qg * 512:(qg + 1) * 512],
                                pa(gg)[0:64, :], 0.125,
                                inc=(sDVE, 1), mark=[f"C.gadone.{gg}"])
                    elif k == "qrope":
                        b, h = it["b"], it["h"]
                        rope(qh, qP)
                    elif k == "krope":
                        b, g = it["b"], it["g"]
                        self.wait(ve, sACT, f"C.kevac.{b}.{g}")
                        rope(kh, kP)
                        self.op(ve, "memset", kh[64:65, :], 1.0,
                                inc=(sDVE, 1), mark=f"C.krope.{b}.{g}")
                    elif k == "sq0":
                        b, h = it["b"], it["h"]
                        if h % 4 == 0:
                            key = self.C_last_qnorm[(b, "k", it["g"])]
                        else:
                            key = self.C_last_qnorm[(b, h - 1)]
                        self.wait(ve, sPE, f"C.ga.{key}")
                        self.op(ve, "tensor_mul", q2[:, :], qh[0:64, :], qh[0:64, :],
                                inc=(sDVE, 1), mark=f"C.sq0.{b}.{h}")
                    elif k == "sq1":
                        b, g = it["b"], it["g"]
                        if g > 0:
                            self.wait(ve, sPE, f"C.ga.{self.C_last_qnorm[(b, g * 4 - 1)]}")
                        elif b > 0:
                            self.wait(ve, sPE, f"C.ga.{self.C_last_qnorm[(b - 1, NH - 1)]}")
                        self.op(ve, "tensor_mul", q2[:, :], kh[0:64, :], kh[0:64, :],
                                inc=(sDVE, 1), mark=f"C.sq1.{b}.{g}")
                    elif k == "m2c":
                        b, g = it["b"], it["g"]
                        self.wait(ve, sACT, f"C.nrow1.{b}.{g}")
                        self.op(ve, "reduce_max", m2c[0:1, 0:1], n_row[0:1, 0:S],
                                axis=AX.X, inc=(sDVE, 1))
                        self.op(ve, "tensor_scalar_mul", m2c[0:1, 1:2], m2c[0:1, 0:1],
                                0.5 / LAM, inc=(sDVE, 1), mark=f"C.m2c.{b}.{g}")
                    elif k == "qnorm":
                        b, qg, gg = it["b"], it["qg"], it["ga"]
                        self.wait(ve, sPE, f"C.ga.{gg}")
                        self.op(ve, "tensor_scalar_mul",
                                b_row[0:1, qg * 512:(qg + 1) * 512],
                                pa(gg)[0:1, :], 0.5 * LAM,
                                inc=(sDVE, 1), mark=[f"C.gadone.{gg}"])
                    elif k == "aug":
                        b, h = it["b"], it["h"]
                        self.op(ve, "tensor_scalar_add", b_row[0:1, :], b_row[0:1, :],
                                m2c[0:1, 1:2], inc=(sDVE, 1))
                        self.op(ve, "tensor_scalar_mul", qh[64:65, :], b_row[0:1, :],
                                -1.0, inc=(sDVE, 1), mark=f"C.aug.{b}.{h}")
                    elif k == "vtrans":
                        b, gt, kt = it["b"], it["gt"], it["kt"]
                        self.wait(ve, sPE, f"C.psT.{b}.{gt}")
                        self.op(ve, "tensor_copy", va[:, kt * 96:kt * 96 + 64],
                                psO[:, 0:64], inc=(sDVE, 1),
                                mark=f"C.psT.done.{b}.{gt}")
                    elif k == "recip":
                        b, h, qg, gy = it["b"], it["h"], it["qg"], it["gy"]
                        self.wait(ve, sPE, f"C.psY.{b}.{gy}")
                        self.op(ve, "reciprocal", rzrow[0:1, :], psY[64:65, :],
                                inc=(sDVE, 1), mark=f"C.rz.{b}.{h}.{qg}")
                    elif k == "ymul":
                        b, h, qg, gy = it["b"], it["h"], it["qg"], it["gy"]
                        self.wait(ve, sACT, f"C.rzb.{b}.{h}.{qg}")
                        strip, row = h // 2, (h % 2) * 64
                        self.op(ve, "tensor_mul",
                                yt[row:row + 64,
                                   strip * S + qg * 512: strip * S + (qg + 1) * 512],
                                psY[0:64, :], rzb[:, :],
                                inc=(sDVE, 1),
                                mark=[f"C.ymul.{b}.{gy}"] +
                                     ([f"C.ytdone.{b}"] if it["last"] else []))
                    elif k == "epilog":
                        b, tt, g2 = it["b"], it["tt"], it["g2"]
                        par = g2 % 2
                        semx = sW if g2 % 2 == 0 else sW2
                        self.wait(ve, semx, f"C.z2t.{g2}")
                        if g2 >= 2:
                            semz = sZ if g2 % 2 == 0 else sZ2
                            self.wait(ve, semz, f"C.zw.{g2 - 2}")
                        self.wait(ve, sPE, f"C.psO.{g2}.0")
                        self.op(ve, "tensor_add", rb[:, par * D:par * D + 512],
                                psO[:, 0:512], z2t[:, par * D:par * D + 512],
                                inc=(sDVE, 1), mark=f"C.psO.done.{g2}.0")
                        self.wait(ve, sPE, f"C.psO.{g2}.1")
                        self.op(ve, "tensor_add", rb[:, par * D + 512:(par + 1) * D],
                                psO[:, 0:512], z2t[:, par * D + 512:(par + 1) * D],
                                inc=(sDVE, 1),
                                mark=[f"C.psO.done.{g2}.1", f"C.zused.{g2}"])
                        self.op(ve, "reduce_sum", stat[:, par * 4:par * 4 + 1],
                                rb[:, par * D:(par + 1) * D], axis=AX.X,
                                negate=True, inc=(sDVE, 1))
                        self.op(ve, "tensor_scalar_mul", stat[:, par * 4 + 1:par * 4 + 2],
                                stat[:, par * 4:par * 4 + 1], 1.0 / D, inc=(sDVE, 1))
                        self.op(ve, "tensor_scalar_add", cb[:, 0:D],
                                rb[:, par * D:(par + 1) * D],
                                stat[:, par * 4 + 1:par * 4 + 2], inc=(sDVE, 1))
                        self.op(ve, "tensor_mul", rb[:, par * D:(par + 1) * D],
                                cb[:, 0:D],
                                cb[:, 0:D], inc=(sDVE, 1))
                        self.op(ve, "reduce_sum", stat[:, par * 4 + 2:par * 4 + 3],
                                rb[:, par * D:(par + 1) * D], axis=AX.X,
                                inc=(sDVE, 1), mark=f"C.var.{g2}")
                        self.wait(ve, sACT, f"C.sd.{g2}")
                        self.op(ve, "reciprocal", stat[:, par * 4 + 3:par * 4 + 4],
                                stat[:, par * 4 + 3:par * 4 + 4], inc=(sDVE, 1))
                        self.op(ve, "tensor_scalar_mul", rb[:, par * D:(par + 1) * D],
                                cb[:, 0:D],
                                stat[:, par * 4 + 3:par * 4 + 4],
                                inc=(sDVE, 1), mark=f"C.z.{g2}")

            def reset_items():
                for it in items:
                    it.pop("ydone", None)

            streams = {"sync": f_sync,
                       "tensor": lambda e: (reset_items(), f_tensor(e)),
                       "vector": f_vector, "scalar": f_scalar}
            self.run_phase(streams)


    def emit_phaseD(self):
        t = self.t
        nc = self.nc
        sX = self.sems["sX"]
        sW, sW2 = self.sems["sW"], self.sems["sW2"]
        sV, sV2 = self.sems["sV"], self.sems["sV2"]
        sZ, sZ2 = self.sems["sZ"], self.sems["sZ2"]
        sPE, sACT = self.sems["sPE"], self.sems["sACT"]
        sDVE, sPOOL = self.sems["sDVE"], self.sems["sPOOL"]
        sD = self.sems["sD2DD"]
        NCYC = 3

        dtmp = nc.dram_tensor("D_tmp16", [S, D], BF16, kind="Internal")

        items = []
        gd = 0

        def G(kind, **kw):
            items.append(dict(kind=kind, **kw))

        G("biasprep")
        for b in range(NB):
            G("xh", b=b, cyc=0)
            G("gctx", b=b)
            G("gnet", b=b)
            G("oload", b=b)
            for cyc in range(NCYC):
                for tg in range(4):
                    for oc in range(NCHUNK):
                        G("h1", b=b, cyc=cyc, tg=tg, oc=oc, gd=gd, esem="sACT"); gd += 1
                    for oc in range(4):
                        G("h2", b=b, cyc=cyc, tg=tg, oc=oc, gd=gd, esem="sACT"); gd += 1
                    for oc in range(2):
                        G("h3", b=b, cyc=cyc, tg=tg, oc=oc, gd=gd, esem="sACT"); gd += 1
                    for j in range(4):
                        G("tf", b=b, cyc=cyc, tg=tg, j=j, tt=tg * 4 + j)
                        G("twist", b=b, cyc=cyc, tg=tg, j=j, tt=tg * 4 + j)
                if cyc < NCYC - 1:
                    for tt in range(16):
                        G("oout", b=b, cyc=cyc, tt=tt)
                    G("xh", b=b, cyc=cyc + 1)
            for tt in range(16):
                G("final", b=b, tt=tt, g2=b * 16 + tt)
        self.D_items = items
        self.D_gd_item = {it["gd"]: it for it in items if "gd" in it}

        with ExitStack() as st:
            xh = self.sb(st, "D_xh", [128, NCHUNK * S], BF16)
            out32 = self.sb(st, "D_out32", [128, 16 * D], F32)
            h1a = self.sb(st, "D_h1", [128, NCHUNK * 512], BF16)
            h2a = self.sb(st, "D_h2", [128, 4 * 512], BF16)
            h3a = self.sb(st, "D_h3", [128, 2 * 512], BF16)
            scr = self.sb(st, "D_scr", [128, D], F32)
            w1a = self.sb(st, "D_w1", [128, NCHUNK * D], BF16)
            w2a = self.sb(st, "D_w2", [128, NCHUNK * 512], BF16)
            w3a = self.sb(st, "D_w3", [128, 4 * 256], BF16)
            w4a = self.sb(st, "D_w4", [128, 2], BF16)
            gw1a = self.sb(st, "D_gw1", [128, NCHUNK * 512], BF16)
            gw2a = self.sb(st, "D_gw2", [128, 4 * 256], BF16)
            gw3a = self.sb(st, "D_gw3", [128, 2], BF16)
            brow = self.sb(st, "D_brow", [1, D], F32)
            b1c = self.sb(st, "D_b1c", [128, NCHUNK], F32)
            b2c = self.sb(st, "D_b2c", [128, 4], F32)
            b3c = self.sb(st, "D_b3c", [128, 2], F32)
            b4c = self.sb(st, "D_b4c", [128, 1], F32)
            g1bc = self.sb(st, "D_g1bc", [128, 4], F32)
            g2bc = self.sb(st, "D_g2bc", [128, 2], F32)
            g3b = self.sb(st, "D_g3b", [1, 1], F32)
            arv = self.sb(st, "D_ar", [1, 1], F32)
            sc1 = self.sb(st, "D_sc1", [1, 8], F32)
            gcT = self.sb(st, "D_gc", [128, NCHUNK], F32)
            gc16 = self.sb(st, "D_gc16", [128, NCHUNK], BF16)
            g1_16 = self.sb(st, "D_g1x", [128, 4], BF16)
            g2_16 = self.sb(st, "D_g2x", [128, 2], BF16)
            acol = self.sb(st, "D_acol", [128, 2], F32)
            cpl = self.sb(st, "D_cpl", [128, 2], F32)
            scol = self.sb(st, "D_scol", [128, 1], F32)
            ob16 = self.sb(st, "D_ob16", [128, 2 * D], BF16)
            id32 = self.sb(st, "D_id32", [128, 128], F32)
            o128_32 = self.sb(st, "D_o128_32", [1, 128], F32)
            z3t = self.sb(st, "D_z3t", [128, 2 * D], F32)
            rb = self.sb(st, "D_rb", [128, 2 * D], F32)
            cb = self.sb(st, "D_cb", [128, 2 * D], F32)
            stat = self.sb(st, "D_stat", [128, 8], F32)
            epsc = self.sb(st, "D_epsc", [128, 1], F32)

            psH = self.ps(st, "D_psH", [128, 2 * 512])
            psM = self.ps(st, "D_psM", [128, 2])
            psB = self.ps(st, "D_psB", [128, 16])

            def ph(i):
                return psH[:, (i % 2) * 512:(i % 2 + 1) * 512]

            def gd_wait(eng, g):
                if g >= 2:
                    it2 = self.D_gd_item[g - 2]
                    self.wait(eng, self.sems[it2["esem"]], f"D.gdone.{g - 2}")

            def f_sync(sy):
                self.dma(sy, id32[:, :], self.c_ident32[:, :], sX)
                self.dma(sy, o128_32[:, :], self.c_ones1x128_32[:, :], sX)
                self.wait(sy, sD, "d2dD")
                self.dma(sy, bass.AP(w1a, 0, [[NCHUNK * D, 128], [D, NCHUNK], [1, D]]),
                         t["cn_w1_16"][:, :].rearrange("(c p) e -> p c e", p=128), sX)
                self.dma(sy, bass.AP(w2a, 0, [[NCHUNK * 512, 128], [512, NCHUNK], [1, 512]]),
                         t["cn_w2_16"][:, :].rearrange("(c p) e -> p c e", p=128), sX)
                self.dma(sy, bass.AP(w3a, 0, [[4 * 256, 128], [256, 4], [1, 256]]),
                         t["cn_w3_16"][:, :].rearrange("(c p) e -> p c e", p=128), sX)
                self.dma(sy, bass.AP(w4a, 0, [[2, 128], [1, 2], [1, 1]]),
                         t["cn_w4_16"][:, :].rearrange("(c p) e -> p c e", p=128), sX,
                         allow_slow_non_contiguous=True)
                self.dma(sy, bass.AP(gw1a, 0, [[NCHUNK * 512, 128], [512, NCHUNK], [1, 512]]),
                         t["gc_w1_16"][:, :].rearrange("(c p) e -> p c e", p=128), sX)
                self.dma(sy, bass.AP(gw2a, 0, [[4 * 256, 128], [256, 4], [1, 256]]),
                         t["gc_w2_16"][:, :].rearrange("(c p) e -> p c e", p=128), sX)
                self.dma(sy, bass.AP(gw3a, 0, [[2, 128], [1, 2], [1, 1]]),
                         t["gc_w3_16"][:, :].rearrange("(c p) e -> p c e", p=128), sX,
                         allow_slow_non_contiguous=True)
                self.dma(sy, arv[:, :], t["adaptive_range"][:].rearrange("(a d) -> a d", a=1), sX)
                self.dma(sy, g3b[:, :], t["gc_b3"][:].rearrange("(a d) -> a d", a=1), sX,
                         mark="D.setup")
                self.wait(sy, sX, "D.setup")
                for i, (nm, ln) in enumerate((("cn_b1", D), ("cn_b2", 512), ("cn_b3", 256),
                                              ("cn_b4", 1), ("gc_b1", 512), ("gc_b2", 256))):
                    if i > 0:
                        self.wait(sy, sPE, f"D.btr.{i - 1}")
                        self.wait(sy, sX, f"D.brow.{i - 1}")
                    self.dma(sy, brow[0:1, 0:ln],
                             t[nm][:].rearrange("(a d) -> a d", a=1), sX,
                             mark=f"D.brow.{i}")
                self.wait(sy, sX, "D.brow.5")
                for it in self.D_items:
                    k = it["kind"]
                    if k == "xh":
                        b, cyc = it["b"], it["cyc"]
                        if cyc == 0:
                            if b > 0:
                                self.wait(sy, sPE, f"D.xhfree.{b - 1}.2")
                            src = t["z3_16"]
                            r0 = b * S
                        else:
                            self.wait(sy, sZ, f"D.ow.{b}.{cyc - 1}")
                            self.wait(sy, sZ2, f"D.ow2.{b}.{cyc - 1}")
                            self.wait(sy, sPE, f"D.xhfree.{b}.{cyc - 1}")
                            src = dtmp
                            r0 = 0
                        for c in range(NCHUNK):
                            self.dma(sy, xh[:, c * S:(c + 1) * S],
                                     src[r0:r0 + S, c * 128:(c + 1) * 128], sX,
                                     transpose=True,
                                     mark=f"D.xh.{b}.{cyc}" if c == NCHUNK - 1 else None)
                    elif k == "oload":
                        b = it["b"]
                        if b > 0:
                            self.wait(sy, sDVE, f"D.zused.{(b - 1) * 16 + 15}")
                        for tt in range(16):
                            self.dma(sy, out32[:, tt * D:(tt + 1) * D],
                                     t["z3_32"][b * S + tt * 128: b * S + (tt + 1) * 128, :],
                                     sV, mark=f"D.ol.{b}.{tt}")
                    elif k == "oout":
                        b, cyc, tt = it["b"], it["cyc"], it["tt"]
                        semz = sZ if tt % 2 == 0 else sZ2
                        self.wait(sy, sACT, f"D.ob16.{b}.{cyc}.{tt}")
                        mk = [f"D.od.{b}.{cyc}.{tt}"]
                        if tt == 14:
                            mk.append(f"D.ow.{b}.{cyc}")
                        if tt == 15:
                            mk.append(f"D.ow2.{b}.{cyc}")
                        self.dma(sy, dtmp[tt * 128:(tt + 1) * 128, :],
                                 ob16[:, (tt % 2) * D:(tt % 2 + 1) * D], semz,
                                 mark=mk)
                    elif k == "final":
                        b, tt, g2 = it["b"], it["tt"], it["g2"]
                        semx = sW if g2 % 2 == 0 else sW2
                        if g2 >= 2:
                            self.wait(sy, sDVE, f"D.zused.{g2 - 2}")
                        r0 = b * S + tt * 128
                        self.dma(sy, z3t[:, (g2 % 2) * D:(g2 % 2 + 1) * D],
                                 t["z3_32"][r0:r0 + 128, :], semx,
                                 mark=f"D.z3t.{g2}")
                        if g2 >= 2:
                            gw = g2 - 2
                            semz = sZ if gw % 2 == 0 else sZ2
                            self.wait(sy, sDVE, f"D.z.{gw}")
                            bb, ttw = gw // 16, gw % 16
                            self.dma(sy, t["out"][bb, ttw * 128:(ttw + 1) * 128, :],
                                     rb[:, (gw % 2) * D:(gw % 2 + 1) * D], semz,
                                     mark=f"D.zw.{gw}")
                for gw in (NB * 16 - 2, NB * 16 - 1):
                    semz = sZ if gw % 2 == 0 else sZ2
                    self.wait(sy, sDVE, f"D.z.{gw}")
                    bb, ttw = gw // 16, gw % 16
                    self.dma(sy, t["out"][bb, ttw * 128:(ttw + 1) * 128, :],
                             rb[:, (gw % 2) * D:(gw % 2 + 1) * D], semz,
                             mark=f"D.zw.{gw}")
                self.wait(sy, sZ, sZ.n)
                self.wait(sy, sZ2, sZ2.n)

            def f_tensor(te):
                for it in self.D_items:
                    k = it["kind"]
                    if k == "biasprep":
                        for i, ln in enumerate((D, 512, 256, 1, 512, 256)):
                            self.wait(te, sX, f"D.brow.{i}")
                            if ln == 1:
                                self.wait(te, sACT, "D.b4s")
                                self.op(te, "nop", inc=(sPE, 1), mark=f"D.btr.{i}")
                                continue
                            nch = ln // 128
                            for cc in range(nch):
                                if (i, cc) != (0, 0):
                                    prev_i = i - 1 if i - 1 != 3 else 2
                                    self.wait(te, sACT, f"D.btec.{i}.{cc - 1}"
                                              if cc > 0 else f"D.btelast.{prev_i}")
                                self.op(te, "transpose", psB[0:128, 0:1],
                                        brow[0:1, cc * 128: cc * 128 + 128],
                                        id32[0:1, 0:1],
                                        inc=(sPE, 1),
                                        mark=f"D.btpe.{i}.{cc}")
                            self.wait(te, sACT, f"D.btelast.{i}")
                            self.op(te, "nop", inc=(sPE, 1), mark=f"D.btr.{i}")
                        self.op(te, "matmul", psM[:, 0:1], o128_32[0:1, :],
                                sc1[0:1, 5:6], start=True, stop=True,
                                inc=(sPE, 1), mark="D.b4bc")
                    elif k == "gctx":
                        pass
                    elif k == "gnet":
                        b = it["b"]
                        self.wait(te, sDVE, f"D.gc16.{b}")
                        self.wait(te, sACT, "D.b4ce" if b == 0 else f"D.acol.{b - 1}")
                        for oc in range(4):
                            for dc in range(NCHUNK):
                                self.op(te, "matmul", psM[:, 0:1],
                                        gw1a[:, dc * 512 + oc * 128: dc * 512 + oc * 128 + 128],
                                        gc16[:, dc:dc + 1],
                                        start=(dc == 0), stop=(dc == NCHUNK - 1),
                                        inc=(sPE, 1) if dc == NCHUNK - 1 else None,
                                        mark=f"D.g1.{b}.{oc}" if dc == NCHUNK - 1 else None)
                            self.wait(te, sACT, f"D.g1e.{b}.{oc}")
                        for oc in range(2):
                            for dc in range(4):
                                self.op(te, "matmul", psM[:, 1:2],
                                        gw2a[:, dc * 256 + oc * 128: dc * 256 + oc * 128 + 128],
                                        g1_16[:, dc:dc + 1],
                                        start=(dc == 0), stop=(dc == 3),
                                        inc=(sPE, 1) if dc == 3 else None,
                                        mark=f"D.g2.{b}.{oc}" if dc == 3 else None)
                            self.wait(te, sACT, f"D.g2e.{b}.{oc}")
                        for dc in range(2):
                            self.op(te, "matmul", psM[0:1, 0:1],
                                    gw3a[:, dc:dc + 1], g2_16[:, dc:dc + 1],
                                    start=(dc == 0), stop=(dc == 1),
                                    inc=(sPE, 1) if dc == 1 else None,
                                    mark=f"D.g3.{b}" if dc == 1 else None)
                        self.wait(te, sDVE, f"D.AB.{b}")
                        self.op(te, "matmul", psM[:, 0:1], o128_32[0:1, :],
                                sc1[0:1, 3:4], start=True, stop=True, inc=(sPE, 1),
                                mark=f"D.Abc.{b}")
                        self.wait(te, sACT, f"D.Ae.{b}")
                        self.op(te, "matmul", psM[:, 0:1], o128_32[0:1, :],
                                sc1[0:1, 4:5], start=True, stop=True, inc=(sPE, 1),
                                mark=f"D.Bbc.{b}")
                    elif k in ("h1", "h2", "h3"):
                        b, cyc, tg, oc, g = it["b"], it["cyc"], it["tg"], it["oc"], it["gd"]
                        if k == "h1" and oc == 0 and tg == 0:
                            self.wait(te, sX, f"D.xh.{b}.{cyc}")
                            if cyc == 0:
                                self.wait(te, sACT, f"D.acol.{b}")
                        gd_wait(te, g)
                        if k == "h1":
                            ndc, warr, wstr, rhs, rstr = NCHUNK, w1a, D, xh, S
                            rview = lambda dc: xh[:, dc * S + tg * 512:
                                                  dc * S + (tg + 1) * 512]
                        elif k == "h2":
                            ndc, warr, wstr = NCHUNK, w2a, 512
                            rview = lambda dc: h1a[:, dc * 512:(dc + 1) * 512]
                        else:
                            ndc, warr, wstr = 4, w3a, 256
                            rview = lambda dc: h2a[:, dc * 512:(dc + 1) * 512]
                        if k == "h2" and oc == 0:
                            self.wait(te, sACT, f"D.h1all.{b}.{cyc}.{tg}")
                        if k == "h3" and oc == 0:
                            self.wait(te, sACT, f"D.h2all.{b}.{cyc}.{tg}")
                        for dc in range(ndc):
                            self.op(te, "matmul", ph(g),
                                    warr[:, dc * wstr + oc * 128: dc * wstr + oc * 128 + 128],
                                    rview(dc),
                                    start=(dc == 0), stop=(dc == ndc - 1),
                                    inc=(sPE, 1) if dc == ndc - 1 else None,
                                    mark=f"D.gd.{g}" if dc == ndc - 1 else None)
                    elif k == "tf":
                        b, cyc, tg, j, tt = it["b"], it["cyc"], it["tg"], it["j"], it["tt"]
                        if j == 0:
                            self.wait(te, sACT, f"D.h3all.{b}.{cyc}.{tg}")
                        prev = None
                        if tt > 0:
                            prev = (b, cyc, tt - 1)
                        elif cyc > 0:
                            prev = (b, cyc - 1, 15)
                        elif b > 0:
                            prev = (b - 1, NCYC - 1, 15)
                        else:
                            self.wait(te, sACT, "D.b4ce")
                        if prev is not None:
                            self.wait(te, sACT, "D.scol.%d.%d.%d" % prev)
                        if cyc == 0 and tg == 0 and j == 0:
                            self.wait(te, sACT, f"D.acol.{b}")
                        for fc in range(2):
                            mk = [f"D.tf.{b}.{cyc}.{tt}"]
                            if tt == 15:
                                mk.append(f"D.xhfree.{b}.{cyc}")
                            self.op(te, "matmul", psM[:, 1:2],
                                    h3a[:, fc * 512 + j * 128: fc * 512 + j * 128 + 128],
                                    w4a[:, fc:fc + 1],
                                    start=(fc == 0), stop=(fc == 1),
                                    inc=(sPE, 1) if fc == 1 else None,
                                    mark=mk if fc == 1 else None)

            def f_scalar(se):
                for it in self.D_items:
                    k = it["kind"]
                    if k == "biasprep":
                        for i, (tgt, ln) in enumerate(((b1c, D), (b2c, 512), (b3c, 256),
                                                       (None, 1), (g1bc, 512), (g2bc, 256))):
                            if ln == 1:
                                self.wait(se, sX, f"D.brow.{i}")
                                self.op(se, "copy", sc1[0:1, 5:6], brow[0:1, 0:1],
                                        inc=(sACT, 1), mark="D.b4s")
                                continue
                            nch = ln // 128
                            for cc in range(nch):
                                self.wait(se, sPE, f"D.btpe.{i}.{cc}")
                                self.op(se, "copy", tgt[0:128, cc:cc + 1],
                                        psB[0:128, 0:1], inc=(sACT, 1),
                                        mark=[f"D.btec.{i}.{cc}"] +
                                             ([f"D.btelast.{i}"] if cc == nch - 1 else []))
                        self.wait(se, sPE, "D.b4bc")
                        self.op(se, "copy", b4c[:, :], psM[:, 0:1],
                                inc=(sACT, 1), mark="D.b4ce")
                    elif k == "gnet":
                        b = it["b"]
                        for oc in range(4):
                            self.wait(se, sPE, f"D.g1.{b}.{oc}")
                            self.op(se, "activation", g1_16[:, oc:oc + 1],
                                    psM[:, 0:1], AF.Gelu, bias=g1bc[:, oc:oc + 1],
                                    inc=(sACT, 1), mark=f"D.g1e.{b}.{oc}")
                        for oc in range(2):
                            self.wait(se, sPE, f"D.g2.{b}.{oc}")
                            self.op(se, "activation", g2_16[:, oc:oc + 1],
                                    psM[:, 1:2], AF.Gelu, bias=g2bc[:, oc:oc + 1],
                                    inc=(sACT, 1), mark=f"D.g2e.{b}.{oc}")
                        self.wait(se, sPE, f"D.g3.{b}")
                        self.op(se, "activation", sc1[0:1, 0:1], psM[0:1, 0:1],
                                AF.Sigmoid, bias=g3b[0:1, 0:1],
                                inc=(sACT, 1), mark=f"D.gf.{b}")
                        self.wait(se, sPE, f"D.Abc.{b}")
                        self.op(se, "copy", acol[:, 0:1], psM[:, 0:1],
                                inc=(sACT, 1), mark=f"D.Ae.{b}")
                        self.wait(se, sPE, f"D.Bbc.{b}")
                        self.op(se, "copy", acol[:, 1:2], psM[:, 0:1],
                                inc=(sACT, 1), mark=f"D.acol.{b}")
                    elif k in ("h1", "h2", "h3"):
                        b, cyc, tg, oc, g = it["b"], it["cyc"], it["tg"], it["oc"], it["gd"]
                        self.wait(se, sPE, f"D.gd.{g}")
                        tgt, bc, nall = {"h1": (h1a, b1c, NCHUNK),
                                         "h2": (h2a, b2c, 4),
                                         "h3": (h3a, b3c, 2)}[k]
                        self.op(se, "activation", tgt[:, oc * 512:(oc + 1) * 512],
                                ph(g), AF.Gelu, bias=bc[:, oc:oc + 1],
                                inc=(sACT, 1),
                                mark=[f"D.gdone.{g}"] +
                                     ([f"D.{k}all.{b}.{cyc}.{tg}"] if oc == nall - 1 else []))
                    elif k == "tf":
                        b, cyc, tt = it["b"], it["cyc"], it["tt"]
                        self.wait(se, sPE, f"D.tf.{b}.{cyc}.{tt}")
                        prev = ((b, cyc, tt - 1) if tt > 0 else
                                ((b, cyc - 1, 15) if cyc > 0 else
                                 ((b - 1, NCYC - 1, 15) if b > 0 else None)))
                        if prev is not None:
                            self.wait(se, sDVE, "D.cpl.%d.%d.%d" % prev)
                        self.op(se, "activation", scol[:, :], psM[:, 1:2],
                                AF.Sigmoid, bias=b4c[:, :],
                                inc=(sACT, 1), mark=f"D.scol.{b}.{cyc}.{tt}")
                    elif k == "oout":
                        b, cyc, tt = it["b"], it["cyc"], it["tt"]
                        self.wait(se, sDVE, f"D.tw.{b}.{cyc}.{tt}")
                        if tt >= 2:
                            semz = sZ if tt % 2 == 0 else sZ2
                            self.wait(se, semz, f"D.od.{b}.{cyc}.{tt - 2}")
                        self.op(se, "activation", ob16[:, (tt % 2) * D:(tt % 2 + 1) * D],
                                out32[:, tt * D:(tt + 1) * D], AF.Copy,
                                inc=(sACT, 1), mark=f"D.ob16.{b}.{cyc}.{tt}")
                    elif k == "final":
                        g2 = it["g2"]
                        par = g2 % 2
                        self.wait(se, sDVE, f"D.var.{g2}")
                        self.op(se, "activation", stat[:, par * 4 + 3:par * 4 + 4],
                                stat[:, par * 4 + 2:par * 4 + 3], AF.Sqrt,
                                scale=1.0 / D, bias=epsc[:, :],
                                inc=(sACT, 1), mark=f"D.sd.{g2}")

            def f_vector(ve):
                self.op(ve, "memset", epsc[:, :], EPS, inc=(sDVE, 1))
                for it in self.D_items:
                    k = it["kind"]
                    if k == "gctx":
                        b = it["b"]
                        self.wait(ve, sX, f"D.xh.{b}.0")
                        for c in range(NCHUNK):
                            self.op(ve, "reduce_sum", gcT[:, c:c + 1],
                                    xh[:, c * S:(c + 1) * S], axis=AX.X,
                                    inc=(sDVE, 1))
                        self.op(ve, "tensor_scalar_mul", gc16[:, :], gcT[:, :],
                                1.0 / S, inc=(sDVE, 1), mark=f"D.gc16.{b}")
                    elif k == "gnet":
                        b = it["b"]
                        self.wait(ve, sACT, f"D.gf.{b}")
                        self.op(ve, "tensor_scalar", sc1[0:1, 1:2], sc1[0:1, 0:1],
                                0.7, -0.5, ALU.mult, ALU.add, inc=(sDVE, 1))
                        self.op(ve, "tensor_mul", sc1[0:1, 2:3], sc1[0:1, 1:2],
                                arv[0:1, 0:1], inc=(sDVE, 1))
                        self.op(ve, "tensor_scalar", sc1[0:1, 3:4], sc1[0:1, 2:3],
                                2.0, 0.1, ALU.mult, ALU.add, inc=(sDVE, 1))
                        self.op(ve, "tensor_scalar_mul", sc1[0:1, 4:5], arv[0:1, 0:1],
                                0.6, inc=(sDVE, 1), mark=f"D.AB.{b}")
                    elif k == "tf":
                        b, cyc, tt = it["b"], it["cyc"], it["tt"]
                        self.wait(ve, sACT, f"D.scol.{b}.{cyc}.{tt}")
                        self.op(ve, "tensor_scalar", cpl[:, 0:1], scol[:, :],
                                acol[:, 1:2], acol[:, 0:1], ALU.mult, ALU.add,
                                inc=(sDVE, 1))
                        self.op(ve, "tensor_scalar_mul", cpl[:, 1:2], cpl[:, 0:1],
                                -1.0, inc=(sDVE, 1), mark=f"D.cpl.{b}.{cyc}.{tt}")
                    elif k == "twist":
                        b, cyc, tt = it["b"], it["cyc"], it["tt"]
                        if cyc == 0 and tt == 0:
                            self.wait(ve, sV, f"D.ol.{b}.15")
                        Q = [out32[:, tt * D + q * 256: tt * D + (q + 1) * 256]
                             for q in range(4)]
                        Sc = [scr[:, q * 256:(q + 1) * 256] for q in range(4)]
                        c0, c1 = cpl[:, 0:1], cpl[:, 1:2]
                        self.op(ve, "scalar_tensor_tensor", Sc[0], Q[2], c0, Q[0],
                                ALU.mult, ALU.add, inc=(sDVE, 1))
                        self.op(ve, "scalar_tensor_tensor", Sc[1], Q[3], c1, Q[1],
                                ALU.mult, ALU.add, inc=(sDVE, 1))
                        self.op(ve, "scalar_tensor_tensor", Sc[2], Q[0], c1, Q[2],
                                ALU.mult, ALU.add, inc=(sDVE, 1))
                        self.op(ve, "scalar_tensor_tensor", Sc[3], Q[1], c0, Q[3],
                                ALU.mult, ALU.add, inc=(sDVE, 1))
                        self.op(ve, "tensor_copy", out32[:, tt * D:(tt + 1) * D],
                                scr[:, :], inc=(sDVE, 1),
                                mark=f"D.tw.{b}.{cyc}.{tt}")
                    elif k == "final":
                        b, tt, g2 = it["b"], it["tt"], it["g2"]
                        par = g2 % 2
                        semx = sW if g2 % 2 == 0 else sW2
                        self.wait(ve, semx, f"D.z3t.{g2}")
                        if g2 >= 2:
                            semz = sZ if g2 % 2 == 0 else sZ2
                            self.wait(ve, semz, f"D.zw.{g2 - 2}")
                        self.op(ve, "tensor_add", rb[:, par * D:(par + 1) * D],
                                out32[:, tt * D:(tt + 1) * D],
                                z3t[:, par * D:(par + 1) * D],
                                inc=(sDVE, 1), mark=[f"D.zused.{g2}"])
                        self.op(ve, "reduce_sum", stat[:, par * 4:par * 4 + 1],
                                rb[:, par * D:(par + 1) * D], axis=AX.X,
                                negate=True, inc=(sDVE, 1))
                        self.op(ve, "tensor_scalar_mul", stat[:, par * 4 + 1:par * 4 + 2],
                                stat[:, par * 4:par * 4 + 1], 1.0 / D, inc=(sDVE, 1))
                        self.op(ve, "tensor_scalar_add", cb[:, par * D:(par + 1) * D],
                                rb[:, par * D:(par + 1) * D],
                                stat[:, par * 4 + 1:par * 4 + 2], inc=(sDVE, 1))
                        self.op(ve, "tensor_mul", rb[:, par * D:(par + 1) * D],
                                cb[:, par * D:(par + 1) * D],
                                cb[:, par * D:(par + 1) * D], inc=(sDVE, 1))
                        self.op(ve, "reduce_sum", stat[:, par * 4 + 2:par * 4 + 3],
                                rb[:, par * D:(par + 1) * D], axis=AX.X,
                                inc=(sDVE, 1), mark=f"D.var.{g2}")
                        self.wait(ve, sACT, f"D.sd.{g2}")
                        self.op(ve, "reciprocal", stat[:, par * 4 + 3:par * 4 + 4],
                                stat[:, par * 4 + 3:par * 4 + 4], inc=(sDVE, 1))
                        self.op(ve, "tensor_scalar_mul", rb[:, par * D:(par + 1) * D],
                                cb[:, par * D:(par + 1) * D],
                                stat[:, par * 4 + 3:par * 4 + 4],
                                inc=(sDVE, 1), mark=f"D.z.{g2}")

            streams = {"sync": f_sync, "tensor": f_tensor,
                       "vector": f_vector, "scalar": f_scalar}
            self.run_phase(streams)

    def build(self):
        self.emit_prologue()
        if "A" in self.phases:
            self.emit_phaseA()
            self.barrier()
        if "B" in self.phases:
            self.emit_phaseB()
            self.barrier()
        if "C" in self.phases:
            self.emit_phaseC()
            self.barrier()
        if "D" in self.phases:
            self.emit_phaseD()
            self.barrier()
        return self.nc


def _build(phases):
    import time
    t0 = time.time()
    mk = MK(phases)
    nc = mk.build()
    print(f"[kernel] built phases={phases} n_instructions={nc.n_instructions()} "
          f"in {time.time() - t0:.1f}s", flush=True)
    return mk, nc


def _run(mk, nc, inputs, shard_c):
    in_maps = []
    for c in range(8):
        m = {}
        for n in mk.ext_in:
            if n == "L0_attn_x":
                m[n] = np.ascontiguousarray(inputs[n][2 * c:2 * c + 2])
            elif n.startswith("z") and n[2:] in ("_32", "_16"):
                m[n] = inputs[n][c]
            else:
                m[n] = inputs[n]
        in_maps.append(m)
    return run_bass_kernel_spmd(nc, in_maps, core_ids=list(range(8)))


def kernel(**inputs):
    phases = PHASES
    mk, nc = _build(phases)
    res = _run(mk, nc, inputs, None)
    outs = res.results
    if "D" in phases:
        return np.concatenate([o["out"] for o in outs], axis=0)
    last = [n for n in mk.ext_out if n.endswith("_32")][-1]
    return np.stack([o[last] for o in outs], axis=0)
